# version 13
# baseline (speedup 1.0000x reference)
import numpy as np
import ml_dtypes
import concourse.bass as bass
import concourse.mybir as mybir
from concourse.bass_utils import run_bass_kernel_spmd
from contextlib import ExitStack

F32 = mybir.dt.float32
BF16 = mybir.dt.bfloat16
AF = mybir.ActivationFunctionType
ALU = mybir.AluOpType
AX = mybir.AxisListType

T = 2048
D = 1024
DI = 2048
DS = 16
DTR = 64
NB = 8

STREAMS = ("T", "V", "S", "G", "Q")
SEM_ENGINES = ("T", "V", "S", "G", "Q", "QG", "QSY")
STREAM_OF = {"T": "T", "V": "V", "S": "S", "G": "G", "Q": "Q", "QG": "G", "QSY": "Q"}


class Step:
    __slots__ = ("engine", "stream", "fn", "deps", "inc", "marked", "val", "name")

    def __init__(self, engine, fn, deps, inc, name=""):
        self.engine = engine
        self.stream = STREAM_OF[engine]
        self.fn = fn
        self.deps = deps
        self.inc = inc
        self.marked = False
        self.val = None
        self.name = name


class Prog:

    def __init__(self):
        self.steps = []
        self.last_write = {}
        self.readers = {}

    def step(self, engine, fn, reads=(), writes=(), deps=(), name=""):
        dd = set(d for d in deps if d is not None)
        for k in reads:
            w = self.last_write.get(k)
            if w is not None:
                dd.add(w)
        for k in writes:
            w = self.last_write.get(k)
            if w is not None:
                dd.add(w)
            for r in self.readers.get(k, ()):
                dd.add(r)
        s = Step(engine, fn, dd, 16 if engine in ("Q", "QG") else 1, name)
        self.steps.append(s)
        for k in reads:
            self.readers.setdefault(k, set()).add(s)
        for k in writes:
            self.last_write[k] = s
            self.readers[k] = set()
        return s

    def dma(self, queue, fn, reads=(), writes=(), deps=(), name=""):
        assert queue in ("Q", "QG")
        return self.step(queue, fn, reads, writes, deps, name)

    def barrier(self):
        lasts = {}
        for s in self.steps:
            lasts[s.engine] = s
        prev = list(lasts.values())
        out = []
        for e in ("T", "V", "S", "G", "QSY"):
            out.append(self.step(e, None, deps=prev, name="barrier"))
        return out

    def emit(self, nc, es):
        for s in self.steps:
            for d in s.deps:
                d.marked = True
        cnt = dict.fromkeys(SEM_ENGINES, 0)
        for s in self.steps:
            if s.marked:
                cnt[s.engine] += s.inc
                s.val = cnt[s.engine]
        epoch_end = {}
        fence = {}
        cur = {"Q": [], "QG": []}
        closed = {"Q": 0, "QG": 0}

        def close_epoch(q):
            if cur[q]:
                end = cur[q][-1].val
                for m in cur[q]:
                    epoch_end[m] = end
                closed[q] = end
                cur[q].clear()
        for s in self.steps:
            if s.engine in ("Q", "QG"):
                q = s.engine
                if s.deps and cur[q]:
                    close_epoch(q)
                if not cur[q] and closed[q] > 0:
                    fence[s] = closed[q]
                cur[q].append(s)
            for d in s.deps:
                if d.engine in ("Q", "QG") and d is not s and d in cur[d.engine] \
                   and not (s.engine == d.engine):
                    close_epoch(d.engine)
        for q in ("Q", "QG"):
            if cur[q]:
                end = cur[q][-1].val
                for m in cur[q]:
                    epoch_end[m] = end
        sems = {e: es.enter_context(nc.semaphore(f"sem_{e}")) for e in SEM_ENGINES}
        per_stream = {e: [s for s in self.steps if s.stream == e] for e in STREAMS}
        watermark = {}

        def run_stream(ename, eng):
            wm = watermark.setdefault(ename, dict.fromkeys(SEM_ENGINES, 0))
            for s in per_stream[ename]:
                if s in fence:
                    v = fence[s]
                    if v > wm[s.engine]:
                        eng.wait_ge(sems[s.engine], v)
                        wm[s.engine] = v
                for d in sorted(s.deps, key=lambda d: (d.engine, d.val or 0)):
                    assert d.marked and d.val is not None, (s.name, d.name)
                    v = epoch_end[d] if d.engine in ("Q", "QG") else d.val
                    if v > wm[d.engine]:
                        eng.wait_ge(sems[d.engine], v)
                        wm[d.engine] = v
                inst = s.fn(eng) if s.fn is not None else None
                if s.marked:
                    if inst is None:
                        eng.sem_inc(sems[s.engine], s.inc)
                    else:
                        inst.then_inc(sems[s.engine], s.inc)

        return run_stream

    def build_block(self, nc, es):
        run_stream = self.emit(nc, es)
        with nc.Block() as block:
            @block.tensor
            def _(e):
                run_stream("T", e)

            @block.vector
            def _(e):
                run_stream("V", e)

            @block.scalar
            def _(e):
                run_stream("S", e)

            @block.gpsimd
            def _(e):
                run_stream("G", e)

            @block.sync
            def _(e):
                run_stream("Q", e)



def prep_common(I):
    hp = {}
    f32 = np.float32

    def T_(a):
        return np.ascontiguousarray(np.asarray(a, f32).T)

    hp["inprojT"] = T_(I["L0_attn_in_proj_w"])
    hp["gateT"] = T_(I["L0_attn_gate_w"])
    hp["fdw1T"] = T_(I["L0_attn_fd_w1"])
    hp["fdw2T"] = T_(I["L0_attn_fd_w2"])
    hp["outprojT"] = T_(I["L0_attn_out_proj_w"])
    xp = np.asarray(I["L0_attn_x_proj_w"], f32)
    hp["xprojT_dt"] = T_(xp[:DTR])
    hp["xprojT_BC"] = T_(xp[DTR:DTR + 32])
    hp["dtprojT"] = T_(I["L0_attn_dt_proj_w"])
    hp["convw"] = np.ascontiguousarray(np.asarray(I["L0_attn_conv_w"], f32)[:, 0, :])
    for k_src, k_dst in [
        ("L0_attn_conv_b", "conv_b"), ("L0_attn_dt_proj_b", "dt_b"),
        ("L0_attn_Dp", "Dp"), ("L0_attn_gate_b", "gate_b"),
        ("L0_attn_fd_b1", "fd_b1"), ("L0_attn_fd_ln_g", "fd_ln_g"),
        ("L0_attn_fd_ln_b", "fd_ln_b"), ("L0_attn_fd_b2", "fd_b2"),
        ("L0_attn_norm_g", "norm_g"), ("L0_attn_norm_b", "norm_b"),
        ("L1_attn_b_qkv", "b_qkv"), ("L1_attn_b_o", "b_o"),
    ]:
        hp[k_dst] = np.ascontiguousarray(np.asarray(I[k_src], f32))
    hp["A_log"] = np.ascontiguousarray(np.asarray(I["L0_attn_A_log"], f32))
    hp["balance"] = np.asarray(I["L0_attn_balance_factor"], f32).reshape(1)

    rm = np.asarray(I["L0_ffn_rail_maker"], f32)
    ru = np.asarray(I["L0_ffn_rail_up_proj"], f32)
    rm_bd = np.zeros((4, 8, 128, 128), f32)
    for h in range(4):
        for g in range(8):
            for p in range(8):
                r = 8 * g + p
                rm_bd[h, g, 16 * p:16 * p + 16, 16 * p:16 * p + 16] = rm[h, r]
    hp["rm_bd"] = rm_bd
    ru_bd = np.zeros((4, 8, 4, 128, 128), f32)
    for h in range(4):
        for g in range(8):
            for mt in range(4):
                for p2 in range(2):
                    r = 8 * g + 2 * mt + p2
                    blk = ru[h, r].T
                    ru_bd[h, g, mt, 16 * (2 * mt + p2):16 * (2 * mt + p2) + 16,
                          64 * p2:64 * p2 + 64] = blk
    hp["ru_bd"] = ru_bd
    hp["cprojT"] = T_(I["L0_ffn_w_cproj"])

    hp["qkvT"] = T_(I["L1_attn_W_qkv"])
    hp["WoT"] = T_(I["L1_attn_W_o"])
    hp["wg_nat"] = np.ascontiguousarray(np.asarray(I["L1_ffn_w_gate"], f32))
    hp["wu_nat"] = np.ascontiguousarray(np.asarray(I["L1_ffn_w_up"], f32))
    hp["wd_nat"] = np.ascontiguousarray(np.asarray(I["L1_ffn_w_down"], f32))
    hp["wgT"] = T_(I["L1_ffn_w_gate"])
    hp["wuT"] = T_(I["L1_ffn_w_up"])
    hp["wdT"] = T_(I["L1_ffn_w_down"])

    hp["ident"] = np.eye(128, dtype=f32)
    sel = np.zeros((32, 32 * 128), f32)
    for r in range(32):
        sel[r, r * 128:(r + 1) * 128] = 1.0
    hp["sel32"] = sel
    e8 = np.zeros((8, 128), f32)
    for p in range(128):
        e8[p // 16, p] = 1.0
    hp["E8"] = e8
    esa = np.zeros((128, 16 * 128), f32)
    for j in range(16):
        for p in range(128):
            k = 8 * j + p // 16
            if k < 128:
                esa[k, j * 128 + p] = 1.0
    hp["EselA"] = esa
    esb = np.zeros((48, 6 * 128), f32)
    for j in range(6):
        for p in range(128):
            esb[8 * j + p // 16, j * 128 + p] = 1.0
    hp["EselB"] = esb
    msk = np.tril(np.ones((128, 128), f32)).T
    hp["mask128"] = np.ascontiguousarray(msk)
    return hp



class Ctx:
    def __init__(self, nc, P, es, fused, want_in, want_out):
        self.nc = nc
        self.P = P
        self.es = es
        self.fused = fused
        self.want_in = want_in
        self.want_out = want_out
        self.dram = {}
        self.kinds = {}
        self.uniq = 0

    def _mk(self, name, shape, dt, kind):
        if name in self.dram:
            return self.dram[name]
        ap = self.nc.dram_tensor(name, list(shape), dt, kind=kind).ap()
        self.dram[name] = ap
        self.kinds[name] = kind
        return ap

    def din(self, name, shape, dt):
        return self._mk(name, shape, dt, "ExternalInput")

    def dscratch(self, name, shape, dt):
        if self.fused:
            return self._mk(name, shape, dt, "Internal")
        kind = "ExternalOutput" if name in self.want_out else "ExternalInput"
        return self._mk(name, shape, dt, kind)

    def dout(self, name, shape, dt):
        return self._mk(name, shape, dt, "ExternalOutput")

    from contextlib import contextmanager as _cm

    @_cm
    def sub_scope(self):
        old_es = self.es
        self.es = ExitStack()
        try:
            yield
        finally:
            self.P.barrier()
            self.es.close()
            self.es = old_es

    def sb(self, name, shape, dt=F32):
        self.uniq += 1
        return self.es.enter_context(self.nc.sbuf_tensor(f"{name}_{self.uniq}", list(shape), dt))

    def psum(self, name, shape, dt=F32):
        self.uniq += 1
        return self.es.enter_context(self.nc.psum_tensor(f"{name}_{self.uniq}", list(shape), dt))

    def load(self, dst, src, key, queue="Q", deps=(), name="load"):
        return self.P.dma(queue, lambda e: e.dma_start(out=dst, in_=src),
                          reads=(), writes=[key], deps=deps, name=name)

    def store(self, src, dst, rkeys, queue="Q", deps=(), name="store"):
        return self.P.dma(queue, lambda e: e.dma_start(out=dst, in_=src),
                          reads=rkeys, writes=(), deps=deps, name=name)

    def mm(self, psum_ap, pairs, rkeys, wkey, deps=(), name="mm"):
        n = len(pairs)

        def fn(eng):
            last = None
            for i, (l, r) in enumerate(pairs):
                last = eng.matmul(psum_ap, l, r, start=(i == 0), stop=(i == n - 1))
            return last
        return self.P.step("T", fn, reads=rkeys, writes=[wkey], deps=deps, name=name)

    def vstep(self, fn, reads=(), writes=(), deps=(), name="v"):
        return self.P.step("V", fn, reads=reads, writes=writes, deps=deps, name=name)

    def sstep(self, fn, reads=(), writes=(), deps=(), name="s"):
        return self.P.step("S", fn, reads=list(reads) + ["consts"], writes=writes,
                           deps=deps, name=name)

    def tstep(self, fn, reads=(), writes=(), deps=(), name="t"):
        return self.P.step("T", fn, reads=reads, writes=writes, deps=deps, name=name)

    def gstep(self, fn, reads=(), writes=(), deps=(), name="g"):
        return self.P.step("G", fn, reads=reads, writes=writes, deps=deps, name=name)


def ceil_div(a, b):
    return (a + b - 1) // b


def emit_qdq(C, src, dst, invb, sclb, r, a, L, sgn, p, F, tag, extra_reads=(),
             skey=None, ikey=None, ckey=None, eng="V", tt_eng=None):
    v = C.vstep if eng == "V" else C.gstep
    vt = C.gstep if tt_eng == "G" else v
    s = C.sstep
    nb = F // 16
    skey = skey or (tag + "S")
    ikey = ikey or (tag + "I")
    ckey = ckey or (tag + "C")
    rk = [skey] + list(extra_reads)
    src3 = src[0:p, 0:F].rearrange("q (b k) -> q b k", k=16)
    v(lambda e: e.tensor_tensor(r[0:p, 0:F].rearrange("q (b k) -> q b k", k=16), src3, invb, ALU.mult),
      reads=rk + [ikey], writes=[tag + "r"], name=tag + "r")
    s(lambda e: e.sign(sgn[0:p, 0:F], src[0:p, 0:F]), reads=rk, writes=[tag + "g"], name=tag + "g")
    s(lambda e: e.activation(a[0:p, 0:F], r[0:p, 0:F], AF.Abs), reads=[tag + "r"], writes=[tag + "a"], name=tag + "a")
    v(lambda e: e.tensor_single_scalar(L[0:p, 0:F], a[0:p, 0:F], 0.25, ALU.is_gt),
      reads=[tag + "a"], writes=[tag + "L"], name=tag + "L")
    for thr in (0.75, 1.25, 1.75):
        v(lambda e, t=thr: e.scalar_tensor_tensor(L[0:p, 0:F], a[0:p, 0:F], t, L[0:p, 0:F], ALU.is_gt, ALU.add),
          reads=[tag + "a", tag + "L"], writes=[tag + "L"], name=tag + "L")
    v(lambda e: e.tensor_scalar(r[0:p, 0:F], a[0:p, 0:F], 5.0, 2.0, ALU.is_gt, ALU.mult),
      reads=[tag + "a", tag + "r"], writes=[tag + "H"], name=tag + "H")
    for thr in (3.5, 2.5):
        v(lambda e, t=thr: e.scalar_tensor_tensor(r[0:p, 0:F], a[0:p, 0:F], t, r[0:p, 0:F], ALU.is_gt, ALU.add),
          reads=[tag + "a", tag + "H"], writes=[tag + "H"], name=tag + "H")
    v(lambda e: e.scalar_tensor_tensor(L[0:p, 0:F], L[0:p, 0:F], 0.5, r[0:p, 0:F], ALU.mult, ALU.add),
      reads=[tag + "L", tag + "H"], writes=[tag + "q"], name=tag + "q")
    v(lambda e: e.tensor_tensor(a[0:p, 0:F].rearrange("q (b k) -> q b k", k=16),
                                sgn[0:p, 0:F].rearrange("q (b k) -> q b k", k=16), sclb, ALU.mult),
      reads=[tag + "g", ckey], writes=[tag + "ss"], name=tag + "ss")
    v(lambda e: e.tensor_tensor(dst[0:p, 0:F], L[0:p, 0:F], a[0:p, 0:F], ALU.mult),
      reads=[tag + "q", tag + "ss"], writes=[tag + "D"], name=tag + "D")


def emit_scales(C, src, scl, inv, p, F, tag, extra_reads=(), ckey=None, ikey=None, eng="V"):
    v = C.vstep if eng == "V" else C.gstep
    vr = C.vstep
    nb = F // 16
    ckey = ckey or (tag + "C")
    ikey = ikey or (tag + "I")
    rk = list(extra_reads)
    v(lambda e: e.tensor_reduce(scl[0:p, 0:nb], src[0:p, 0:F].rearrange("q (b k) -> q b k", k=16),
                                AX.X, ALU.max, apply_absolute_value=True),
      reads=rk, writes=[ckey], name=tag + "amax")
    v(lambda e: e.tensor_scalar(scl[0:p, 0:nb], scl[0:p, 0:nb], 1.0 / 6.0, 1e-12, ALU.mult, ALU.max),
      reads=[ckey], writes=[ckey], name=tag + "scl")
    vr(lambda e: e.reciprocal(inv[0:p, 0:nb], scl[0:p, 0:nb]),
      reads=[ckey], writes=[ikey], name=tag + "inv")


def bview(t, p, F):
    return t[0:p, 0:F // 16].unsqueeze(-1).broadcast_to([p, F // 16, 16])


def fview(t, p, F):
    return t[0:p, 0:F].rearrange("q (b k) -> q b k", k=16)


CONSTS = [1e-5, 1.0, 0.5, 0.7071067811865476, 30.0, -30.0, 0.125, 6.0]


class Commons:
    def __init__(self, C):
        nc = C.nc
        self.ident = C.sb("ident", (128, 128), F32)
        ident_d = C.din("ident", (128, 128), F32)
        C.load(self.ident[:], ident_d, key="ident")
        self.ones128 = C.sb("ones128", (128, 1), F32)
        self.ones1 = C.sb("ones1", (1, 128), F32)
        C.vstep(lambda e: e.memset(self.ones128[:], 1.0), writes=["ones128"], name="o128")
        C.vstep(lambda e: e.memset(self.ones1[:], 1.0), writes=["ones1"], name="o1")
        self.identb = C.sb("identb", (128, 128), BF16)
        C.vstep(lambda e: e.tensor_copy(self.identb[:], self.ident[:]),
                reads=["ident"], writes=["identb"], name="identb")
        self.cst = C.sb("cst", (128, len(CONSTS)), F32)
        cst_d = C.din("constv", (128, len(CONSTS)), F32)
        C.load(self.cst[:], cst_d, key="consts")
        for i, v in enumerate(CONSTS):
            nc.const_aps.register(F32, float(v), self.cst[:, i:i + 1])


class LnScratch:
    def __init__(self, C, W, tag):
        self.tag = tag
        self.W = W
        self.sq = [C.sb(tag + "sq", (128, W)) for _ in range(2)]
        self.ps1 = C.psum(tag + "p12", (1, W))
        self.psB = C.psum(tag + "pB", (128, W))
        self.rowm = C.sb(tag + "rm", (1, W))
        self.rowr = C.sb(tag + "rr", (1, W))
        self.rowt = C.sb(tag + "rt", (1, W))
        self.meanb = C.sb(tag + "mb", (128, W))
        self.rstdb = C.sb(tag + "rb", (128, W))


def emit_lnT(C, com, xt, keys, W, ln, eps=1e-5):
    n = len(xt)
    Cdim = n * 128
    tag = ln.tag
    sq, ps1, psB = ln.sq, ln.ps1, ln.psB
    rowm, rowr, rowt, meanb, rstdb = ln.rowm, ln.rowr, ln.rowt, ln.meanb, ln.rstdb
    assert W <= ln.W
    for i in range(n):
        C.tstep(lambda e, i=i: e.matmul(ps1[0:1, 0:W], com.ones128[:, 0:1], xt[i],
                                        start=(i == 0), stop=(i == n - 1)),
                reads=[keys[i], "ones128"], writes=[tag + "p1"], name=tag + "s1")
    C.vstep(lambda e: e.tensor_scalar_mul(rowm[0:1, 0:W], ps1[0:1, 0:W], 1.0 / Cdim),
            reads=[tag + "p1"], writes=[tag + "rm"], name=tag + "mean")
    for i in range(n):
        sq_ = sq[i % 2]
        C.sstep(lambda e, i=i, sq_=sq_: e.activation(sq_[:, 0:W], xt[i], AF.Square),
                reads=[keys[i]], writes=[(tag + "sq", i % 2)], name=tag + "sq")
        C.tstep(lambda e, i=i, sq_=sq_: e.matmul(ps1[0:1, 0:W], com.ones128[:, 0:1], sq_[:, 0:W],
                                        start=(i == 0), stop=(i == n - 1)),
                reads=[(tag + "sq", i % 2), "ones128", tag + "rm"], writes=[tag + "p1"], name=tag + "s2")
    C.vstep(lambda e: e.tensor_scalar_mul(rowr[0:1, 0:W], ps1[0:1, 0:W], 1.0 / Cdim),
            reads=[tag + "p1"], writes=[tag + "rr"], name=tag + "msq")
    C.vstep(lambda e: e.tensor_mul(rowt[0:1, 0:W], rowm[0:1, 0:W], rowm[0:1, 0:W]),
            reads=[tag + "rm"], writes=[tag + "rt"], name=tag + "m2")
    C.vstep(lambda e: e.tensor_sub(rowr[0:1, 0:W], rowr[0:1, 0:W], rowt[0:1, 0:W]),
            reads=[tag + "rr", tag + "rt"], writes=[tag + "rr"], name=tag + "var")
    C.sstep(lambda e: e.activation(rowr[0:1, 0:W], rowr[0:1, 0:W], AF.Sqrt, bias=eps),
            reads=[tag + "rr"], writes=[tag + "rr"], name=tag + "sqrt")
    C.vstep(lambda e: e.reciprocal(rowr[0:1, 0:W], rowr[0:1, 0:W]),
            reads=[tag + "rr"], writes=[tag + "rr"], name=tag + "rcp")
    extra_pb = getattr(ln, "pb_alias", [])
    C.tstep(lambda e: e.matmul(psB[:, 0:W], com.ones1[0:1, :], rowm[0:1, 0:W], start=True, stop=True),
            reads=[tag + "rm", "ones1"], writes=[tag + "pB"] + extra_pb, name=tag + "bm")
    C.vstep(lambda e: e.tensor_copy(meanb[:, 0:W], psB[:, 0:W]),
            reads=[tag + "pB"], writes=[tag + "mb"], name=tag + "em")
    C.tstep(lambda e: e.matmul(psB[:, 0:W], com.ones1[0:1, :], rowr[0:1, 0:W], start=True, stop=True),
            reads=[tag + "rr", "ones1"], writes=[tag + "pB"] + extra_pb, name=tag + "br")
    C.vstep(lambda e: e.tensor_copy(rstdb[:, 0:W], psB[:, 0:W]),
            reads=[tag + "pB"], writes=[tag + "rb"], name=tag + "er")
    return meanb, rstdb



def ph_p0(C, com):
    xT = C.din("xT", (D, T), F32)
    xnbf = C.dscratch("xnT_bf", (D, T), BF16)
    W = 512
    xt = [C.sb(f"p0x{i}", (128, W)) for i in range(8)]
    tmp = C.sb("p0t", (128, W))
    xnb = C.sb("p0nb", (128, W), BF16)
    ln = LnScratch(C, W, "p0ln")
    for c in range(T // W):
        keys = []
        for i in range(8):
            k = ("p0x", i)
            C.load(xt[i][:], xT[128 * i:128 * (i + 1), c * W:(c + 1) * W], key=k, name=f"p0ld{i}")
            keys.append(k)
        meanb, rstdb = emit_lnT(C, com, [t[:, 0:W] for t in xt], keys, W, ln)
        for i in range(8):
            C.vstep(lambda e, i=i: e.tensor_sub(tmp[:, 0:W], xt[i][:, 0:W], meanb[:, 0:W]),
                    reads=[("p0x", i), "p0ln" + "mb"], writes=["p0t"], name="p0sub")
            C.vstep(lambda e: e.tensor_mul(xnb[:, 0:W], tmp[:, 0:W], rstdb[:, 0:W]),
                    reads=["p0t", "p0ln" + "rb"], writes=["p0nb"], name="p0mul")
            C.store(xnb[:, 0:W], xnbf[128 * i:128 * (i + 1), c * W:(c + 1) * W],
                    rkeys=["p0nb"], name="p0st")





PHASE_IO = {
    "pw": ([], ["inprojT_bf", "outprojT_bf", "cprojT_bf", "wgqT_bf", "wuqT_bf", "wdqT_bf"]),
    "p0": (["xT"], ["xnT_bf"]),
    "p1": (["xnT_bf"], ["gT", "dT"]),
    "p2": (["xT", "xnT_bf", "gT", "dT", "inprojT_bf", "outprojT_bf"], ["accT"]),
    "p3": (["xnT_bf", "accT", "cprojT_bf"], ["x1T"]),
    "p4": (["x1T"], ["xn1T_bf", "xqT_bf"]),
    "p5": (["xn1T_bf", "x1T"], ["x2aT"]),
    "p6": (["xqT_bf", "x2aT", "wgqT_bf", "wuqT_bf", "wdqT_bf"], ["outT"]),
}



NPDT = {"float32": np.float32, "bfloat16": ml_dtypes.bfloat16}


def build_program(phases, fused):
    nc = bass.Bass("TRN2", target_bir_lowering=False)
    es = ExitStack()
    P = Prog()
    want_in, want_out = set(), set()
    for ph in phases:
        i, o = PHASE_IO[ph]
        want_in |= set(i)
        want_out |= set(o)
    C = Ctx(nc, P, es, fused, want_in, want_out)
    groups = []
    i = 0
    phl = list(phases)
    while i < len(phl):
        if phl[i] == "pw" and i + 1 < len(phl) and phl[i + 1] == "p0":
            groups.append(("pwc",))
            groups.append(("pwq", "p0"))
            i += 2
        else:
            groups.append((phl[i],))
            i += 1
    with es:
        com = Commons(C)
        first = True
        for grp in groups:
            if not first:
                P.barrier()
            first = False
            if len(phl) > 1:
                with C.sub_scope():
                    if len(grp) == 2:
                        s0 = len(P.steps)
                        globals()["ph_" + grp[0]](C, com)
                        s1 = len(P.steps)
                        globals()["ph_" + grp[1]](C, com)
                        s2 = len(P.steps)
                        a, b = P.steps[s0:s1], P.steps[s1:s2]
                        ratio = max(1, len(a) // max(1, len(b)))
                        merged, ai, bi = [], 0, 0
                        while ai < len(a) or bi < len(b):
                            merged.extend(a[ai:ai + ratio])
                            ai += ratio
                            if bi < len(b):
                                merged.append(b[bi])
                                bi += 1
                        P.steps[s0:s2] = merged
                    else:
                        globals()["ph_" + grp[0]](C, com)
            else:
                globals()["ph_" + grp[0]](C, com)
        out_dmas = [s for s in P.steps if s.engine in ("Q", "QG")]
        P.step("QSY", None, deps=out_dmas, name="final_wait")
        P.build_block(nc, es)
    return nc, C


def run_phase_group(phases, env_per_core, fused=False):
    nc, C = build_program(phases, fused)
    in_names = [n for n, k in C.kinds.items() if k == "ExternalInput"]
    out_names = [n for n, k in C.kinds.items() if k == "ExternalOutput"]
    in_maps = [{n: env[n] for n in in_names} for env in env_per_core]
    res = run_bass_kernel_spmd(nc, in_maps, core_ids=list(range(len(env_per_core))))
    for env, r in zip(env_per_core, res.results):
        for n in out_names:
            env[n] = np.asarray(r[n])
    return env_per_core


def load_wbf(C, tag, dram, R, Ccols, queue="Q"):
    nk = R // 128
    f32buf = [C.sb(tag + "f", (128, Ccols)) for _ in range(2)]
    tiles = []
    keys = []
    for k in range(nk):
        fb = f32buf[k % 2]
        fkey = (tag, "f", k % 2)
        C.load(fb[:], dram[128 * k:128 * (k + 1), 0:Ccols], key=fkey, queue=queue, name=tag + "ld")
        bt = C.sb(tag + "b", (128, Ccols), BF16)
        C.vstep(lambda e, fb=fb, bt=bt: e.tensor_copy(bt[:], fb[:]),
                reads=[fkey], writes=[(tag, "b", k)], name=tag + "cast")
        tiles.append(bt)
        keys.append((tag, "b", k))
    return tiles, keys


def load_bias_cols(C, tag, dram, n):
    nt = n // 128
    t = C.sb(tag, (128, nt))
    for j in range(nt):
        C.load(t[:, j:j + 1], dram[128 * j:128 * (j + 1)].unsqueeze(-1), key=(tag, j), name=tag + "ld")
    return t, [(tag, j) for j in range(nt)]


def emit_proj(C, tag, lhsT, lkeys, rhs, rkeys, Cout, W, nchunks, psums, evac_fn,
              chunk_list=None, pskeys=None, rhs_local=False):
    nk = len(lhsT)
    nm = Cout // 128
    if pskeys is None:
        pskeys = [(id(p), "ps") for p in psums]
    it = 0
    for m in range(nm):
        for c in (chunk_list if chunk_list is not None else range(nchunks)):
            ps = psums[it % len(psums)]
            pkey = pskeys[it % len(psums)]
            c_r = 0 if rhs_local else c
            pairs = [(lhsT[k][:, 128 * m:128 * (m + 1)], rhs[k][:, W * c_r:W * (c_r + 1)])
                     for k in range(nk)]
            C.mm(ps[:, 0:W], pairs, rkeys=lkeys + rkeys, wkey=pkey, name=tag + "mm")
            evac_fn(m, c, ps, pkey)
            it += 1



def ph_p1(C, com):
    xnbf = C.dscratch("xnT_bf", (D, T), BF16)
    gT = C.dscratch("gT", (D, T), F32)
    dT = C.dscratch("dT", (D, T), F32)
    gate_w = C.din("gateT", (D, D), F32)
    fdw1 = C.din("fdw1T", (D, D), F32)
    fdw2 = C.din("fdw2T", (D, D), F32)
    gate_b = C.din("gate_b", (D,), F32)
    fd_b1 = C.din("fd_b1", (D,), F32)
    fd_ln_g = C.din("fd_ln_g", (D,), F32)
    fd_ln_b = C.din("fd_ln_b", (D,), F32)
    fd_b2 = C.din("fd_b2", (D,), F32)

    xn = []
    xkeys = []
    for k in range(8):
        t = C.sb("p1xn", (128, T), BF16)
        C.load(t[:], xnbf[128 * k:128 * (k + 1), :], key=("p1xn", k), name="p1xnld")
        xn.append(t)
        xkeys.append(("p1xn", k))
    wg, wgk = load_wbf(C, "p1wg", gate_w, D, D)
    w1, w1k = load_wbf(C, "p1w1", fdw1, D, D)
    w2, w2k = load_wbf(C, "p1w2", fdw2, D, D)
    gb, _ = load_bias_cols(C, "p1gb", gate_b, D)
    b1, _ = load_bias_cols(C, "p1b1", fd_b1, D)
    lng, _ = load_bias_cols(C, "p1lng", fd_ln_g, D)
    lnb, _ = load_bias_cols(C, "p1lnb", fd_ln_b, D)
    b2, _ = load_bias_cols(C, "p1b2", fd_b2, D)

    W = 512
    NCH = T // W
    psums = [C.psum("p1ps", (128, W)) for _ in range(2)]
    gtmp = C.sb("p1gt", (128, W))

    def gate_evac(m, c, ps, pkey):
        C.sstep(lambda e: e.activation(gtmp[:, 0:W], ps[:, 0:W], AF.Sigmoid, bias=gb[:, m:m + 1]),
                reads=[pkey, ("p1gb", m)], writes=["p1gt"], name="p1gs")
        C.store(gtmp[:, 0:W], gT[128 * m:128 * (m + 1), W * c:W * (c + 1)],
                rkeys=["p1gt"], name="p1gst")
    emit_proj(C, "p1g", wg, wgk, xn, xkeys, D, W, NCH, psums, gate_evac)

    d1 = [C.sb("p1d1", (128, W)) for _ in range(8)]
    dlb = [C.sb("p1dl", (128, W), BF16) for _ in range(8)]
    ln = LnScratch(C, W, "p1ln")
    dtmp = C.sb("p1dt", (128, W))
    for c in range(NCH):
        def d1_evac(m, c2, ps, pkey, c0=c):
            C.vstep(lambda e, m=m: e.tensor_scalar_add(d1[m][:, 0:W], ps[:, 0:W], b1[:, m:m + 1]),
                    reads=[pkey, ("p1b1", m)], writes=[("p1d1", m)], name="p1d1e")
        emit_proj(C, "p1a", w1, w1k, xn, xkeys, D, W, NCH, psums, d1_evac, chunk_list=[c])
        meanb, rstdb = emit_lnT(C, com, [t[:, 0:W] for t in d1],
                                [("p1d1", m) for m in range(8)], W, ln)
        for m in range(8):
            C.vstep(lambda e, m=m: e.tensor_sub(dtmp[:, 0:W], d1[m][:, 0:W], meanb[:, 0:W]),
                    reads=[("p1d1", m), "p1lnmb"], writes=["p1dt"], name="p1sub")
            C.vstep(lambda e: e.tensor_mul(dtmp[:, 0:W], dtmp[:, 0:W], rstdb[:, 0:W]),
                    reads=["p1dt", "p1lnrb"], writes=["p1dt"], name="p1mul")
            C.vstep(lambda e, m=m: e.tensor_scalar(dtmp[:, 0:W], dtmp[:, 0:W],
                                                   lng[:, m:m + 1], lnb[:, m:m + 1], ALU.mult, ALU.add),
                    reads=["p1dt", ("p1lng", m), ("p1lnb", m)], writes=["p1dt"], name="p1aff")
            C.vstep(lambda e, m=m: e.tensor_scalar_max(dlb[m][:, 0:W], dtmp[:, 0:W], 0.0),
                    reads=["p1dt"], writes=[("p1dl", m)], name="p1relu")
        def d2_evac(m, c2, ps, pkey, c0=c):
            C.vstep(lambda e, m=m: e.tensor_scalar_add(dtmp[:, 0:W], ps[:, 0:W], b2[:, m:m + 1]),
                    reads=[pkey, ("p1b2", m)], writes=["p1dt"], name="p1d2e")
            C.store(dtmp[:, 0:W], dT[128 * m:128 * (m + 1), W * c0:W * (c0 + 1)],
                    rkeys=["p1dt"], name="p1dst")
        emit_proj(C, "p1b", w2, w2k, dlb, [("p1dl", m) for m in range(8)], D, W, NCH,
                  psums, d2_evac, chunk_list=[c], rhs_local=True)



def quant_weight(C, com, wn, R, Cc, qs):
    nat = C.din(wn + "_nat", (R, Cc), F32)
    wT = C.din(wn + "T", (Cc, R), F32)
    out = C.dscratch(wn + "qT_bf", (Cc, R), BF16)
    nb = Cc // 16
    ntn = R // 128
    ntT = Cc // 128
    (natf, sclbuf, invbuf, sclTA, sclTB, invTA, invTB, scl8, inv8,
     sclrep, invrep, wtile, qt, qr, qa, qL, qsg, pstr, psE) = qs
    e8 = qs_e8(C)
    for r in range(ntn):
        C.load(natf[:, 0:Cc], nat[128 * r:128 * (r + 1), :], key=("qwSS", 0), name="qwldn")
        emit_scales(C, natf[:, 0:Cc], sclbuf[:, r * nb:(r + 1) * nb],
                    invbuf[:, r * nb:(r + 1) * nb], 128, Cc, f"qs{wn}{r}",
                    extra_reads=[("qwSS", 0)], ckey=("qw_scl", r), ikey=("qw_inv", r))
    for r in range(ntn):
        for half, (b0, bw) in enumerate([(0, min(nb, 128))] + ([(128, nb - 128)] if nb > 128 else [])):
            dstS = sclTA if half == 0 else sclTB
            dstI = invTA if half == 0 else invTB
            for buf, dst, kk in ((sclbuf, dstS, "qw_scl"), (invbuf, dstI, "qw_inv")):
                C.tstep(lambda e, r=r, b0=b0, bw=bw, buf=buf: e.transpose(
                            pstr[0:bw, 0:128], buf[:, r * nb + b0:r * nb + b0 + bw], com.ident[:]),
                        reads=[(kk, r), "ident"], writes=["qw_pstr"], name="qwtr")
                C.vstep(lambda e, r=r, bw=bw, dst=dst: e.tensor_copy(
                            dst[0:bw, 128 * r:128 * (r + 1)], pstr[0:bw, 0:128]),
                        reads=["qw_pstr"], writes=[("qw_sclT", kk, half)], name="qwtre")
    esel_a, esel_b = qs_esel(C)
    for j in range(ntT):
        jp = j % 2
        if 8 * (j + 1) <= 128:
            esl = esel_a[0:(64 if ntn == 22 else 128), 128 * j:128 * (j + 1)]
            srcS, srcI = sclTA, invTA
            rows = 64 if ntn == 22 else 128
            skey2, ikey2 = ("qw_sclT", "qw_scl", 0), ("qw_sclT", "qw_inv", 0)
        else:
            esl = esel_b[0:48, 128 * (j - 16):128 * (j - 15)]
            srcS, srcI = sclTB, invTB
            rows = 48
            skey2, ikey2 = ("qw_sclT", "qw_scl", 1), ("qw_sclT", "qw_inv", 1)
        nchunk = ceil_div(R, 512)
        psE_ = psE[jp]
        for cc in range(nchunk):
            w = min(512, R - cc * 512)
            C.tstep(lambda e, cc=cc, w=w, jp=jp, psE_=psE_, esl=esl, srcS=srcS, rows=rows: e.matmul(
                        psE_[:, 0:w], esl, srcS[0:rows, cc * 512:cc * 512 + w],
                        start=True, stop=True),
                    reads=[skey2, "Esel"], writes=[("qw_psE", jp)], name="qwrep")
            C.vstep(lambda e, cc=cc, w=w, jp=jp, psE_=psE_: e.tensor_copy(
                        sclrep[jp][:, cc * 512:cc * 512 + w], psE_[:, 0:w]),
                    reads=[("qw_psE", jp)], writes=[("qw_srep", jp)], name="qwrepe")
            C.tstep(lambda e, cc=cc, w=w, jp=jp, psE_=psE_, esl=esl, srcI=srcI, rows=rows: e.matmul(
                        psE_[:, 0:w], esl, srcI[0:rows, cc * 512:cc * 512 + w],
                        start=True, stop=True),
                    reads=[ikey2, "Esel"], writes=[("qw_psE", jp)], name="qwrep")
            C.vstep(lambda e, cc=cc, w=w, jp=jp, psE_=psE_: e.tensor_copy(
                        invrep[jp][:, cc * 512:cc * 512 + w], psE_[:, 0:w]),
                    reads=[("qw_psE", jp)], writes=[("qw_irep", jp)], name="qwrepe")
        jp = j % 2
        wt_, qt_ = wtile[jp], qt[jp]
        C.load(wt_[:, 0:R], wT[128 * j:128 * (j + 1), :], key=("qwSS", jp), name="qwldT")
        HB = R // 2
        C.P.step("V", None, reads=["qw_qtb"], name="qtgate") if False else None
        C.P.step("V", lambda e: e.engine_nop(), reads=[], writes=["qw_qtb"], name="qtgate")
        for hf in range(2):
            emit_qdq(C, wt_[:, hf * HB:hf * HB + HB], qt_[:, hf * HB:hf * HB + HB],
                     fview(invrep[jp][:, hf * HB:hf * HB + HB], 128, HB),
                     fview(sclrep[jp][:, hf * HB:hf * HB + HB], 128, HB),
                     qr[jp], qa[jp], qL[jp], qsg[jp], 128, HB, f"qw_q{jp}",
                     skey=("qwSS", jp), ikey=("qw_irep", jp), ckey=("qw_srep", jp))
        C.store(qt_[:, 0:R], out[128 * j:128 * (j + 1), :], rkeys=[f"qw_q{jp}D", "qw_qtb"], name="qwst")


def qs_esel(C):
    if not hasattr(C, "_esel"):
        ta = C.sb("EselAsb", (128, 16 * 128))
        C.load(ta[:], C.din("EselA", (128, 16 * 128), F32), key="Esel", name="eselld")
        tb = C.sb("EselBsb", (48, 6 * 128))
        C.load(tb[:], C.din("EselB", (48, 6 * 128), F32), key="Esel", name="eselld")
        C._esel = (ta, tb)
    return C._esel


def qs_e8(C):
    if not hasattr(C, "_e8"):
        t = C.sb("E8sb", (8, 128))
        C.load(t[:], C.din("E8", (8, 128), F32), key="E8", name="e8ld")
        C._e8 = t
    return C._e8


def ph_pw(C, com):
    ph_pwc(C, com)
    ph_pwq(C, com)


def ph_pwc(C, com):
    with C.sub_scope():
        castf = [C.sb("pwcf", (128, 4096)) for _ in range(4)]
        castb = [C.sb("pwcb", (128, 4096), BF16) for _ in range(4)]
        for name, R, Cc in [("inprojT", 1024, 4096), ("outprojT", 2048, 1024),
                            ("cprojT", 16384, 1024)]:
            src = C.din(name, (R, Cc), F32)
            dst = C.dscratch(name + "_bf", (R, Cc), BF16)
            for k in range(R // 128):
                i = k % 4
                C.load(castf[i][:, 0:Cc], src[128 * k:128 * (k + 1), :], key=("pwf", i), name="pwld")
                est = C.vstep if k % 2 == 0 else C.sstep
                fn = (lambda e, i=i, Cc=Cc: e.tensor_copy(castb[i][:, 0:Cc], castf[i][:, 0:Cc]))                     if k % 2 == 0 else                     (lambda e, i=i, Cc=Cc: e.activation(castb[i][:, 0:Cc], castf[i][:, 0:Cc], AF.Copy))
                est(fn, reads=[("pwf", i)], writes=[("pwb", i)], name="pwc")
                C.store(castb[i][:, 0:Cc], dst[128 * k:128 * (k + 1), :],
                        rkeys=[("pwb", i)], queue="QG", name="pwst")


def ph_pwq(C, com):
    _qwt = [C.sb("q_wt", (128, 2816)) for _ in range(2)]
    qs = (_qwt[0], C.sb("q_scl", (128, 1408)), C.sb("q_inv", (128, 1408)),
          C.sb("q_sTA", (128, 2816)), C.sb("q_sTB", (48, 1024)),
          C.sb("q_iTA", (128, 2816)), C.sb("q_iTB", (48, 1024)),
          None, None,
          [C.sb("q_srep", (128, 2816)) for _ in range(2)], [C.sb("q_irep", (128, 2816)) for _ in range(2)],
          _qwt, [C.sb("q_qt", (128, 2816), BF16)] * 2,
          [C.sb("q_r", (128, 1408)) for _ in range(2)], [C.sb("q_a", (128, 1408)) for _ in range(2)],
          [C.sb("q_L", (128, 1408)) for _ in range(2)], [C.sb("q_sg", (128, 1408)) for _ in range(2)],
          C.psum("q_pstr", (128, 128)), [C.psum("q_psE0", (128, 512)), C.psum("q_psE1", (128, 512))])
    quant_weight(C, com, "wg", 2816, 1024, qs)
    C.P.barrier()
    quant_weight(C, com, "wu", 2816, 1024, qs)
    C.P.barrier()
    quant_weight(C, com, "wd", 1024, 2816, qs)



def ph_p2(C, com):
    xT = C.din("xT", (D, T), F32)
    xnbf = C.dscratch("xnT_bf", (D, T), BF16)
    gT = C.dscratch("gT", (D, T), F32)
    dT = C.dscratch("dT", (D, T), F32)
    accT = C.dscratch("accT", (D, T), F32)
    ipj_d = C.dscratch("inprojT_bf", (D, 2 * DI), BF16)
    opj_d = C.dscratch("outprojT_bf", (DI, D), BF16)
    xpd_d = C.din("xprojT_dt", (DI, DTR), F32)
    xpbc_d = C.din("xprojT_BC", (DI, 32), F32)
    dtp_d = C.din("dtprojT", (DTR, DI), F32)
    cw_d = C.din("convw", (DI, 4), F32)
    sel_d = C.din("sel32", (32, 32 * 128), F32)
    A_d = C.din("A_log", (DI, DS), F32)
    bal_d = C.din("balance", (1,), F32)

    import os
    Lc = 256
    NCH = T // Lc
    if os.environ.get('P2_NCH'):
        NCH = int(os.environ['P2_NCH'])
    STOP = int(os.environ.get('P2_STOP', '10'))
    NDT = DI // 128

    xpd, xpdk = load_wbf(C, "p2xpd", xpd_d, DI, DTR)
    xpbc, xpbck = load_wbf(C, "p2xpbc", xpbc_d, DI, 32)
    dtp = C.sb("p2dtp", (DTR, DI), BF16)
    with C.sub_scope():
        dtpf = C.sb("p2dtpf", (DTR, DI))
        C.load(dtpf[:], dtp_d, key="p2dtpf", name="p2ldtp")
        C.vstep(lambda e: e.tensor_copy(dtp[:], dtpf[:]), reads=["p2dtpf"], writes=["p2dtp"], name="p2cdtp")
    cw = C.sb("p2cw", (128, 4 * NDT))
    for dt in range(NDT):
        C.load(cw[:, 4 * dt:4 * dt + 4], cw_d[128 * dt:128 * (dt + 1), :], key=("p2cw", dt), name="p2lcw")
    convb, _ = load_bias_cols(C, "p2cb", C.din("conv_b", (DI,), F32), DI)
    dtb, _ = load_bias_cols(C, "p2dtb", C.din("dt_b", (DI,), F32), DI)
    Dpc, _ = load_bias_cols(C, "p2Dp", C.din("Dp", (DI,), F32), DI)
    ng, _ = load_bias_cols(C, "p2ng", C.din("norm_g", (D,), F32), D)
    nb_, _ = load_bias_cols(C, "p2nb", C.din("norm_b", (D,), F32), D)
    Ab = C.sb("p2A", (128, DS * NDT))
    for dt in range(NDT):
        C.load(Ab[:, DS * dt:DS * (dt + 1)], A_d[128 * dt:128 * (dt + 1), :], key=("p2A", dt), name="p2lA")
    C.sstep(lambda e: e.activation(Ab[:, :], Ab[:, :], AF.Exp),
            reads=[("p2A", dt) for dt in range(NDT)], writes=["p2Ae"], name="p2expA")
    C.vstep(lambda e: e.tensor_scalar_mul(Ab[:, :], Ab[:, :], -1.0),
            reads=["p2Ae"], writes=["p2Ae"], name="p2negA")
    ssel = C.sb("p2sel", (32, 32 * 128), BF16)
    with C.sub_scope():
        sself = C.sb("p2self", (32, 32 * 128))
        C.load(sself[:], sel_d, key="p2self", name="p2lsel")
        C.vstep(lambda e: e.tensor_copy(ssel[:], sself[:]), reads=["p2self"], writes=["p2sel"], name="p2csel")
    balv = C.sb("p2bal", (1, 1))
    C.load(balv[:], bal_d.unsqueeze(0), key="p2bal", name="p2lbal")
    C.sstep(lambda e: e.activation(balv[:], balv[:], AF.Sigmoid), reads=["p2bal"], writes=["p2bal2"], name="p2sbal")
    fcol = C.sb("p2f", (128, 1))
    fmcol = C.sb("p2fm", (128, 1))

    ipjbig = C.sb("p2ipjB", (128, 8 * 1024), BF16)
    opj = [C.sb("p2opj", (128, 2 * D), BF16) for _ in range(2)]
    xnbig = C.sb("p2xnc", (128, 8 * Lc), BF16)
    xnc = [xnbig[:, Lc * k:Lc * (k + 1)] for k in range(8)]
    xi = [C.sb("p2xi", (128, Lc + 3)) for _ in range(NDT)]
    xc = [C.sb("p2xc", (128, Lc), BF16) for _ in range(NDT)]
    svz = [C.sb("p2svz", (128, Lc), BF16) for _ in range(NDT)]
    delta = [C.sb("p2del", (128, Lc)) for _ in range(NDT)]
    y3bf = [C.sb("p2y3", (128, Lc), BF16) for _ in range(NDT)]
    LcR = Lc + 1
    h_all2 = [C.sb("p2hall", (128, DS * LcR), BF16) for _ in range(2)]
    a_all2 = [C.sb("p2aall", (128, DS * LcR), BF16) for _ in range(2)]
    Ball = C.sb("p2Ball", (128, DS * Lc), BF16)
    Call = C.sb("p2Call", (128, DS * Lc), BF16)
    hstate = C.sb("p2hst", (128, DS * NDT))
    du_t = [C.sb("p2du", (128, Lc)) for _ in range(2)]
    dub_t = [C.sb("p2dub", (128, Lc), BF16) for _ in range(2)]

    dtTbf = C.sb("p2dtT", (DTR, Lc), BF16)
    bc_sb = C.sb("p2bc", (32, Lc), BF16)
    raw = [C.sb("p2raw", (128, Lc)) for _ in range(2)]
    dt1 = [C.sb("p2dt1", (128, Lc)) for _ in range(2)]
    dt2 = [C.sb("p2dt2", (128, Lc)) for _ in range(2)]
    tmp1 = C.sb("p2t1", (128, Lc))
    ctmp1 = [C.sb("p2ct1", (128, Lc)) for _ in range(2)]
    ctmp2 = [C.sb("p2ct2", (128, Lc)) for _ in range(2)]

    comb = [C.sb("p2comb", (128, Lc)) for _ in range(8)]
    gmt = [C.sb("p2gm", (128, Lc)) for _ in range(2)]
    dmt = [C.sb("p2dm", (128, Lc)) for _ in range(2)]
    xmt = [C.sb("p2xm", (128, Lc)) for _ in range(2)]
    ln = LnScratch(C, Lc, "p2ln")
    ln.pb_alias = [("p2psO", 3)]
    psA = C.psum("p2psA", (128, Lc))
    psA2 = C.psum("p2psA2", (128, Lc))
    psX = C.psum("p2psX", (128, 2 * Lc))
    psO = [C.psum(f"p2psO{i}", (128, Lc)) for i in range(3)]
    psO.append(ln.psB)
    psums = [psA, psA2]
    pskeys = ["p2psA", "p2psA2"]
    C.tstep(lambda e: e.matmul(psX[:, 0:1], com.ones1[0:1, :], balv[0:1, 0:1], start=True, stop=True),
            reads=["p2bal2", "ones1"], writes=["p2psX"], name="p2bf")
    C.vstep(lambda e: e.tensor_copy(fcol[:], psX[:, 0:1]), reads=["p2psX"], writes=["p2f"], name="p2ef")
    C.vstep(lambda e: e.tensor_scalar(fmcol[:], fcol[:], -1.0, 1.0, ALU.mult, ALU.add),
            reads=["p2f"], writes=["p2fm"], name="p2fmc")

    C.vstep(lambda e: e.memset(hstate[:], 0.0), writes=["p2hst"], name="p2hs0")
    for par in range(2):
        C.vstep(lambda e, par=par: e.memset(a_all2[par][:, 0:DS * LcR:LcR], 0.0),
                writes=[("p2arst", par)], name="p2arst")
    for dt in range(NDT):
        C.vstep(lambda e, dt=dt: e.memset(xi[dt][:, 0:3], 0.0), writes=[("p2xih", dt)], name="p2xih0")

    for c in range(NCH):
        C.load(xnbig[:, :], xnbf.rearrange("(k p) t -> p k t", p=128)[:, :, Lc * c:Lc * (c + 1)],
               key="p2xnB", name="p2lxn")
        it = 0
        for q in range(2):
            C.load(ipjbig[:, :],
                   ipj_d.rearrange("(k p) n -> p k n", p=128)[:, :, q * 1024:(q + 1) * 1024],
                   key="p2ipjB", queue=("Q" if q == 0 else "QG"), name="p2lip")
            for mi8 in range(8):
                mi = q * 8 + mi8
                ps, pk = psums[it % len(psums)], pskeys[it % len(psums)]
                pairs = [(ipjbig[:, 1024 * k + 128 * mi8:1024 * k + 128 * (mi8 + 1)],
                          xnc[k][:, 0:Lc]) for k in range(8)]
                C.mm(ps[:, 0:Lc], pairs, rkeys=["p2ipjB", "p2xnB"], wkey=pk, name="p2ipmm")
                C.sstep(lambda e, mi=mi, ps=ps: e.activation(xi[mi][:, 3:3 + Lc], ps[:, 0:Lc], AF.Copy),
                        reads=[pk], writes=[("p2xi", mi)], name="p2ipe")
                it += 1
        for dt in range(NDT):
            cva = dt1[dt % 2]
            C.vstep(lambda e, dt=dt, cva=cva: e.tensor_scalar_mul(cva[:, 0:Lc], xi[dt][:, 0:Lc],
                                                                  cw[:, 4 * dt:4 * dt + 1]),
                    reads=[("p2xi", dt), ("p2xih", dt), ("p2cw", dt)], writes=[("p2dt1", dt % 2)], name="p2cv0")
            for j in (1, 2, 3):
                C.vstep(lambda e, dt=dt, j=j, cva=cva: e.scalar_tensor_tensor(
                            cva[:, 0:Lc], xi[dt][:, j:j + Lc], cw[:, 4 * dt + j:4 * dt + j + 1],
                            cva[:, 0:Lc], ALU.mult, ALU.add),
                        reads=[("p2xi", dt), ("p2dt1", dt % 2), ("p2cw", dt)], writes=[("p2dt1", dt % 2)], name="p2cvj")
            C.sstep(lambda e, dt=dt, cva=cva: e.activation(xc[dt][:, 0:Lc], cva[:, 0:Lc], AF.Silu,
                                                  bias=convb[:, dt:dt + 1]),
                    reads=[("p2dt1", dt % 2), ("p2cb", dt)], writes=[("p2xc", dt)], name="p2silu")
            C.vstep(lambda e, dt=dt: e.tensor_copy(xi[dt][:, 0:3], xi[dt][:, Lc:Lc + 3]),
                    reads=[("p2xi", dt)], writes=[("p2xih", dt)], name="p2hist")
        if STOP < 3:
            C.store(xc[0][:, 0:Lc], accT[0:128, Lc * c:Lc * (c + 1)], rkeys=[("p2xc", 0)], name="p2st")
            continue
        for q in range(2):
            C.load(ipjbig[:, :],
                   ipj_d.rearrange("(k p) n -> p k n", p=128)[:, :, DI + q * 1024:DI + (q + 1) * 1024],
                   key="p2ipjB", queue=("Q" if q == 0 else "QG"), name="p2lipz")
            for mi8 in range(8):
                mi = q * 8 + mi8
                ps, pk = psums[it % len(psums)], pskeys[it % len(psums)]
                pairs = [(ipjbig[:, 1024 * k + 128 * mi8:1024 * k + 128 * (mi8 + 1)],
                          xnc[k][:, 0:Lc]) for k in range(8)]
                C.mm(ps[:, 0:Lc], pairs, rkeys=["p2ipjB", "p2xnB"], wkey=pk, name="p2zmm")
                C.sstep(lambda e, mi=mi, ps=ps: e.activation(svz[mi][:, 0:Lc], ps[:, 0:Lc], AF.Silu),
                        reads=[pk], writes=[("p2svz", mi)], name="p2ze")
                it += 1
        if STOP < 4:
            C.store(xc[0][:, 0:Lc], accT[0:128, Lc * c:Lc * (c + 1)], rkeys=[("p2xc", 0)], name="p2st")
            continue
        xck = [("p2xc", k) for k in range(NDT)]
        C.mm(psX[0:DTR, 0:Lc], [(xpd[k][:, 0:DTR], xc[k][:, 0:Lc]) for k in range(NDT)],
             rkeys=xck + xpdk, wkey="p2psX", name="p2xdmm")
        C.vstep(lambda e: e.tensor_copy(dtTbf[:, 0:Lc], psX[0:DTR, 0:Lc]),
                reads=["p2psX"], writes=["p2dtT"], name="p2xde")
        C.mm(psX[0:32, Lc:2 * Lc], [(xpbc[k][:, 0:32], xc[k][:, 0:Lc]) for k in range(NDT)],
             rkeys=xck + xpbck, wkey="p2psX", name="p2bcmm")
        C.vstep(lambda e: e.tensor_copy(bc_sb[:, 0:Lc], psX[0:32, Lc:2 * Lc]),
                reads=["p2psX"], writes=["p2bc"], name="p2bce")
        if STOP < 5:
            C.store(xc[0][:, 0:Lc], accT[0:128, Lc * c:Lc * (c + 1)], rkeys=[("p2xc", 0)], name="p2st")
            continue
        def d_sum(dm):
            par = dm % 2
            t1_, t2_ = dt1[par], dt2[par]
            C.vstep(lambda e, dm=dm, t1_=t1_, t2_=t2_: e.tensor_add(delta[dm][:, 0:Lc], t1_[:, 0:Lc], t2_[:, 0:Lc]),
                    reads=[("p2dt1", par), ("p2dt2", par)], writes=[("p2del", dm)], name="p2dsum")
        for dm in range(NDT):
            par = dm % 2
            ps_d = psums[par]
            pk_d = pskeys[par]
            rw, t1_, t2_ = raw[par], dt1[par], dt2[par]
            C.tstep(lambda e, dm=dm, ps_d=ps_d: e.matmul(ps_d[:, 0:Lc], dtp[:, 128 * dm:128 * (dm + 1)],
                                              dtTbf[:, 0:Lc], start=True, stop=True),
                    reads=["p2dtT", "p2dtp"], writes=[pk_d], name="p2dmm")
            C.vstep(lambda e, dm=dm, rw=rw, ps_d=ps_d: e.tensor_scalar_add(rw[:, 0:Lc], ps_d[:, 0:Lc], dtb[:, dm:dm + 1]),
                    reads=[pk_d, ("p2dtb", dm)], writes=[("p2raw", par)], name="p2draw")
            C.vstep(lambda e, rw=rw, t1_=t1_: e.tensor_scalar_min(t1_[:, 0:Lc], rw[:, 0:Lc], 30.0),
                    reads=[("p2raw", par)], writes=[("p2dt1", par)], name="p2dmin")
            C.vstep(lambda e, rw=rw, t2_=t2_: e.tensor_scalar(t2_[:, 0:Lc], rw[:, 0:Lc], -30.0, 0.0, ALU.add, ALU.max),
                    reads=[("p2raw", par)], writes=[("p2dt2", par)], name="p2dtail")
            C.sstep(lambda e, t1_=t1_: e.activation(t1_[:, 0:Lc], t1_[:, 0:Lc], AF.Exp),
                    reads=[("p2dt1", par)], writes=[("p2dt1", par)], name="p2dexp")
            C.sstep(lambda e, t1_=t1_: e.activation(t1_[:, 0:Lc], t1_[:, 0:Lc], AF.Ln, bias=1.0),
                    reads=[("p2dt1", par)], writes=[("p2dt1", par)], name="p2dln")
            if dm >= 1:
                d_sum(dm - 1)
        d_sum(NDT - 1)
        if STOP < 6:
            C.store(delta[0][:, 0:Lc], accT[0:128, Lc * c:Lc * (c + 1)], rkeys=[("p2del", 0)], name="p2st")
            continue
        bci = 0
        for s in range(DS):
            for which, dstb, selc in ((0, Ball, s), (1, Call, 16 + s)):
                ps_b, pk_b = psums[bci % 2], pskeys[bci % 2]
                bci += 1
                C.tstep(lambda e, selc=selc, ps_b=ps_b: e.matmul(
                            ps_b[:, 0:Lc], ssel[:, 128 * selc:128 * (selc + 1)],
                            bc_sb[0:32, 0:Lc], start=True, stop=True),
                        reads=["p2bc", "p2sel"], writes=[pk_b], name="p2Bmm")
                C.sstep(lambda e, s=s, dstb=dstb, ps_b=ps_b: e.activation(
                            dstb[:, Lc * s:Lc * (s + 1)], ps_b[:, 0:Lc], AF.Copy),
                        reads=[pk_b], writes=["p2Ball" if dstb is Ball else "p2Call"], name="p2Be")
        if STOP < 7:
            C.store(delta[0][:, 0:Lc], accT[0:128, Lc * c:Lc * (c + 1)], rkeys=[("p2del", 0)], name="p2st")
            continue
        def scan_pre(dt):
            par = dt % 2
            du_, dub_, h_all = du_t[par], dub_t[par], h_all2[par]
            C.vstep(lambda e, dt=dt, du_=du_: e.tensor_mul(du_[:, 0:Lc], delta[dt][:, 0:Lc], xc[dt][:, 0:Lc]),
                    reads=[("p2del", dt), ("p2xc", dt)], writes=[("p2du", par)], name="p2dut")
            C.vstep(lambda e, du_=du_, dub_=dub_: e.tensor_copy(dub_[:, 0:Lc], du_[:, 0:Lc]),
                    reads=[("p2du", par)], writes=[("p2dub", par)], name="p2dub")
            dub3 = dub_[:, 0:Lc].unsqueeze(1).broadcast_to([128, DS, Lc])
            hsl = h_all[:, 0:DS * LcR].rearrange("p (s u) -> p s u", u=LcR)[:, :, 1:LcR]
            C.vstep(lambda e, dub3=dub3, hsl=hsl: e.tensor_tensor(
                        hsl, dub3,
                        Ball[:, :].rearrange("p (s t) -> p s t", s=DS), ALU.mult),
                    reads=[("p2dub", par), "p2Ball"] + [("p2h", par, s) for s in range(DS)],
                    writes=[("p2h", par, s) for s in range(DS)] + [("p2hm", par)], name="p2bmul")
            C.vstep(lambda e, dt=dt, h_all=h_all: e.tensor_copy(
                        h_all[:, 0:DS * LcR:LcR], hstate[:, DS * dt:DS * (dt + 1)]),
                    reads=["p2hst"], writes=[("p2h", par, "rst")], name="p2seed")

        scan_pre(0)
        for dt in range(NDT):
            par = dt % 2
            du_, dub_, h_all = du_t[par], dub_t[par], h_all2[par]
            t1_ = ctmp1[par]
            if dt + 1 < NDT:
                scan_pre(dt + 1)
            a_all = a_all2[par]
            for s in range(DS):
                C.sstep(lambda e, dt=dt, s=s, a_all=a_all: e.activation(
                            a_all[:, LcR * s + 1:LcR * (s + 1)], delta[dt][:, 0:Lc], AF.Exp,
                            scale=Ab[:, DS * dt + s:DS * dt + s + 1]),
                        reads=[("p2del", dt), "p2Ae", ("p2arst", par)],
                        writes=[("p2a", par)], name="p2aexp")
            C.vstep(lambda e, h_all=h_all, a_all=a_all: e.tensor_tensor_scan(
                        h_all[:, 0:DS * LcR], a_all[:, 0:DS * LcR],
                        h_all[:, 0:DS * LcR], 0.0, ALU.mult, ALU.add),
                    reads=[("p2a", par), ("p2arst", par), ("p2h", par, "rst")] +
                          [("p2h", par, s) for s in range(DS)],
                    writes=[("p2h", par, s) for s in range(DS)] + [("p2hm", par)], name="p2scan")
            C.vstep(lambda e, dt=dt, h_all=h_all: e.tensor_copy(
                        hstate[:, DS * dt:DS * (dt + 1)],
                        h_all[:, LcR - 1:DS * LcR:LcR]),
                    reads=[("p2h", par, s) for s in range(DS)], writes=["p2hst"], name="p2hupd")
            hsl2 = h_all[:, 0:DS * LcR].rearrange("p (s u) -> p s u", u=LcR)[:, :, 1:LcR]
            C.vstep(lambda e, hsl2=hsl2: e.tensor_tensor(hsl2, hsl2,
                        Call[:, :].rearrange("p (s t) -> p s t", s=DS), ALU.mult),
                    reads=[("p2h", par, s) for s in range(DS)] + ["p2Call", "p2hst"],
                    writes=[("p2hm", par)] + [("p2h", par, s) for s in range(DS)], name="p2ymul")
            C.vstep(lambda e, du_=du_, h_all=h_all: e.tensor_reduce(du_[:, 0:Lc],
                        h_all[:, 0:DS * LcR].rearrange("p (s u) -> p u s", u=LcR)[:, 1:LcR, :],
                        AX.X, ALU.add),
                    reads=[("p2hm", par)], writes=[("p2du", par)], name="p2yred")
            C.vstep(lambda e, dt=dt, du_=du_, t1_=t1_: e.scalar_tensor_tensor(
                        t1_[:, 0:Lc], xc[dt][:, 0:Lc], Dpc[:, dt:dt + 1], du_[:, 0:Lc],
                        ALU.mult, ALU.add),
                    reads=[("p2xc", dt), ("p2Dp", dt), ("p2du", par)], writes=[("p2ct1", par)], name="p2yD")
            C.vstep(lambda e, dt=dt, t1_=t1_: e.tensor_mul(y3bf[dt][:, 0:Lc], t1_[:, 0:Lc], svz[dt][:, 0:Lc]),
                    reads=[("p2ct1", par), ("p2svz", dt)], writes=[("p2y3", dt)], name="p2y3")
        if STOP < 8:
            C.vstep(lambda e: e.tensor_copy(tmp1[:, 0:Lc], y3bf[0][:, 0:Lc]),
                    reads=[("p2y3", 0)], writes=["p2t1"], name="p2dbgc")
            C.store(tmp1[:, 0:Lc], accT[0:128, Lc * c:Lc * (c + 1)], rkeys=["p2t1"], name="p2st")
            continue
        y3k = [("p2y3", k) for k in range(NDT)]
        for half in range(2):
            for k4 in range(NDT // 2):
                kk = k4 % 2
                C.load(opj[kk][:, :],
                       opj_d.rearrange("(g p) n -> p g n", p=128)[:, 2 * k4:2 * (k4 + 1), :],
                       key=("p2opj", kk), queue=("Q" if k4 % 2 == 0 else "QG"), name="p2lop")
                for ks in range(2):
                    k = 2 * k4 + ks
                    for mm4 in range(4):
                        m = 4 * half + mm4
                        po = psO[mm4]
                        wk = [("p2psO", mm4)] + (["p2lnpB"] if mm4 == 3 else [])
                        C.tstep(lambda e, m=m, k=k, ks=ks, kk=kk, po=po: e.matmul(
                                    po[:, 0:Lc], opj[kk][:, D * ks + 128 * m:D * ks + 128 * (m + 1)],
                                    y3bf[k][:, 0:Lc], start=(k == 0), stop=(k == NDT - 1),
                                    skip_group_check=True),
                                reads=[("p2opj", kk), y3k[k]], writes=wk, name="p2omm")
            for mm4 in range(4):
                m = 4 * half + mm4
                C.vstep(lambda e, mm4=mm4, m=m: e.tensor_copy(comb[m][:, 0:Lc], psO[mm4][:, 0:Lc]),
                        reads=[("p2psO", mm4)], writes=[("p2mps", m)], name="p2oev")
        if STOP < 9:
            C.store(comb[0][:, 0:Lc], accT[0:128, Lc * c:Lc * (c + 1)], rkeys=[("p2mps", 0)], name="p2st")
            continue
        for m in range(8):
            par = m % 2
            t1_, t2_ = ctmp1[par], ctmp2[par]
            gm_, dm_ = gmt[par], dmt[par]
            C.load(gm_[:], gT[128 * m:128 * (m + 1), Lc * c:Lc * (c + 1)], key=("p2gm", par),
                   queue="QG", name="p2lg")
            C.load(dm_[:], dT[128 * m:128 * (m + 1), Lc * c:Lc * (c + 1)], key=("p2dm", par),
                   queue="QG", name="p2ld")
            C.vstep(lambda e, m=m, t1_=t1_: e.tensor_scalar_mul(t1_[:, 0:Lc], comb[m][:, 0:Lc],
                                                                fcol[:, 0:1]),
                    reads=[("p2mps", m), "p2f"], writes=[("p2ct1", par)], name="p2cf")
            C.vstep(lambda e, t1_=t1_, dm_=dm_: e.scalar_tensor_tensor(t1_[:, 0:Lc], dm_[:, 0:Lc], fmcol[:, 0:1],
                                                                       t1_[:, 0:Lc], ALU.mult, ALU.add),
                    reads=[("p2dm", par), "p2fm", ("p2ct1", par)], writes=[("p2ct1", par)], name="p2cd")
            C.vstep(lambda e, t2_=t2_, gm_=gm_: e.tensor_scalar(t2_[:, 0:Lc], gm_[:, 0:Lc], -1.0, 1.0, ALU.mult, ALU.add),
                    reads=[("p2gm", par)], writes=[("p2ct2", par)], name="p2cg")
            C.vstep(lambda e, t1_=t1_, t2_=t2_: e.tensor_mul(t1_[:, 0:Lc], t1_[:, 0:Lc], t2_[:, 0:Lc]),
                    reads=[("p2ct1", par), ("p2ct2", par)], writes=[("p2ct1", par)], name="p2cm")
            C.vstep(lambda e, m=m, t2_=t2_, gm_=gm_: e.tensor_mul(t2_[:, 0:Lc], xnc[m][:, 0:Lc], gm_[:, 0:Lc]),
                    reads=["p2xnB", ("p2gm", par)], writes=[("p2ct2", par)], name="p2cx")
            C.vstep(lambda e, m=m, t1_=t1_, t2_=t2_: e.tensor_add(comb[m][:, 0:Lc], t1_[:, 0:Lc], t2_[:, 0:Lc]),
                    reads=[("p2ct1", par), ("p2ct2", par)], writes=[("p2comb", m)], name="p2cc")
        if STOP < 10:
            C.store(comb[0][:, 0:Lc], accT[0:128, Lc * c:Lc * (c + 1)], rkeys=[("p2comb", 0)], name="p2st")
            continue
        meanb, rstdb = emit_lnT(C, com, [t[:, 0:Lc] for t in comb],
                                [("p2comb", m) for m in range(8)], Lc, ln)
        for m in range(8):
            par = m % 2
            t1_, t2_, xm_ = ctmp1[par], ctmp2[par], xmt[par]
            C.vstep(lambda e, m=m, t1_=t1_: e.tensor_sub(t1_[:, 0:Lc], comb[m][:, 0:Lc], meanb[:, 0:Lc]),
                    reads=[("p2comb", m), "p2lnmb"], writes=[("p2ct1", par)], name="p2nsub")
            C.vstep(lambda e, t1_=t1_: e.tensor_mul(t1_[:, 0:Lc], t1_[:, 0:Lc], rstdb[:, 0:Lc]),
                    reads=[("p2ct1", par), "p2lnrb"], writes=[("p2ct1", par)], name="p2nmul")
            C.vstep(lambda e, m=m, t1_=t1_: e.tensor_scalar(t1_[:, 0:Lc], t1_[:, 0:Lc],
                                                            ng[:, m:m + 1], nb_[:, m:m + 1], ALU.mult, ALU.add),
                    reads=[("p2ct1", par), ("p2ng", m), ("p2nb", m)], writes=[("p2ct1", par)], name="p2naff")
            C.load(xm_[:], xT[128 * m:128 * (m + 1), Lc * c:Lc * (c + 1)], key=("p2xm", par),
                   queue="QG", name="p2lx")
            C.vstep(lambda e, t1_=t1_, t2_=t2_, xm_=xm_: e.tensor_add(t2_[:, 0:Lc], t1_[:, 0:Lc], xm_[:, 0:Lc]),
                    reads=[("p2ct1", par), ("p2xm", par)], writes=[("p2ct2", par)], name="p2acc")
            C.store(t2_[:, 0:Lc], accT[128 * m:128 * (m + 1), Lc * c:Lc * (c + 1)],
                    rkeys=[("p2ct2", par)], name="p2st")



def ph_p3(C, com):
    xnbf = C.dscratch("xnT_bf", (D, T), BF16)
    accT = C.dscratch("accT", (D, T), F32)
    x1T = C.dscratch("x1T", (D, T), F32)
    cpj_d = C.dscratch("cprojT_bf", (16 * D, D), BF16)
    rm_d = C.din("rm_bd", (4, 8, 128, 128), F32)
    ru_d = C.din("ru_bd", (4, 8, 4, 128, 128), F32)

    import os
    Lc = 256
    NCH = T // Lc
    if os.environ.get('P3_NCH'):
        NCH = int(os.environ['P3_NCH'])

    xn = []
    for k in range(8):
        t = C.sb("p3xn", (128, T), BF16)
        C.load(t[:], xnbf[128 * k:128 * (k + 1), :], key=("p3xn", k), name="p3lxn")
        xn.append(t)
    rm = {}
    ru = {}
    for h in range(4):
        for g in range(8):
            rm[h, g] = C.sb("p3rm", (128, 128), BF16)
            for mt in range(4):
                ru[h, g, mt] = C.sb("p3ru", (128, 128), BF16)
    with C.sub_scope():
        stage = [C.sb("p3stg", (128, 128)) for _ in range(2)]
        si = 0
        for h in range(4):
            for g in range(8):
                st_ = stage[si % 2]
                C.load(st_[:], rm_d[h, g], key=("p3stg", si % 2), name="p3lrm")
                C.vstep(lambda e, t=rm[h, g], st_=st_: e.tensor_copy(t[:], st_[:]),
                        reads=[("p3stg", si % 2)], writes=[("p3rm", h, g)], name="p3crm")
                si += 1
                for mt in range(4):
                    st_ = stage[si % 2]
                    C.load(st_[:], ru_d[h, g, mt], key=("p3stg", si % 2), name="p3lru")
                    C.vstep(lambda e, t2=ru[h, g, mt], st_=st_: e.tensor_copy(t2[:], st_[:]),
                            reads=[("p3stg", si % 2)], writes=[("p3ru", h, g, mt)], name="p3cru")
                    si += 1
    y1h = [C.sb("p3y1", (128, Lc), BF16) for _ in range(8)]
    z1c = [C.sb("p3z1", (128, Lc), BF16) for _ in range(128)]
    wc = [C.sb("p3wc", (128, 4 * D), BF16) for _ in range(2)]
    amt = [C.sb("p3am", (128, Lc)) for _ in range(2)]
    x1t = [C.sb("p3x1", (128, Lc)) for _ in range(2)]
    psY = [C.psum("p3psY0", (128, Lc)), C.psum("p3psY1", (128, Lc))]
    psZ = [C.psum("p3psZ0", (128, Lc)), C.psum("p3psZ1", (128, Lc))]
    psC = [C.psum(f"p3psC{b}", (128, 2 * Lc)) for b in range(4)]

    for c in range(NCH):
        for h in range(4):
            for g in range(8):
                pY, pYk = psY[g % 2], ("p3psY", g % 2)
                C.tstep(lambda e, h=h, g=g, c=c, pY=pY: e.matmul(
                            pY[:, 0:Lc], rm[h, g][:], xn[g][:, Lc * c:Lc * (c + 1)],
                            start=True, stop=True),
                        reads=[("p3rm", h, g), ("p3xn", g)], writes=[pYk], name="p3y1mm")
                C.vstep(lambda e, g=g, pY=pY: e.tensor_copy(y1h[g][:, 0:Lc], pY[:, 0:Lc]),
                        reads=[pYk], writes=[("p3y1", g)], name="p3y1e")
            zi = 0
            for g in range(8):
                for mt in range(4):
                    pZ, pZk = psZ[zi % 2], ("p3psZ", zi % 2)
                    zi += 1
                    C.tstep(lambda e, h=h, g=g, mt=mt, pZ=pZ: e.matmul(
                                pZ[:, 0:Lc], ru[h, g, mt][:], y1h[g][:, 0:Lc],
                                start=True, stop=True),
                            reads=[("p3ru", h, g, mt), ("p3y1", g)], writes=[pZk], name="p3z1mm")
                    kt = h * 32 + g * 4 + mt
                    C.sstep(lambda e, kt=kt, pZ=pZ: e.activation(z1c[kt][:, 0:Lc], pZ[:, 0:Lc], AF.Gelu),
                            reads=[pZk], writes=[("p3z1", kt)], name="p3gelu")
        for k4 in range(32):
            kk = k4 % 2
            C.load(wc[kk][:, :],
                   cpj_d.rearrange("(g p) n -> p g n", p=128)[:, 4 * k4:4 * (k4 + 1), :],
                   key=("p3wc", kk), queue=("Q" if k4 % 2 == 0 else "QG"), name="p3lwc")
            for ks in range(4):
                kt = 4 * k4 + ks

                def cfn(e, kt=kt, ks=ks, kk=kk):
                    last = None
                    for m in range(8):
                        b = m // 2
                        off = (m % 2) * Lc
                        last = e.matmul(
                            psC[b][:, off:off + Lc], wc[kk][:, D * ks + 128 * m:D * ks + 128 * (m + 1)],
                            z1c[kt][:, 0:Lc],
                            start=(kt == 0 and m % 2 == 0), stop=(kt == 127 and m % 2 == 1),
                            skip_group_check=True)
                    return last
                C.tstep(cfn, reads=[("p3wc", kk), ("p3z1", kt)],
                        writes=[("p3psC", b) for b in range(4)], name="p3cmm")
        for m in range(8):
            b = m // 2
            off = (m % 2) * Lc
            par = m % 2
            C.load(amt[par][:, 0:Lc], accT[128 * m:128 * (m + 1), Lc * c:Lc * (c + 1)],
                   key=("p3am", par), queue="QG", name="p3lac")
            C.vstep(lambda e, b=b, off=off, par=par: e.tensor_add(
                        x1t[par][:, 0:Lc], psC[b][:, off:off + Lc], amt[par][:, 0:Lc]),
                    reads=[("p3psC", b), ("p3am", par)], writes=[("p3x1", par)], name="p3add")
            C.store(x1t[par][:, 0:Lc], x1T[128 * m:128 * (m + 1), Lc * c:Lc * (c + 1)],
                    rkeys=[("p3x1", par)], name="p3st")



def ph_p4(C, com):
    x1T = C.dscratch("x1T", (D, T), F32)
    xn1bf = C.dscratch("xn1T_bf", (D, T), BF16)
    xqbf = C.dscratch("xqT_bf", (D, T), BF16)
    W = 512
    NCH = T // W
    x1 = [C.sb("p4x1", (128, W)) for _ in range(8)]
    xn1 = [C.sb("p4xn", (128, W)) for _ in range(8)]
    xnb = C.sb("p4xnb", (128, W), BF16)
    ln = LnScratch(C, W, "p4ln")
    xtok = [C.sb("p4tok", (128, D)) for _ in range(4)]
    xqtok = [C.sb("p4qt", (128, D), BF16) for _ in range(4)]
    scl = C.sb("p4scl", (128, D // 16))
    inv = C.sb("p4inv", (128, D // 16))
    qr = C.sb("p4qr", (128, D))
    qa = C.sb("p4qa", (128, D))
    qL = C.sb("p4qL", (128, D))
    qsg = C.sb("p4qsg", (128, D))
    xqT = C.sb("p4xqT", (128, W), BF16)
    pstr = [C.psum("p4pstr0", (128, 128)), C.psum("p4pstr1", (128, 128))]
    pstrb = [C.psum("p4pstrb0", (128, 128), BF16), C.psum("p4pstrb1", (128, 128), BF16)]
    for c in range(NCH):
        keys = []
        for i in range(8):
            k = ("p4x1", i)
            C.load(x1[i][:], x1T[128 * i:128 * (i + 1), c * W:(c + 1) * W], key=k, name="p4lx")
            keys.append(k)
        meanb, rstdb = emit_lnT(C, com, [t[:, 0:W] for t in x1], keys, W, ln)
        for i in range(8):
            C.vstep(lambda e, i=i: e.tensor_sub(xn1[i][:, 0:W], x1[i][:, 0:W], meanb[:, 0:W]),
                    reads=[("p4x1", i), "p4lnmb"], writes=[("p4f32", i)], name="p4sub")
            C.vstep(lambda e, i=i: e.tensor_mul(xn1[i][:, 0:W], xn1[i][:, 0:W], rstdb[:, 0:W]),
                    reads=[("p4f32", i), "p4lnrb"], writes=[("p4f32", i)], name="p4mul")
            C.vstep(lambda e, i=i: e.tensor_copy(xnb[:, 0:W], xn1[i][:, 0:W]),
                    reads=[("p4f32", i)], writes=["p4xnb"], name="p4cb")
            C.store(xnb[:, 0:W], xn1bf[128 * i:128 * (i + 1), c * W:(c + 1) * W],
                    rkeys=["p4xnb"], name="p4st")
        trl = [(tt, ft) for tt in range(4) for ft in range(8)]
        for i, (tt, ft) in enumerate(trl):
            par = i % 2
            C.tstep(lambda e, tt=tt, ft=ft, par=par: e.transpose(
                        pstr[par][:, 0:128], xn1[ft][:, 128 * tt:128 * (tt + 1)], com.ident[:]),
                    reads=[("p4f32", ft), "ident"], writes=[("p4pstr", par)], name="p4tr")
            C.vstep(lambda e, tt=tt, ft=ft, par=par: e.tensor_copy(
                        xtok[tt][:, 128 * ft:128 * (ft + 1)], pstr[par][:, 0:128]),
                    reads=[("p4pstr", par)], writes=[("p4tok", tt)], name="p4tre")
        for tt in range(4):
            emit_scales(C, xtok[tt][:, 0:D], scl, inv, 128, D, "p4s",
                        extra_reads=[("p4tok", tt)], ckey="p4sC", ikey="p4sI")
            emit_qdq(C, xtok[tt][:, 0:D], xqtok[tt][:, 0:D], bview(inv, 128, D), bview(scl, 128, D),
                     qr, qa, qL, qsg, 128, D, "p4q",
                     skey=("p4tok", tt), ikey="p4sI", ckey="p4sC", tt_eng="G")
            C.P.last_write[("p4qd", tt)] = C.P.last_write["p4qD"]
        for ft in range(8):
            for tt in range(4):
                par = tt % 2
                C.tstep(lambda e, tt=tt, ft=ft, par=par: e.transpose(
                            pstrb[par][:, 0:128], xqtok[tt][:, 128 * ft:128 * (ft + 1)], com.identb[:]),
                        reads=[("p4qd", tt), "identb", "p4qD"], writes=[("p4pstrb", par)], name="p4trq")
                C.vstep(lambda e, tt=tt, ft=ft, par=par: e.tensor_copy(
                            xqT[:, 128 * tt:128 * (tt + 1)], pstrb[par][:, 0:128]),
                        reads=[("p4pstrb", par)], writes=["p4xqT"], name="p4trqe")
            C.store(xqT[:, 0:W], xqbf[128 * ft:128 * (ft + 1), c * W:(c + 1) * W],
                    rkeys=["p4xqT"], name="p4stq")



def ph_p6(C, com):
    xqbf = C.dscratch("xqT_bf", (D, T), BF16)
    x2aT = C.dscratch("x2aT", (D, T), F32)
    outT = C.dout("outT", (D, T), F32)
    wgq_d = C.dscratch("wgqT_bf", (D, 2816), BF16)
    wuq_d = C.dscratch("wuqT_bf", (D, 2816), BF16)
    wdq_d = C.dscratch("wdqT_bf", (2816, D), BF16)
    I2816 = 2816
    NKD = 22

    xq = []
    for k in range(8):
        t = C.sb("p6xq", (128, T), BF16)
        C.load(t[:], xqbf[128 * k:128 * (k + 1), :], key=("p6xq", k), name="p6lxq")
        xq.append(t)
    wg = []
    wu = []
    for k in range(8):
        t = C.sb("p6wg", (128, I2816), BF16)
        C.load(t[:], wgq_d[128 * k:128 * (k + 1), :], key=("p6wg", k), name="p6lwg")
        wg.append(t)
        t2 = C.sb("p6wu", (128, I2816), BF16)
        C.load(t2[:], wuq_d[128 * k:128 * (k + 1), :], key=("p6wu", k), name="p6lwu")
        wu.append(t2)
    wd = [C.sb("p6wd", (128, 4 * D), BF16) for _ in range(2)]

    hbuf = C.sb("p6h", (128, I2816))
    sg = C.sb("p6sg", (128, 512))
    hq = C.sb("p6hq", (128, I2816), BF16)
    hqT = [C.sb("p6hqT", (128, 512), BF16) for _ in range(NKD)]
    HF = I2816 // 2
    scl = C.sb("p6scl", (128, 88))
    inv = C.sb("p6inv", (128, 88))
    qr = C.sb("p6qr", (128, HF))
    qa = C.sb("p6qa", (128, HF))
    qL = C.sb("p6qL", (128, HF))
    qsg = C.sb("p6qsg", (128, HF))
    x2m = C.sb("p6x2", (128, 512))
    om = C.sb("p6om", (128, 512))
    x2m2 = [x2m, x2m]
    om2 = [om, om]
    psG = [C.psum("p6psG0", (128, 512)), C.psum("p6psG1", (128, 512))]
    psU = [C.psum("p6psU0", (128, 512)), C.psum("p6psU1", (128, 512))]
    psD = [C.psum(f"p6psD{i}", (128, 512)) for i in range(2)]
    psT = [C.psum("p6psT0", (128, 128), BF16), C.psum("p6psT1", (128, 128), BF16)]
    sg2 = [sg, C.sb("p6sg2", (128, 512))]

    NW = [512] * 5 + [256]
    for tg in range(4):
        for t4 in range(4):
            tt = tg * 4 + t4
            offs = []
            o = 0
            for wn in NW:
                offs.append((o, wn))
                o += wn

            def gu_part(wi):
                off, wn = offs[wi]
                par = wi % 2
                for kk in range(8):
                    C.tstep(lambda e, tt=tt, kk=kk, off=off, wn=wn, par=par: e.matmul(
                                psG[par][:, 0:wn], xq[kk][:, 128 * tt:128 * (tt + 1)],
                                wg[kk][:, off:off + wn], start=(kk == 0), stop=(kk == 7)),
                            reads=[("p6xq", kk), ("p6wg", kk)], writes=[("p6psG", par)], name="p6gmm")
                for kk in range(8):
                    C.tstep(lambda e, tt=tt, kk=kk, off=off, wn=wn, par=par: e.matmul(
                                psU[par][:, 0:wn], xq[kk][:, 128 * tt:128 * (tt + 1)],
                                wu[kk][:, off:off + wn], start=(kk == 0), stop=(kk == 7)),
                            reads=[("p6xq", kk), ("p6wu", kk)], writes=[("p6psU", par)], name="p6umm")

            def sh_part(wi):
                off, wn = offs[wi]
                par = wi % 2
                sgp = sg2[par]
                sgk = ("p6sg", par)
                C.sstep(lambda e, wn=wn, sgp=sgp, par=par: e.activation(sgp[:, 0:wn], psG[par][:, 0:wn], AF.Silu),
                        reads=[("p6psG", par)], writes=[sgk], name="p6silu")
                C.vstep(lambda e, off=off, wn=wn, sgp=sgp, par=par: e.tensor_mul(
                            hbuf[:, off:off + wn], sgp[:, 0:wn], psU[par][:, 0:wn]),
                        reads=[sgk, ("p6psU", par)], writes=["p6hb"], name="p6hmul")
            gu_part(0)
            for wi in range(1, len(NW)):
                gu_part(wi)
                sh_part(wi - 1)
            sh_part(len(NW) - 1)
            for hf in range(2):
                emit_scales(C, hbuf[:, hf * HF:(hf + 1) * HF], scl, inv, 128, HF, "p6s",
                            extra_reads=["p6hb"], ckey="p6sC", ikey="p6sI")
                emit_qdq(C, hbuf[:, hf * HF:(hf + 1) * HF], hq[:, hf * HF:(hf + 1) * HF],
                         bview(inv, 128, HF), bview(scl, 128, HF), qr, qa, qL, qsg,
                         128, HF, "p6q", skey="p6hb", ikey="p6sI", ckey="p6sC", tt_eng="G")
                C.P.last_write["p6qD2"] = C.P.last_write["p6qD"]
            for k in range(NKD):
                pT, pTk = psT[k % 2], ("p6psT", k % 2)
                C.tstep(lambda e, k=k, pT=pT: e.transpose(pT[:, 0:128], hq[:, 128 * k:128 * (k + 1)],
                                                          com.identb[:]),
                        reads=["p6qD", "p6qD2", "identb"], writes=[pTk], name="p6trh")
                C.vstep(lambda e, k=k, t4=t4, pT=pT: e.tensor_copy(hqT[k][:, 128 * t4:128 * (t4 + 1)],
                                                                   pT[:, 0:128]),
                        reads=[pTk], writes=[("p6hqT", k)], name="p6trhe")
        for half in range(4):
            for k4 in range(6):
                kk = k4 % 2
                nk = min(4, NKD - 4 * k4)
                C.load(wd[kk][:, 0:nk * D],
                       wdq_d.rearrange("(g p) n -> p g n", p=128)[:, 4 * k4:4 * k4 + nk, :],
                       key=("p6wd", kk), queue=("Q" if k4 % 2 == 0 else "QG"), name="p6lwd")
                for ks in range(nk):
                    k = 4 * k4 + ks
                    for mm2 in range(2):
                        m = 2 * half + mm2
                        C.tstep(lambda e, m=m, k=k, ks=ks, kk=kk, mm2=mm2: e.matmul(
                                    psD[mm2][:, 0:512], wd[kk][:, D * ks + 128 * m:D * ks + 128 * (m + 1)],
                                    hqT[k][:, 0:512],
                                    start=(k == 0), stop=(k == NKD - 1)),
                                reads=[("p6wd", kk)] + [("p6hqT", k)], writes=[("p6psD", mm2)], name="p6dmm")
            for mm2 in range(2):
                m = 2 * half + mm2
                C.load(x2m[:, 0:512], x2aT[128 * m:128 * (m + 1), 512 * tg:512 * (tg + 1)],
                       key="p6x2", queue="QG", name="p6lx2")
                C.vstep(lambda e, mm2=mm2: e.tensor_add(om[:, 0:512], psD[mm2][:, 0:512],
                                                        x2m[:, 0:512]),
                        reads=[("p6psD", mm2), "p6x2"], writes=["p6om"], name="p6add")
                C.store(om[:, 0:512], outT[128 * m:128 * (m + 1), 512 * tg:512 * (tg + 1)],
                        rkeys=["p6om"], name="p6st")



def ph_p5(C, com):
    xn1bf = C.dscratch("xn1T_bf", (D, T), BF16)
    x1T = C.dscratch("x1T", (D, T), F32)
    x2aT = C.dscratch("x2aT", (D, T), F32)
    qkv_d = C.din("qkvT", (D, 3 * D), F32)
    wo_d = C.din("WoT", (D, D), F32)
    bqkv_d = C.din("b_qkv", (3 * D,), F32)
    bo_d = C.din("b_o", (D,), F32)
    msk_d = C.din("mask128", (128, 128), F32)

    xn1 = []
    for k in range(8):
        t = C.sb("p5xn", (128, T), BF16)
        C.load(t[:], xn1bf[128 * k:128 * (k + 1), :], key=("p5xn", k), name="p5lxn")
        xn1.append(t)
    qkv, qkvk = load_wbf(C, "p5qkv", qkv_d, D, 3 * D)
    wo, wok = load_wbf(C, "p5wo", wo_d, D, D)
    bq = C.sb("p5bq", (64, 48))
    for j in range(48):
        C.load(bq[:, j:j + 1], bqkv_d[64 * j:64 * (j + 1)].unsqueeze(-1), key=("p5bq", j), name="p5lbq")
    bo, _ = load_bias_cols(C, "p5bo", bo_d, D)
    mskb = C.sb("p5msk", (128, 128), BF16)
    with C.sub_scope():
        mskf = C.sb("p5mskf", (128, 128))
        C.load(mskf[:], msk_d, key="p5mskf", name="p5lmsk")
        C.vstep(lambda e: e.tensor_copy(mskb[:], mskf[:]), reads=["p5mskf"], writes=["p5msk"], name="p5cmsk")

    QT = C.sb("p5QT", (64, T), BF16)
    KT = C.sb("p5KT", (64, T), BF16)
    VT = C.sb("p5VT", (64, T), BF16)
    Vtok = [C.sb("p5Vt", (128, 65), BF16) for _ in range(16)]
    PT = [C.sb("p5PT", (128, 512), BF16) for _ in range(2)]
    OT = [C.sb("p5OT", (128, T), BF16) for _ in range(8)]
    rec = C.sb("p5rec", (1, 512))
    recb = C.sb("p5recb", (64, 512))
    x1m = C.sb("p5x1m", (128, 512))
    aom = C.sb("p5aom", (128, 512))
    psQ = C.psum("p5psQ", (64, 512))
    psS = [C.psum("p5psS0", (128, 512)), C.psum("p5psS1", (128, 512))]
    psO = C.psum("p5psO", (65, 512))
    psR = C.psum("p5psR", (64, 512))
    psVt = C.psum("p5psVt", (128, 64), BF16)
    psW = C.psum("p5psW", (128, 512))

    for tt in range(16):
        C.vstep(lambda e, tt=tt: e.memset(Vtok[tt][:, 64:65], 1.0), writes=[("p5vone", tt)], name="p5one")

    for h in range(16):
        for part, dst, dkey in ((0, QT, "p5QT"), (1, KT, "p5KT"), (2, VT, "p5VT")):
            col = part * D + h * 64
            bidx = part * 16 + h
            for qc in range(4):
                pairs = [(qkv[k][:, col:col + 64], xn1[k][:, 512 * qc:512 * (qc + 1)])
                         for k in range(8)]
                C.mm(psQ[0:64, 0:512], pairs, rkeys=qkvk + [("p5xn", k) for k in range(8)],
                     wkey="p5psQ", name="p5qkvmm")
                if part == 0:
                    C.vstep(lambda e, qc=qc, bidx=bidx: e.tensor_scalar(
                                QT[0:64, 512 * qc:512 * (qc + 1)], psQ[0:64, 0:512],
                                bq[:, bidx:bidx + 1], 0.125, ALU.add, ALU.mult),
                            reads=["p5psQ", ("p5bq", bidx)], writes=[dkey], name="p5qe")
                else:
                    C.vstep(lambda e, qc=qc, bidx=bidx, dst=dst: e.tensor_scalar_add(
                                dst[0:64, 512 * qc:512 * (qc + 1)], psQ[0:64, 0:512],
                                bq[:, bidx:bidx + 1]),
                            reads=["p5psQ", ("p5bq", bidx)], writes=[dkey], name="p5kve")
        for tt in range(16):
            C.tstep(lambda e, tt=tt: e.transpose(psVt[:, 0:64], VT[0:64, 128 * tt:128 * (tt + 1)],
                                                 com.identb[0:64, 0:64]),
                    reads=["p5VT", "identb"], writes=["p5psVt"], name="p5vtr")
            C.vstep(lambda e, tt=tt: e.tensor_copy(Vtok[tt][:, 0:64], psVt[:, 0:64]),
                    reads=["p5psVt"], writes=[("p5vt", tt)], name="p5vte")
        ot = OT[h // 2]
        orow = (h % 2) * 64
        for qc in range(4):
            nkt = 4 * qc + 4

            def s_part(kt, qc=qc):
                par = kt % 2
                pS, pSk = psS[par], ("p5psS", par)
                pt_, ptk = PT[par], ("p5PT", par)
                C.tstep(lambda e, kt=kt, qc=qc, pS=pS: e.matmul(
                            pS[:, 0:512], KT[0:64, 128 * kt:128 * (kt + 1)],
                            QT[0:64, 512 * qc:512 * (qc + 1)], start=True, stop=True),
                        reads=["p5KT", "p5QT"], writes=[pSk], name="p5smm")
                scol = (kt - 4 * qc) * 128
                if kt >= 4 * qc:
                    if scol > 0:
                        C.vstep(lambda e, scol=scol, pt_=pt_: e.memset(pt_[:, 0:scol], 0.0),
                                reads=[], writes=[ptk], name="p5pz")
                    C.sstep(lambda e, scol=scol, pt_=pt_, pS=pS: e.activation(pt_[:, scol:512], pS[:, scol:512], AF.Exp),
                            reads=[pSk, ptk], writes=[ptk], name="p5exp")
                    C.vstep(lambda e, scol=scol, pt_=pt_: e.tensor_mul(pt_[:, scol:scol + 128],
                                                              pt_[:, scol:scol + 128], mskb[:]),
                            reads=[ptk, "p5msk"], writes=[ptk], name="p5msk2")
                else:
                    C.sstep(lambda e, pt_=pt_, pS=pS: e.activation(pt_[:, 0:512], pS[:, 0:512], AF.Exp),
                            reads=[pSk], writes=[ptk], name="p5exp")

            def pv_part(kt, nkt=nkt):
                par = kt % 2
                pt_, ptk = PT[par], ("p5PT", par)
                C.tstep(lambda e, kt=kt, nkt=nkt, pt_=pt_: e.matmul(
                            psO[0:65, 0:512], Vtok[kt][:, 0:65], pt_[:, 0:512],
                            start=(kt == 0), stop=(kt == nkt - 1)),
                        reads=[("p5vt", kt), ("p5vone", kt), ptk], writes=["p5psO"], name="p5pv")
            s_part(0)
            for kt in range(1, nkt):
                s_part(kt)
                pv_part(kt - 1)
            pv_part(nkt - 1)
            C.vstep(lambda e: e.reciprocal(rec[0:1, 0:512], psO[64:65, 0:512]),
                    reads=["p5psO"], writes=["p5rec"], name="p5rcp")
            C.tstep(lambda e: e.matmul(psR[0:64, 0:512], com.ones1[0:1, 0:64], rec[0:1, 0:512],
                                       start=True, stop=True),
                    reads=["p5rec", "ones1"], writes=["p5psR"], name="p5rb")
            C.vstep(lambda e: e.tensor_copy(recb[0:64, 0:512], psR[0:64, 0:512]),
                    reads=["p5psR"], writes=["p5recb"], name="p5rbe")
            C.vstep(lambda e, ot=ot, orow=orow, qc=qc: e.tensor_mul(
                        ot[orow:orow + 64, 512 * qc:512 * (qc + 1)], psO[0:64, 0:512],
                        recb[0:64, 0:512]),
                    reads=["p5psO", "p5recb"], writes=[("p5OT", h // 2)], name="p5onorm")
    for m in range(8):
        for qc in range(4):
            pairs = [(wo[k][:, 128 * m:128 * (m + 1)], OT[k][:, 512 * qc:512 * (qc + 1)])
                     for k in range(8)]
            C.mm(psW[:, 0:512], pairs, rkeys=wok + [("p5OT", k) for k in range(8)],
                 wkey="p5psW", name="p5womm")
            C.load(x1m[:, 0:512], x1T[128 * m:128 * (m + 1), 512 * qc:512 * (qc + 1)],
                   key="p5x1m", queue="QG", name="p5lx1")
            C.vstep(lambda e, m=m: e.tensor_scalar_add(aom[:, 0:512], psW[:, 0:512], bo[:, m:m + 1]),
                    reads=["p5psW", ("p5bo", m)], writes=["p5aom"], name="p5wbe")
            C.vstep(lambda e: e.tensor_add(aom[:, 0:512], aom[:, 0:512], x1m[:, 0:512]),
                    reads=["p5aom", "p5x1m"], writes=["p5aom"], name="p5res")
            C.store(aom[:, 0:512], x2aT[128 * m:128 * (m + 1), 512 * qc:512 * (qc + 1)],
                    rkeys=["p5aom"], name="p5st")



PHASE_ORDER = ["pw", "p0", "p1", "p2", "p3", "p4", "p5", "p6"]
_BUILD_CACHE = {}


def _get_program(phases, fused):
    key = (tuple(phases), fused)
    if key not in _BUILD_CACHE:
        _BUILD_CACHE[key] = build_program(phases, fused)
    return _BUILD_CACHE[key]


FUSED = True


def kernel(**inputs):
    hp = prep_common(inputs)
    hp["constv"] = np.tile(np.array(CONSTS, np.float32), (128, 1))
    x = np.asarray(inputs["L0_attn_x"], np.float32)
    envs = []
    for b in range(NB):
        env = dict(hp)
        env["xT"] = np.ascontiguousarray(x[b].T)
        envs.append(env)
    if FUSED:
        nc, C = _get_program(PHASE_ORDER, True)
        in_names = [n for n, k in C.kinds.items() if k == "ExternalInput"]
        in_maps = [{n: env[n] for n in in_names} for env in envs]
        res = run_bass_kernel_spmd(nc, in_maps, core_ids=list(range(NB)))
        out = np.stack([np.ascontiguousarray(np.asarray(r["outT"]).T) for r in res.results], axis=0)
        return out.astype(np.float32)
    for ph in PHASE_ORDER:
        nc, C = _get_program([ph], False)
        in_names = [n for n, k in C.kinds.items() if k == "ExternalInput"]
        out_names = [n for n, k in C.kinds.items() if k == "ExternalOutput"]
        in_maps = [{n: env[n] for n in in_names} for env in envs]
        res = run_bass_kernel_spmd(nc, in_maps, core_ids=list(range(NB)))
        for env, r in zip(envs, res.results):
            for n in out_names:
                env[n] = np.asarray(r[n])
    out = np.stack([np.ascontiguousarray(env["outT"].T) for env in envs], axis=0)
    return out.astype(np.float32)


# version 14 (final)
# speedup vs baseline: 1.0106x; 1.0106x over previous
import numpy as np
import ml_dtypes
import concourse.bass as bass
import concourse.mybir as mybir
from concourse.bass_utils import run_bass_kernel_spmd
from contextlib import ExitStack

F32 = mybir.dt.float32
BF16 = mybir.dt.bfloat16
AF = mybir.ActivationFunctionType
ALU = mybir.AluOpType
AX = mybir.AxisListType

T = 2048
D = 1024
DI = 2048
DS = 16
DTR = 64
NB = 8

STREAMS = ("T", "V", "S", "G", "Q")
SEM_ENGINES = ("T", "V", "S", "G", "Q", "QG", "QSY")
STREAM_OF = {"T": "T", "V": "V", "S": "S", "G": "G", "Q": "Q", "QG": "G", "QSY": "Q"}


class Step:
    __slots__ = ("engine", "stream", "fn", "deps", "inc", "marked", "val", "name")

    def __init__(self, engine, fn, deps, inc, name=""):
        self.engine = engine
        self.stream = STREAM_OF[engine]
        self.fn = fn
        self.deps = deps
        self.inc = inc
        self.marked = False
        self.val = None
        self.name = name


class Prog:

    def __init__(self):
        self.steps = []
        self.last_write = {}
        self.readers = {}

    def step(self, engine, fn, reads=(), writes=(), deps=(), name=""):
        dd = set(d for d in deps if d is not None)
        for k in reads:
            w = self.last_write.get(k)
            if w is not None:
                dd.add(w)
        for k in writes:
            w = self.last_write.get(k)
            if w is not None:
                dd.add(w)
            for r in self.readers.get(k, ()):
                dd.add(r)
        s = Step(engine, fn, dd, 16 if engine in ("Q", "QG") else 1, name)
        self.steps.append(s)
        for k in reads:
            self.readers.setdefault(k, set()).add(s)
        for k in writes:
            self.last_write[k] = s
            self.readers[k] = set()
        return s

    def dma(self, queue, fn, reads=(), writes=(), deps=(), name=""):
        assert queue in ("Q", "QG")
        return self.step(queue, fn, reads, writes, deps, name)

    def barrier(self):
        lasts = {}
        for s in self.steps:
            lasts[s.engine] = s
        prev = list(lasts.values())
        out = []
        for e in ("T", "V", "S", "G", "QSY"):
            out.append(self.step(e, None, deps=prev, name="barrier"))
        return out

    def emit(self, nc, es):
        for s in self.steps:
            for d in s.deps:
                d.marked = True
        cnt = dict.fromkeys(SEM_ENGINES, 0)
        for s in self.steps:
            if s.marked:
                cnt[s.engine] += s.inc
                s.val = cnt[s.engine]
        epoch_end = {}
        fence = {}
        cur = {"Q": [], "QG": []}
        closed = {"Q": 0, "QG": 0}

        def close_epoch(q):
            if cur[q]:
                end = cur[q][-1].val
                for m in cur[q]:
                    epoch_end[m] = end
                closed[q] = end
                cur[q].clear()
        for s in self.steps:
            if s.engine in ("Q", "QG"):
                q = s.engine
                if s.deps and cur[q]:
                    close_epoch(q)
                if not cur[q] and closed[q] > 0:
                    fence[s] = closed[q]
                cur[q].append(s)
            for d in s.deps:
                if d.engine in ("Q", "QG") and d is not s and d in cur[d.engine] \
                   and not (s.engine == d.engine):
                    close_epoch(d.engine)
        for q in ("Q", "QG"):
            if cur[q]:
                end = cur[q][-1].val
                for m in cur[q]:
                    epoch_end[m] = end
        sems = {e: es.enter_context(nc.semaphore(f"sem_{e}")) for e in SEM_ENGINES}
        per_stream = {e: [s for s in self.steps if s.stream == e] for e in STREAMS}
        watermark = {}

        def run_stream(ename, eng):
            wm = watermark.setdefault(ename, dict.fromkeys(SEM_ENGINES, 0))
            for s in per_stream[ename]:
                if s in fence:
                    v = fence[s]
                    if v > wm[s.engine]:
                        eng.wait_ge(sems[s.engine], v)
                        wm[s.engine] = v
                for d in sorted(s.deps, key=lambda d: (d.engine, d.val or 0)):
                    assert d.marked and d.val is not None, (s.name, d.name)
                    v = epoch_end[d] if d.engine in ("Q", "QG") else d.val
                    if v > wm[d.engine]:
                        eng.wait_ge(sems[d.engine], v)
                        wm[d.engine] = v
                inst = s.fn(eng) if s.fn is not None else None
                if s.marked:
                    if inst is None:
                        eng.sem_inc(sems[s.engine], s.inc)
                    else:
                        inst.then_inc(sems[s.engine], s.inc)

        return run_stream

    def build_block(self, nc, es):
        run_stream = self.emit(nc, es)
        with nc.Block() as block:
            @block.tensor
            def _(e):
                run_stream("T", e)

            @block.vector
            def _(e):
                run_stream("V", e)

            @block.scalar
            def _(e):
                run_stream("S", e)

            @block.gpsimd
            def _(e):
                run_stream("G", e)

            @block.sync
            def _(e):
                run_stream("Q", e)



def prep_common(I):
    hp = {}
    f32 = np.float32

    def T_(a):
        return np.ascontiguousarray(np.asarray(a, f32).T)

    hp["inprojT"] = T_(I["L0_attn_in_proj_w"])
    hp["gateT"] = T_(I["L0_attn_gate_w"])
    hp["fdw1T"] = T_(I["L0_attn_fd_w1"])
    hp["fdw2T"] = T_(I["L0_attn_fd_w2"])
    hp["outprojT"] = T_(I["L0_attn_out_proj_w"])
    xp = np.asarray(I["L0_attn_x_proj_w"], f32)
    hp["xprojT_dt"] = T_(xp[:DTR])
    hp["xprojT_BC"] = T_(xp[DTR:DTR + 32])
    hp["dtprojT"] = T_(I["L0_attn_dt_proj_w"])
    hp["convw"] = np.ascontiguousarray(np.asarray(I["L0_attn_conv_w"], f32)[:, 0, :])
    for k_src, k_dst in [
        ("L0_attn_conv_b", "conv_b"), ("L0_attn_dt_proj_b", "dt_b"),
        ("L0_attn_Dp", "Dp"), ("L0_attn_gate_b", "gate_b"),
        ("L0_attn_fd_b1", "fd_b1"), ("L0_attn_fd_ln_g", "fd_ln_g"),
        ("L0_attn_fd_ln_b", "fd_ln_b"), ("L0_attn_fd_b2", "fd_b2"),
        ("L0_attn_norm_g", "norm_g"), ("L0_attn_norm_b", "norm_b"),
        ("L1_attn_b_qkv", "b_qkv"), ("L1_attn_b_o", "b_o"),
    ]:
        hp[k_dst] = np.ascontiguousarray(np.asarray(I[k_src], f32))
    hp["A_log"] = np.ascontiguousarray(np.asarray(I["L0_attn_A_log"], f32))
    hp["balance"] = np.asarray(I["L0_attn_balance_factor"], f32).reshape(1)

    rm = np.asarray(I["L0_ffn_rail_maker"], f32)
    ru = np.asarray(I["L0_ffn_rail_up_proj"], f32)
    rm_bd = np.zeros((4, 8, 128, 128), f32)
    for h in range(4):
        for g in range(8):
            for p in range(8):
                r = 8 * g + p
                rm_bd[h, g, 16 * p:16 * p + 16, 16 * p:16 * p + 16] = rm[h, r]
    hp["rm_bd"] = rm_bd
    ru_bd = np.zeros((4, 8, 4, 128, 128), f32)
    for h in range(4):
        for g in range(8):
            for mt in range(4):
                for p2 in range(2):
                    r = 8 * g + 2 * mt + p2
                    blk = ru[h, r].T
                    ru_bd[h, g, mt, 16 * (2 * mt + p2):16 * (2 * mt + p2) + 16,
                          64 * p2:64 * p2 + 64] = blk
    hp["ru_bd"] = ru_bd
    hp["cprojT"] = T_(I["L0_ffn_w_cproj"])

    hp["qkvT"] = T_(I["L1_attn_W_qkv"])
    hp["WoT"] = T_(I["L1_attn_W_o"])
    hp["wg_nat"] = np.ascontiguousarray(np.asarray(I["L1_ffn_w_gate"], f32))
    hp["wu_nat"] = np.ascontiguousarray(np.asarray(I["L1_ffn_w_up"], f32))
    hp["wd_nat"] = np.ascontiguousarray(np.asarray(I["L1_ffn_w_down"], f32))
    hp["wgT"] = T_(I["L1_ffn_w_gate"])
    hp["wuT"] = T_(I["L1_ffn_w_up"])
    hp["wdT"] = T_(I["L1_ffn_w_down"])

    hp["ident"] = np.eye(128, dtype=f32)
    sel = np.zeros((32, 32 * 128), f32)
    for r in range(32):
        sel[r, r * 128:(r + 1) * 128] = 1.0
    hp["sel32"] = sel
    e8 = np.zeros((8, 128), f32)
    for p in range(128):
        e8[p // 16, p] = 1.0
    hp["E8"] = e8
    esa = np.zeros((128, 16 * 128), f32)
    for j in range(16):
        for p in range(128):
            k = 8 * j + p // 16
            if k < 128:
                esa[k, j * 128 + p] = 1.0
    hp["EselA"] = esa
    esb = np.zeros((48, 6 * 128), f32)
    for j in range(6):
        for p in range(128):
            esb[8 * j + p // 16, j * 128 + p] = 1.0
    hp["EselB"] = esb
    msk = np.tril(np.ones((128, 128), f32)).T
    hp["mask128"] = np.ascontiguousarray(msk)
    return hp



class Ctx:
    def __init__(self, nc, P, es, fused, want_in, want_out):
        self.nc = nc
        self.P = P
        self.es = es
        self.fused = fused
        self.want_in = want_in
        self.want_out = want_out
        self.dram = {}
        self.kinds = {}
        self.uniq = 0

    def _mk(self, name, shape, dt, kind):
        if name in self.dram:
            return self.dram[name]
        ap = self.nc.dram_tensor(name, list(shape), dt, kind=kind).ap()
        self.dram[name] = ap
        self.kinds[name] = kind
        return ap

    def din(self, name, shape, dt):
        return self._mk(name, shape, dt, "ExternalInput")

    def dscratch(self, name, shape, dt):
        if self.fused:
            return self._mk(name, shape, dt, "Internal")
        kind = "ExternalOutput" if name in self.want_out else "ExternalInput"
        return self._mk(name, shape, dt, kind)

    def dout(self, name, shape, dt):
        return self._mk(name, shape, dt, "ExternalOutput")

    from contextlib import contextmanager as _cm

    @_cm
    def sub_scope(self):
        old_es = self.es
        self.es = ExitStack()
        try:
            yield
        finally:
            self.P.barrier()
            self.es.close()
            self.es = old_es

    def sb(self, name, shape, dt=F32):
        self.uniq += 1
        return self.es.enter_context(self.nc.sbuf_tensor(f"{name}_{self.uniq}", list(shape), dt))

    def psum(self, name, shape, dt=F32):
        self.uniq += 1
        return self.es.enter_context(self.nc.psum_tensor(f"{name}_{self.uniq}", list(shape), dt))

    def load(self, dst, src, key, queue="Q", deps=(), name="load"):
        return self.P.dma(queue, lambda e: e.dma_start(out=dst, in_=src),
                          reads=(), writes=[key], deps=deps, name=name)

    def store(self, src, dst, rkeys, queue="Q", deps=(), name="store"):
        return self.P.dma(queue, lambda e: e.dma_start(out=dst, in_=src),
                          reads=rkeys, writes=(), deps=deps, name=name)

    def mm(self, psum_ap, pairs, rkeys, wkey, deps=(), name="mm"):
        n = len(pairs)

        def fn(eng):
            last = None
            for i, (l, r) in enumerate(pairs):
                last = eng.matmul(psum_ap, l, r, start=(i == 0), stop=(i == n - 1))
            return last
        return self.P.step("T", fn, reads=rkeys, writes=[wkey], deps=deps, name=name)

    def vstep(self, fn, reads=(), writes=(), deps=(), name="v"):
        return self.P.step("V", fn, reads=reads, writes=writes, deps=deps, name=name)

    def sstep(self, fn, reads=(), writes=(), deps=(), name="s"):
        return self.P.step("S", fn, reads=list(reads) + ["consts"], writes=writes,
                           deps=deps, name=name)

    def tstep(self, fn, reads=(), writes=(), deps=(), name="t"):
        return self.P.step("T", fn, reads=reads, writes=writes, deps=deps, name=name)

    def gstep(self, fn, reads=(), writes=(), deps=(), name="g"):
        return self.P.step("G", fn, reads=reads, writes=writes, deps=deps, name=name)


def ceil_div(a, b):
    return (a + b - 1) // b


def emit_qdq(C, src, dst, invb, sclb, r, a, L, sgn, p, F, tag, extra_reads=(),
             skey=None, ikey=None, ckey=None, eng="V", tt_eng=None):
    v = C.vstep if eng == "V" else C.gstep
    vt = C.gstep if tt_eng == "G" else v
    s = C.sstep
    nb = F // 16
    skey = skey or (tag + "S")
    ikey = ikey or (tag + "I")
    ckey = ckey or (tag + "C")
    rk = [skey] + list(extra_reads)
    src3 = src[0:p, 0:F].rearrange("q (b k) -> q b k", k=16)
    v(lambda e: e.tensor_tensor(r[0:p, 0:F].rearrange("q (b k) -> q b k", k=16), src3, invb, ALU.mult),
      reads=rk + [ikey], writes=[tag + "r"], name=tag + "r")
    s(lambda e: e.sign(sgn[0:p, 0:F], src[0:p, 0:F]), reads=rk, writes=[tag + "g"], name=tag + "g")
    s(lambda e: e.activation(a[0:p, 0:F], r[0:p, 0:F], AF.Abs), reads=[tag + "r"], writes=[tag + "a"], name=tag + "a")
    v(lambda e: e.tensor_single_scalar(L[0:p, 0:F], a[0:p, 0:F], 0.25, ALU.is_gt),
      reads=[tag + "a"], writes=[tag + "L"], name=tag + "L")
    for thr in (0.75, 1.25, 1.75):
        v(lambda e, t=thr: e.scalar_tensor_tensor(L[0:p, 0:F], a[0:p, 0:F], t, L[0:p, 0:F], ALU.is_gt, ALU.add),
          reads=[tag + "a", tag + "L"], writes=[tag + "L"], name=tag + "L")
    v(lambda e: e.tensor_scalar(r[0:p, 0:F], a[0:p, 0:F], 5.0, 2.0, ALU.is_gt, ALU.mult),
      reads=[tag + "a", tag + "r"], writes=[tag + "H"], name=tag + "H")
    for thr in (3.5, 2.5):
        v(lambda e, t=thr: e.scalar_tensor_tensor(r[0:p, 0:F], a[0:p, 0:F], t, r[0:p, 0:F], ALU.is_gt, ALU.add),
          reads=[tag + "a", tag + "H"], writes=[tag + "H"], name=tag + "H")
    v(lambda e: e.scalar_tensor_tensor(L[0:p, 0:F], L[0:p, 0:F], 0.5, r[0:p, 0:F], ALU.mult, ALU.add),
      reads=[tag + "L", tag + "H"], writes=[tag + "q"], name=tag + "q")
    v(lambda e: e.tensor_tensor(a[0:p, 0:F].rearrange("q (b k) -> q b k", k=16),
                                sgn[0:p, 0:F].rearrange("q (b k) -> q b k", k=16), sclb, ALU.mult),
      reads=[tag + "g", ckey], writes=[tag + "ss"], name=tag + "ss")
    v(lambda e: e.tensor_tensor(dst[0:p, 0:F], L[0:p, 0:F], a[0:p, 0:F], ALU.mult),
      reads=[tag + "q", tag + "ss"], writes=[tag + "D"], name=tag + "D")


def emit_scales(C, src, scl, inv, p, F, tag, extra_reads=(), ckey=None, ikey=None, eng="V"):
    v = C.vstep if eng == "V" else C.gstep
    vr = C.vstep
    nb = F // 16
    ckey = ckey or (tag + "C")
    ikey = ikey or (tag + "I")
    rk = list(extra_reads)
    v(lambda e: e.tensor_reduce(scl[0:p, 0:nb], src[0:p, 0:F].rearrange("q (b k) -> q b k", k=16),
                                AX.X, ALU.max, apply_absolute_value=True),
      reads=rk, writes=[ckey], name=tag + "amax")
    v(lambda e: e.tensor_scalar(scl[0:p, 0:nb], scl[0:p, 0:nb], 1.0 / 6.0, 1e-12, ALU.mult, ALU.max),
      reads=[ckey], writes=[ckey], name=tag + "scl")
    vr(lambda e: e.reciprocal(inv[0:p, 0:nb], scl[0:p, 0:nb]),
      reads=[ckey], writes=[ikey], name=tag + "inv")


def bview(t, p, F):
    return t[0:p, 0:F // 16].unsqueeze(-1).broadcast_to([p, F // 16, 16])


def fview(t, p, F):
    return t[0:p, 0:F].rearrange("q (b k) -> q b k", k=16)


CONSTS = [1e-5, 1.0, 0.5, 0.7071067811865476, 30.0, -30.0, 0.125, 6.0]


class Commons:
    def __init__(self, C):
        nc = C.nc
        self.ident = C.sb("ident", (128, 128), F32)
        ident_d = C.din("ident", (128, 128), F32)
        C.load(self.ident[:], ident_d, key="ident")
        self.ones128 = C.sb("ones128", (128, 1), F32)
        self.ones1 = C.sb("ones1", (1, 128), F32)
        C.vstep(lambda e: e.memset(self.ones128[:], 1.0), writes=["ones128"], name="o128")
        C.vstep(lambda e: e.memset(self.ones1[:], 1.0), writes=["ones1"], name="o1")
        self.identb = C.sb("identb", (128, 128), BF16)
        C.vstep(lambda e: e.tensor_copy(self.identb[:], self.ident[:]),
                reads=["ident"], writes=["identb"], name="identb")
        self.cst = C.sb("cst", (128, len(CONSTS)), F32)
        cst_d = C.din("constv", (128, len(CONSTS)), F32)
        C.load(self.cst[:], cst_d, key="consts")
        for i, v in enumerate(CONSTS):
            nc.const_aps.register(F32, float(v), self.cst[:, i:i + 1])


class LnScratch:
    def __init__(self, C, W, tag):
        self.tag = tag
        self.W = W
        self.sq = [C.sb(tag + "sq", (128, W)) for _ in range(2)]
        self.ps1 = C.psum(tag + "p12", (1, W))
        self.psB = C.psum(tag + "pB", (128, W))
        self.rowm = C.sb(tag + "rm", (1, W))
        self.rowr = C.sb(tag + "rr", (1, W))
        self.rowt = C.sb(tag + "rt", (1, W))
        self.meanb = C.sb(tag + "mb", (128, W))
        self.rstdb = C.sb(tag + "rb", (128, W))


def emit_lnT(C, com, xt, keys, W, ln, eps=1e-5):
    n = len(xt)
    Cdim = n * 128
    tag = ln.tag
    sq, ps1, psB = ln.sq, ln.ps1, ln.psB
    rowm, rowr, rowt, meanb, rstdb = ln.rowm, ln.rowr, ln.rowt, ln.meanb, ln.rstdb
    assert W <= ln.W
    for i in range(n):
        C.tstep(lambda e, i=i: e.matmul(ps1[0:1, 0:W], com.ones128[:, 0:1], xt[i],
                                        start=(i == 0), stop=(i == n - 1)),
                reads=[keys[i], "ones128"], writes=[tag + "p1"], name=tag + "s1")
    C.vstep(lambda e: e.tensor_scalar_mul(rowm[0:1, 0:W], ps1[0:1, 0:W], 1.0 / Cdim),
            reads=[tag + "p1"], writes=[tag + "rm"], name=tag + "mean")
    for i in range(n):
        sq_ = sq[i % 2]
        C.sstep(lambda e, i=i, sq_=sq_: e.activation(sq_[:, 0:W], xt[i], AF.Square),
                reads=[keys[i]], writes=[(tag + "sq", i % 2)], name=tag + "sq")
        C.tstep(lambda e, i=i, sq_=sq_: e.matmul(ps1[0:1, 0:W], com.ones128[:, 0:1], sq_[:, 0:W],
                                        start=(i == 0), stop=(i == n - 1)),
                reads=[(tag + "sq", i % 2), "ones128", tag + "rm"], writes=[tag + "p1"], name=tag + "s2")
    C.vstep(lambda e: e.tensor_scalar_mul(rowr[0:1, 0:W], ps1[0:1, 0:W], 1.0 / Cdim),
            reads=[tag + "p1"], writes=[tag + "rr"], name=tag + "msq")
    C.vstep(lambda e: e.tensor_mul(rowt[0:1, 0:W], rowm[0:1, 0:W], rowm[0:1, 0:W]),
            reads=[tag + "rm"], writes=[tag + "rt"], name=tag + "m2")
    C.vstep(lambda e: e.tensor_sub(rowr[0:1, 0:W], rowr[0:1, 0:W], rowt[0:1, 0:W]),
            reads=[tag + "rr", tag + "rt"], writes=[tag + "rr"], name=tag + "var")
    C.sstep(lambda e: e.activation(rowr[0:1, 0:W], rowr[0:1, 0:W], AF.Sqrt, bias=eps),
            reads=[tag + "rr"], writes=[tag + "rr"], name=tag + "sqrt")
    C.vstep(lambda e: e.reciprocal(rowr[0:1, 0:W], rowr[0:1, 0:W]),
            reads=[tag + "rr"], writes=[tag + "rr"], name=tag + "rcp")
    extra_pb = getattr(ln, "pb_alias", [])
    C.tstep(lambda e: e.matmul(psB[:, 0:W], com.ones1[0:1, :], rowm[0:1, 0:W], start=True, stop=True),
            reads=[tag + "rm", "ones1"], writes=[tag + "pB"] + extra_pb, name=tag + "bm")
    C.vstep(lambda e: e.tensor_copy(meanb[:, 0:W], psB[:, 0:W]),
            reads=[tag + "pB"], writes=[tag + "mb"], name=tag + "em")
    C.tstep(lambda e: e.matmul(psB[:, 0:W], com.ones1[0:1, :], rowr[0:1, 0:W], start=True, stop=True),
            reads=[tag + "rr", "ones1"], writes=[tag + "pB"] + extra_pb, name=tag + "br")
    C.vstep(lambda e: e.tensor_copy(rstdb[:, 0:W], psB[:, 0:W]),
            reads=[tag + "pB"], writes=[tag + "rb"], name=tag + "er")
    return meanb, rstdb



def ph_p0(C, com):
    xT = C.din("xT", (D, T), F32)
    xnbf = C.dscratch("xnT_bf", (D, T), BF16)
    W = 512
    xt = [C.sb(f"p0x{i}", (128, W)) for i in range(8)]
    tmp = C.sb("p0t", (128, W))
    xnb = C.sb("p0nb", (128, W), BF16)
    ln = LnScratch(C, W, "p0ln")
    for c in range(T // W):
        keys = []
        for i in range(8):
            k = ("p0x", i)
            C.load(xt[i][:], xT[128 * i:128 * (i + 1), c * W:(c + 1) * W], key=k, name=f"p0ld{i}")
            keys.append(k)
        meanb, rstdb = emit_lnT(C, com, [t[:, 0:W] for t in xt], keys, W, ln)
        for i in range(8):
            C.vstep(lambda e, i=i: e.tensor_sub(tmp[:, 0:W], xt[i][:, 0:W], meanb[:, 0:W]),
                    reads=[("p0x", i), "p0ln" + "mb"], writes=["p0t"], name="p0sub")
            C.vstep(lambda e: e.tensor_mul(xnb[:, 0:W], tmp[:, 0:W], rstdb[:, 0:W]),
                    reads=["p0t", "p0ln" + "rb"], writes=["p0nb"], name="p0mul")
            C.store(xnb[:, 0:W], xnbf[128 * i:128 * (i + 1), c * W:(c + 1) * W],
                    rkeys=["p0nb"], name="p0st")





PHASE_IO = {
    "pw": ([], ["inprojT_bf", "outprojT_bf", "cprojT_bf", "wgqT_bf", "wuqT_bf", "wdqT_bf"]),
    "p0": (["xT"], ["xnT_bf"]),
    "p1": (["xnT_bf"], ["gT", "dT"]),
    "p2": (["xT", "xnT_bf", "gT", "dT", "inprojT_bf", "outprojT_bf"], ["accT"]),
    "p3": (["xnT_bf", "accT", "cprojT_bf"], ["x1T"]),
    "p4": (["x1T"], ["xn1T_bf", "xqT_bf"]),
    "p5": (["xn1T_bf", "x1T"], ["x2aT"]),
    "p6": (["xqT_bf", "x2aT", "wgqT_bf", "wuqT_bf", "wdqT_bf"], ["outT"]),
}



NPDT = {"float32": np.float32, "bfloat16": ml_dtypes.bfloat16}


def build_program(phases, fused):
    nc = bass.Bass("TRN2", target_bir_lowering=False)
    es = ExitStack()
    P = Prog()
    want_in, want_out = set(), set()
    for ph in phases:
        i, o = PHASE_IO[ph]
        want_in |= set(i)
        want_out |= set(o)
    C = Ctx(nc, P, es, fused, want_in, want_out)
    groups = []
    i = 0
    phl = list(phases)
    while i < len(phl):
        if phl[i] == "pw" and i + 1 < len(phl) and phl[i + 1] == "p0":
            groups.append(("pwc",))
            groups.append(("pwq", "p0"))
            i += 2
        else:
            groups.append((phl[i],))
            i += 1
    with es:
        com = Commons(C)
        first = True
        for grp in groups:
            if not first:
                P.barrier()
            first = False
            if len(phl) > 1:
                with C.sub_scope():
                    if len(grp) == 2:
                        s0 = len(P.steps)
                        globals()["ph_" + grp[0]](C, com)
                        s1 = len(P.steps)
                        globals()["ph_" + grp[1]](C, com)
                        s2 = len(P.steps)
                        a, b = P.steps[s0:s1], P.steps[s1:s2]
                        ratio = max(1, len(a) // max(1, len(b)))
                        merged, ai, bi = [], 0, 0
                        while ai < len(a) or bi < len(b):
                            merged.extend(a[ai:ai + ratio])
                            ai += ratio
                            if bi < len(b):
                                merged.append(b[bi])
                                bi += 1
                        P.steps[s0:s2] = merged
                    else:
                        globals()["ph_" + grp[0]](C, com)
            else:
                globals()["ph_" + grp[0]](C, com)
        out_dmas = [s for s in P.steps if s.engine in ("Q", "QG")]
        P.step("QSY", None, deps=out_dmas, name="final_wait")
        P.build_block(nc, es)
    return nc, C


def run_phase_group(phases, env_per_core, fused=False):
    nc, C = build_program(phases, fused)
    in_names = [n for n, k in C.kinds.items() if k == "ExternalInput"]
    out_names = [n for n, k in C.kinds.items() if k == "ExternalOutput"]
    in_maps = [{n: env[n] for n in in_names} for env in env_per_core]
    res = run_bass_kernel_spmd(nc, in_maps, core_ids=list(range(len(env_per_core))))
    for env, r in zip(env_per_core, res.results):
        for n in out_names:
            env[n] = np.asarray(r[n])
    return env_per_core


def load_wbf(C, tag, dram, R, Ccols, queue="Q"):
    nk = R // 128
    f32buf = [C.sb(tag + "f", (128, Ccols)) for _ in range(2)]
    tiles = []
    keys = []
    for k in range(nk):
        fb = f32buf[k % 2]
        fkey = (tag, "f", k % 2)
        C.load(fb[:], dram[128 * k:128 * (k + 1), 0:Ccols], key=fkey, queue=queue, name=tag + "ld")
        bt = C.sb(tag + "b", (128, Ccols), BF16)
        C.vstep(lambda e, fb=fb, bt=bt: e.tensor_copy(bt[:], fb[:]),
                reads=[fkey], writes=[(tag, "b", k)], name=tag + "cast")
        tiles.append(bt)
        keys.append((tag, "b", k))
    return tiles, keys


def load_bias_cols(C, tag, dram, n):
    nt = n // 128
    t = C.sb(tag, (128, nt))
    for j in range(nt):
        C.load(t[:, j:j + 1], dram[128 * j:128 * (j + 1)].unsqueeze(-1), key=(tag, j), name=tag + "ld")
    return t, [(tag, j) for j in range(nt)]


def emit_proj(C, tag, lhsT, lkeys, rhs, rkeys, Cout, W, nchunks, psums, evac_fn,
              chunk_list=None, pskeys=None, rhs_local=False):
    nk = len(lhsT)
    nm = Cout // 128
    if pskeys is None:
        pskeys = [(id(p), "ps") for p in psums]
    it = 0
    for m in range(nm):
        for c in (chunk_list if chunk_list is not None else range(nchunks)):
            ps = psums[it % len(psums)]
            pkey = pskeys[it % len(psums)]
            c_r = 0 if rhs_local else c
            pairs = [(lhsT[k][:, 128 * m:128 * (m + 1)], rhs[k][:, W * c_r:W * (c_r + 1)])
                     for k in range(nk)]
            C.mm(ps[:, 0:W], pairs, rkeys=lkeys + rkeys, wkey=pkey, name=tag + "mm")
            evac_fn(m, c, ps, pkey)
            it += 1



def ph_p1(C, com):
    xnbf = C.dscratch("xnT_bf", (D, T), BF16)
    gT = C.dscratch("gT", (D, T), F32)
    dT = C.dscratch("dT", (D, T), F32)
    gate_w = C.din("gateT", (D, D), F32)
    fdw1 = C.din("fdw1T", (D, D), F32)
    fdw2 = C.din("fdw2T", (D, D), F32)
    gate_b = C.din("gate_b", (D,), F32)
    fd_b1 = C.din("fd_b1", (D,), F32)
    fd_ln_g = C.din("fd_ln_g", (D,), F32)
    fd_ln_b = C.din("fd_ln_b", (D,), F32)
    fd_b2 = C.din("fd_b2", (D,), F32)

    xn = []
    xkeys = []
    for k in range(8):
        t = C.sb("p1xn", (128, T), BF16)
        C.load(t[:], xnbf[128 * k:128 * (k + 1), :], key=("p1xn", k), name="p1xnld")
        xn.append(t)
        xkeys.append(("p1xn", k))
    wg, wgk = load_wbf(C, "p1wg", gate_w, D, D)
    w1, w1k = load_wbf(C, "p1w1", fdw1, D, D)
    w2, w2k = load_wbf(C, "p1w2", fdw2, D, D)
    gb, _ = load_bias_cols(C, "p1gb", gate_b, D)
    b1, _ = load_bias_cols(C, "p1b1", fd_b1, D)
    lng, _ = load_bias_cols(C, "p1lng", fd_ln_g, D)
    lnb, _ = load_bias_cols(C, "p1lnb", fd_ln_b, D)
    b2, _ = load_bias_cols(C, "p1b2", fd_b2, D)

    W = 512
    NCH = T // W
    psums = [C.psum("p1ps", (128, W)) for _ in range(2)]
    gtmp = C.sb("p1gt", (128, W))

    def gate_evac(m, c, ps, pkey):
        C.sstep(lambda e: e.activation(gtmp[:, 0:W], ps[:, 0:W], AF.Sigmoid, bias=gb[:, m:m + 1]),
                reads=[pkey, ("p1gb", m)], writes=["p1gt"], name="p1gs")
        C.store(gtmp[:, 0:W], gT[128 * m:128 * (m + 1), W * c:W * (c + 1)],
                rkeys=["p1gt"], name="p1gst")
    emit_proj(C, "p1g", wg, wgk, xn, xkeys, D, W, NCH, psums, gate_evac)

    d1 = [C.sb("p1d1", (128, W)) for _ in range(8)]
    dlb = [C.sb("p1dl", (128, W), BF16) for _ in range(8)]
    ln = LnScratch(C, W, "p1ln")
    dtmp = C.sb("p1dt", (128, W))
    for c in range(NCH):
        def d1_evac(m, c2, ps, pkey, c0=c):
            C.vstep(lambda e, m=m: e.tensor_scalar_add(d1[m][:, 0:W], ps[:, 0:W], b1[:, m:m + 1]),
                    reads=[pkey, ("p1b1", m)], writes=[("p1d1", m)], name="p1d1e")
        emit_proj(C, "p1a", w1, w1k, xn, xkeys, D, W, NCH, psums, d1_evac, chunk_list=[c])
        meanb, rstdb = emit_lnT(C, com, [t[:, 0:W] for t in d1],
                                [("p1d1", m) for m in range(8)], W, ln)
        for m in range(8):
            C.vstep(lambda e, m=m: e.tensor_sub(dtmp[:, 0:W], d1[m][:, 0:W], meanb[:, 0:W]),
                    reads=[("p1d1", m), "p1lnmb"], writes=["p1dt"], name="p1sub")
            C.vstep(lambda e: e.tensor_mul(dtmp[:, 0:W], dtmp[:, 0:W], rstdb[:, 0:W]),
                    reads=["p1dt", "p1lnrb"], writes=["p1dt"], name="p1mul")
            C.vstep(lambda e, m=m: e.tensor_scalar(dtmp[:, 0:W], dtmp[:, 0:W],
                                                   lng[:, m:m + 1], lnb[:, m:m + 1], ALU.mult, ALU.add),
                    reads=["p1dt", ("p1lng", m), ("p1lnb", m)], writes=["p1dt"], name="p1aff")
            C.vstep(lambda e, m=m: e.tensor_scalar_max(dlb[m][:, 0:W], dtmp[:, 0:W], 0.0),
                    reads=["p1dt"], writes=[("p1dl", m)], name="p1relu")
        def d2_evac(m, c2, ps, pkey, c0=c):
            C.vstep(lambda e, m=m: e.tensor_scalar_add(dtmp[:, 0:W], ps[:, 0:W], b2[:, m:m + 1]),
                    reads=[pkey, ("p1b2", m)], writes=["p1dt"], name="p1d2e")
            C.store(dtmp[:, 0:W], dT[128 * m:128 * (m + 1), W * c0:W * (c0 + 1)],
                    rkeys=["p1dt"], name="p1dst")
        emit_proj(C, "p1b", w2, w2k, dlb, [("p1dl", m) for m in range(8)], D, W, NCH,
                  psums, d2_evac, chunk_list=[c], rhs_local=True)



def quant_weight(C, com, wn, R, Cc, qs):
    nat = C.din(wn + "_nat", (R, Cc), F32)
    wT = C.din(wn + "T", (Cc, R), F32)
    out = C.dscratch(wn + "qT_bf", (Cc, R), BF16)
    nb = Cc // 16
    ntn = R // 128
    ntT = Cc // 128
    (natf, sclbuf, invbuf, sclTA, sclTB, invTA, invTB, scl8, inv8,
     sclrep, invrep, wtile, qt, qr, qa, qL, qsg, pstr, psE) = qs
    e8 = qs_e8(C)
    for r in range(ntn):
        C.load(natf[:, 0:Cc], nat[128 * r:128 * (r + 1), :], key=("qwSS", 0), name="qwldn")
        emit_scales(C, natf[:, 0:Cc], sclbuf[:, r * nb:(r + 1) * nb],
                    invbuf[:, r * nb:(r + 1) * nb], 128, Cc, f"qs{wn}{r}",
                    extra_reads=[("qwSS", 0)], ckey=("qw_scl", r), ikey=("qw_inv", r))
    for r in range(ntn):
        for half, (b0, bw) in enumerate([(0, min(nb, 128))] + ([(128, nb - 128)] if nb > 128 else [])):
            dstS = sclTA if half == 0 else sclTB
            dstI = invTA if half == 0 else invTB
            for buf, dst, kk in ((sclbuf, dstS, "qw_scl"), (invbuf, dstI, "qw_inv")):
                C.tstep(lambda e, r=r, b0=b0, bw=bw, buf=buf: e.transpose(
                            pstr[0:bw, 0:128], buf[:, r * nb + b0:r * nb + b0 + bw], com.ident[:]),
                        reads=[(kk, r), "ident"], writes=["qw_pstr"], name="qwtr")
                C.vstep(lambda e, r=r, bw=bw, dst=dst: e.tensor_copy(
                            dst[0:bw, 128 * r:128 * (r + 1)], pstr[0:bw, 0:128]),
                        reads=["qw_pstr"], writes=[("qw_sclT", kk, half)], name="qwtre")
    esel_a, esel_b = qs_esel(C)
    for j in range(ntT):
        jp = j % 2
        if 8 * (j + 1) <= 128:
            esl = esel_a[0:(64 if ntn == 22 else 128), 128 * j:128 * (j + 1)]
            srcS, srcI = sclTA, invTA
            rows = 64 if ntn == 22 else 128
            skey2, ikey2 = ("qw_sclT", "qw_scl", 0), ("qw_sclT", "qw_inv", 0)
        else:
            esl = esel_b[0:48, 128 * (j - 16):128 * (j - 15)]
            srcS, srcI = sclTB, invTB
            rows = 48
            skey2, ikey2 = ("qw_sclT", "qw_scl", 1), ("qw_sclT", "qw_inv", 1)
        nchunk = ceil_div(R, 512)
        psE_ = psE[jp]
        for cc in range(nchunk):
            w = min(512, R - cc * 512)
            C.tstep(lambda e, cc=cc, w=w, jp=jp, psE_=psE_, esl=esl, srcS=srcS, rows=rows: e.matmul(
                        psE_[:, 0:w], esl, srcS[0:rows, cc * 512:cc * 512 + w],
                        start=True, stop=True),
                    reads=[skey2, "Esel"], writes=[("qw_psE", jp)], name="qwrep")
            C.vstep(lambda e, cc=cc, w=w, jp=jp, psE_=psE_: e.tensor_copy(
                        sclrep[jp][:, cc * 512:cc * 512 + w], psE_[:, 0:w]),
                    reads=[("qw_psE", jp)], writes=[("qw_srep", jp)], name="qwrepe")
            C.tstep(lambda e, cc=cc, w=w, jp=jp, psE_=psE_, esl=esl, srcI=srcI, rows=rows: e.matmul(
                        psE_[:, 0:w], esl, srcI[0:rows, cc * 512:cc * 512 + w],
                        start=True, stop=True),
                    reads=[ikey2, "Esel"], writes=[("qw_psE", jp)], name="qwrep")
            C.vstep(lambda e, cc=cc, w=w, jp=jp, psE_=psE_: e.tensor_copy(
                        invrep[jp][:, cc * 512:cc * 512 + w], psE_[:, 0:w]),
                    reads=[("qw_psE", jp)], writes=[("qw_irep", jp)], name="qwrepe")
        jp = j % 2
        wt_, qt_ = wtile[jp], qt[jp]
        C.load(wt_[:, 0:R], wT[128 * j:128 * (j + 1), :], key=("qwSS", jp), name="qwldT")
        HB = R // 2
        C.P.step("V", None, reads=["qw_qtb"], name="qtgate") if False else None
        C.P.step("V", lambda e: e.engine_nop(), reads=[], writes=["qw_qtb"], name="qtgate")
        for hf in range(2):
            emit_qdq(C, wt_[:, hf * HB:hf * HB + HB], qt_[:, hf * HB:hf * HB + HB],
                     fview(invrep[jp][:, hf * HB:hf * HB + HB], 128, HB),
                     fview(sclrep[jp][:, hf * HB:hf * HB + HB], 128, HB),
                     qr[jp], qa[jp], qL[jp], qsg[jp], 128, HB, f"qw_q{jp}",
                     skey=("qwSS", jp), ikey=("qw_irep", jp), ckey=("qw_srep", jp))
        C.store(qt_[:, 0:R], out[128 * j:128 * (j + 1), :], rkeys=[f"qw_q{jp}D", "qw_qtb"], name="qwst")


def qs_esel(C):
    if not hasattr(C, "_esel"):
        ta = C.sb("EselAsb", (128, 16 * 128))
        C.load(ta[:], C.din("EselA", (128, 16 * 128), F32), key="Esel", name="eselld")
        tb = C.sb("EselBsb", (48, 6 * 128))
        C.load(tb[:], C.din("EselB", (48, 6 * 128), F32), key="Esel", name="eselld")
        C._esel = (ta, tb)
    return C._esel


def qs_e8(C):
    if not hasattr(C, "_e8"):
        t = C.sb("E8sb", (8, 128))
        C.load(t[:], C.din("E8", (8, 128), F32), key="E8", name="e8ld")
        C._e8 = t
    return C._e8


def ph_pw(C, com):
    ph_pwc(C, com)
    ph_pwq(C, com)


def ph_pwc(C, com):
    with C.sub_scope():
        castf = [C.sb("pwcf", (128, 4096)) for _ in range(4)]
        castb = [C.sb("pwcb", (128, 4096), BF16) for _ in range(4)]
        for name, R, Cc in [("inprojT", 1024, 4096), ("outprojT", 2048, 1024),
                            ("cprojT", 16384, 1024)]:
            src = C.din(name, (R, Cc), F32)
            dst = C.dscratch(name + "_bf", (R, Cc), BF16)
            for k in range(R // 128):
                i = k % 4
                C.load(castf[i][:, 0:Cc], src[128 * k:128 * (k + 1), :], key=("pwf", i), name="pwld")
                est = C.vstep if k % 2 == 0 else C.sstep
                fn = (lambda e, i=i, Cc=Cc: e.tensor_copy(castb[i][:, 0:Cc], castf[i][:, 0:Cc]))                     if k % 2 == 0 else                     (lambda e, i=i, Cc=Cc: e.activation(castb[i][:, 0:Cc], castf[i][:, 0:Cc], AF.Copy))
                est(fn, reads=[("pwf", i)], writes=[("pwb", i)], name="pwc")
                C.store(castb[i][:, 0:Cc], dst[128 * k:128 * (k + 1), :],
                        rkeys=[("pwb", i)], queue="QG", name="pwst")


def ph_pwq(C, com):
    _qwt = [C.sb("q_wt", (128, 2816)) for _ in range(2)]
    qs = (_qwt[0], C.sb("q_scl", (128, 1408)), C.sb("q_inv", (128, 1408)),
          C.sb("q_sTA", (128, 2816)), C.sb("q_sTB", (48, 1024)),
          C.sb("q_iTA", (128, 2816)), C.sb("q_iTB", (48, 1024)),
          None, None,
          [C.sb("q_srep", (128, 2816)) for _ in range(2)], [C.sb("q_irep", (128, 2816)) for _ in range(2)],
          _qwt, [C.sb("q_qt", (128, 2816), BF16)] * 2,
          [C.sb("q_r", (128, 1408)) for _ in range(2)], [C.sb("q_a", (128, 1408)) for _ in range(2)],
          [C.sb("q_L", (128, 1408)) for _ in range(2)], [C.sb("q_sg", (128, 1408)) for _ in range(2)],
          C.psum("q_pstr", (128, 128)), [C.psum("q_psE0", (128, 512)), C.psum("q_psE1", (128, 512))])
    quant_weight(C, com, "wg", 2816, 1024, qs)
    C.P.barrier()
    quant_weight(C, com, "wu", 2816, 1024, qs)
    C.P.barrier()
    quant_weight(C, com, "wd", 1024, 2816, qs)



def ph_p2(C, com):
    xT = C.din("xT", (D, T), F32)
    xnbf = C.dscratch("xnT_bf", (D, T), BF16)
    gT = C.dscratch("gT", (D, T), F32)
    dT = C.dscratch("dT", (D, T), F32)
    accT = C.dscratch("accT", (D, T), F32)
    ipj_d = C.dscratch("inprojT_bf", (D, 2 * DI), BF16)
    opj_d = C.dscratch("outprojT_bf", (DI, D), BF16)
    xpd_d = C.din("xprojT_dt", (DI, DTR), F32)
    xpbc_d = C.din("xprojT_BC", (DI, 32), F32)
    dtp_d = C.din("dtprojT", (DTR, DI), F32)
    cw_d = C.din("convw", (DI, 4), F32)
    sel_d = C.din("sel32", (32, 32 * 128), F32)
    A_d = C.din("A_log", (DI, DS), F32)
    bal_d = C.din("balance", (1,), F32)

    import os
    Lc = 256
    NCH = T // Lc
    if os.environ.get('P2_NCH'):
        NCH = int(os.environ['P2_NCH'])
    STOP = int(os.environ.get('P2_STOP', '10'))
    NDT = DI // 128

    xpd, xpdk = load_wbf(C, "p2xpd", xpd_d, DI, DTR)
    xpbc, xpbck = load_wbf(C, "p2xpbc", xpbc_d, DI, 32)
    dtp = C.sb("p2dtp", (DTR, DI), BF16)
    with C.sub_scope():
        dtpf = C.sb("p2dtpf", (DTR, DI))
        C.load(dtpf[:], dtp_d, key="p2dtpf", name="p2ldtp")
        C.vstep(lambda e: e.tensor_copy(dtp[:], dtpf[:]), reads=["p2dtpf"], writes=["p2dtp"], name="p2cdtp")
    cw = C.sb("p2cw", (128, 4 * NDT))
    for dt in range(NDT):
        C.load(cw[:, 4 * dt:4 * dt + 4], cw_d[128 * dt:128 * (dt + 1), :], key=("p2cw", dt), name="p2lcw")
    convb, _ = load_bias_cols(C, "p2cb", C.din("conv_b", (DI,), F32), DI)
    dtb, _ = load_bias_cols(C, "p2dtb", C.din("dt_b", (DI,), F32), DI)
    Dpc, _ = load_bias_cols(C, "p2Dp", C.din("Dp", (DI,), F32), DI)
    ng, _ = load_bias_cols(C, "p2ng", C.din("norm_g", (D,), F32), D)
    nb_, _ = load_bias_cols(C, "p2nb", C.din("norm_b", (D,), F32), D)
    Ab = C.sb("p2A", (128, DS * NDT))
    for dt in range(NDT):
        C.load(Ab[:, DS * dt:DS * (dt + 1)], A_d[128 * dt:128 * (dt + 1), :], key=("p2A", dt), name="p2lA")
    C.sstep(lambda e: e.activation(Ab[:, :], Ab[:, :], AF.Exp),
            reads=[("p2A", dt) for dt in range(NDT)], writes=["p2Ae"], name="p2expA")
    C.vstep(lambda e: e.tensor_scalar_mul(Ab[:, :], Ab[:, :], -1.0),
            reads=["p2Ae"], writes=["p2Ae"], name="p2negA")
    ssel = C.sb("p2sel", (32, 32 * 128), BF16)
    with C.sub_scope():
        sself = C.sb("p2self", (32, 32 * 128))
        C.load(sself[:], sel_d, key="p2self", name="p2lsel")
        C.vstep(lambda e: e.tensor_copy(ssel[:], sself[:]), reads=["p2self"], writes=["p2sel"], name="p2csel")
    balv = C.sb("p2bal", (1, 1))
    C.load(balv[:], bal_d.unsqueeze(0), key="p2bal", name="p2lbal")
    C.sstep(lambda e: e.activation(balv[:], balv[:], AF.Sigmoid), reads=["p2bal"], writes=["p2bal2"], name="p2sbal")
    fcol = C.sb("p2f", (128, 1))
    fmcol = C.sb("p2fm", (128, 1))

    ipjbig = C.sb("p2ipjB", (128, 8 * 1024), BF16)
    opj = [C.sb("p2opj", (128, 2 * D), BF16) for _ in range(2)]
    xnbig = C.sb("p2xnc", (128, 8 * Lc), BF16)
    xnc = [xnbig[:, Lc * k:Lc * (k + 1)] for k in range(8)]
    xi = [C.sb("p2xi", (128, Lc + 3)) for _ in range(NDT)]
    xc = [C.sb("p2xc", (128, Lc), BF16) for _ in range(NDT)]
    svz = [C.sb("p2svz", (128, Lc), BF16) for _ in range(NDT)]
    delta = [C.sb("p2del", (128, Lc)) for _ in range(NDT)]
    y3bf = [C.sb("p2y3", (128, Lc), BF16) for _ in range(NDT)]
    LcR = Lc + 1
    h_all2 = [C.sb("p2hall", (128, DS * LcR), BF16) for _ in range(2)]
    a_all2 = [C.sb("p2aall", (128, DS * LcR), BF16) for _ in range(2)]
    Ball = C.sb("p2Ball", (128, DS * Lc), BF16)
    Call = C.sb("p2Call", (128, DS * Lc), BF16)
    hstate = C.sb("p2hst", (128, DS * NDT))
    du_t = [C.sb("p2du", (128, Lc)) for _ in range(2)]
    dub_t = [C.sb("p2dub", (128, Lc), BF16) for _ in range(2)]

    dtTbf = C.sb("p2dtT", (DTR, Lc), BF16)
    bc_sb = C.sb("p2bc", (32, Lc), BF16)
    raw = [C.sb("p2raw", (128, Lc)) for _ in range(2)]
    dt1 = [C.sb("p2dt1", (128, Lc)) for _ in range(2)]
    dt2 = [C.sb("p2dt2", (128, Lc)) for _ in range(2)]
    tmp1 = C.sb("p2t1", (128, Lc))
    ctmp1 = [C.sb("p2ct1", (128, Lc)) for _ in range(2)]
    ctmp2 = [C.sb("p2ct2", (128, Lc)) for _ in range(2)]

    comb = [C.sb("p2comb", (128, Lc)) for _ in range(8)]
    gmt = [C.sb("p2gm", (128, Lc)) for _ in range(2)]
    dmt = [C.sb("p2dm", (128, Lc)) for _ in range(2)]
    xmt = [C.sb("p2xm", (128, Lc)) for _ in range(2)]
    ln = LnScratch(C, Lc, "p2ln")
    ln.pb_alias = [("p2psO", 3)]
    psA = C.psum("p2psA", (128, Lc))
    psA2 = C.psum("p2psA2", (128, Lc))
    psX = C.psum("p2psX", (128, 2 * Lc))
    psO = [C.psum(f"p2psO{i}", (128, Lc)) for i in range(3)]
    psO.append(ln.psB)
    psums = [psA, psA2]
    pskeys = ["p2psA", "p2psA2"]
    C.tstep(lambda e: e.matmul(psX[:, 0:1], com.ones1[0:1, :], balv[0:1, 0:1], start=True, stop=True),
            reads=["p2bal2", "ones1"], writes=["p2psX"], name="p2bf")
    C.vstep(lambda e: e.tensor_copy(fcol[:], psX[:, 0:1]), reads=["p2psX"], writes=["p2f"], name="p2ef")
    C.vstep(lambda e: e.tensor_scalar(fmcol[:], fcol[:], -1.0, 1.0, ALU.mult, ALU.add),
            reads=["p2f"], writes=["p2fm"], name="p2fmc")

    C.vstep(lambda e: e.memset(hstate[:], 0.0), writes=["p2hst"], name="p2hs0")
    for par in range(2):
        C.vstep(lambda e, par=par: e.memset(a_all2[par][:, 0:DS * LcR:LcR], 0.0),
                writes=[("p2arst", par)], name="p2arst")
    for dt in range(NDT):
        C.vstep(lambda e, dt=dt: e.memset(xi[dt][:, 0:3], 0.0), writes=[("p2xih", dt)], name="p2xih0")

    for c in range(NCH):
        C.load(xnbig[:, :], xnbf.rearrange("(k p) t -> p k t", p=128)[:, :, Lc * c:Lc * (c + 1)],
               key="p2xnB", name="p2lxn")
        it = 0
        for q in range(2):
            C.load(ipjbig[:, :],
                   ipj_d.rearrange("(k p) n -> p k n", p=128)[:, :, q * 1024:(q + 1) * 1024],
                   key="p2ipjB", queue=("Q" if q == 0 else "QG"), name="p2lip")
            for mi8 in range(8):
                mi = q * 8 + mi8
                ps, pk = psums[it % len(psums)], pskeys[it % len(psums)]
                pairs = [(ipjbig[:, 1024 * k + 128 * mi8:1024 * k + 128 * (mi8 + 1)],
                          xnc[k][:, 0:Lc]) for k in range(8)]
                C.mm(ps[:, 0:Lc], pairs, rkeys=["p2ipjB", "p2xnB"], wkey=pk, name="p2ipmm")
                C.sstep(lambda e, mi=mi, ps=ps: e.activation(xi[mi][:, 3:3 + Lc], ps[:, 0:Lc], AF.Copy),
                        reads=[pk], writes=[("p2xi", mi)], name="p2ipe")
                it += 1
        for dt in range(NDT):
            cva = dt1[dt % 2]
            C.vstep(lambda e, dt=dt, cva=cva: e.tensor_scalar_mul(cva[:, 0:Lc], xi[dt][:, 0:Lc],
                                                                  cw[:, 4 * dt:4 * dt + 1]),
                    reads=[("p2xi", dt), ("p2xih", dt), ("p2cw", dt)], writes=[("p2dt1", dt % 2)], name="p2cv0")
            for j in (1, 2, 3):
                C.vstep(lambda e, dt=dt, j=j, cva=cva: e.scalar_tensor_tensor(
                            cva[:, 0:Lc], xi[dt][:, j:j + Lc], cw[:, 4 * dt + j:4 * dt + j + 1],
                            cva[:, 0:Lc], ALU.mult, ALU.add),
                        reads=[("p2xi", dt), ("p2dt1", dt % 2), ("p2cw", dt)], writes=[("p2dt1", dt % 2)], name="p2cvj")
            C.sstep(lambda e, dt=dt, cva=cva: e.activation(xc[dt][:, 0:Lc], cva[:, 0:Lc], AF.Silu,
                                                  bias=convb[:, dt:dt + 1]),
                    reads=[("p2dt1", dt % 2), ("p2cb", dt)], writes=[("p2xc", dt)], name="p2silu")
            C.vstep(lambda e, dt=dt: e.tensor_copy(xi[dt][:, 0:3], xi[dt][:, Lc:Lc + 3]),
                    reads=[("p2xi", dt)], writes=[("p2xih", dt)], name="p2hist")
        if STOP < 3:
            C.store(xc[0][:, 0:Lc], accT[0:128, Lc * c:Lc * (c + 1)], rkeys=[("p2xc", 0)], name="p2st")
            continue
        for q in range(2):
            C.load(ipjbig[:, :],
                   ipj_d.rearrange("(k p) n -> p k n", p=128)[:, :, DI + q * 1024:DI + (q + 1) * 1024],
                   key="p2ipjB", queue=("Q" if q == 0 else "QG"), name="p2lipz")
            for mi8 in range(8):
                mi = q * 8 + mi8
                ps, pk = psums[it % len(psums)], pskeys[it % len(psums)]
                pairs = [(ipjbig[:, 1024 * k + 128 * mi8:1024 * k + 128 * (mi8 + 1)],
                          xnc[k][:, 0:Lc]) for k in range(8)]
                C.mm(ps[:, 0:Lc], pairs, rkeys=["p2ipjB", "p2xnB"], wkey=pk, name="p2zmm")
                C.sstep(lambda e, mi=mi, ps=ps: e.activation(svz[mi][:, 0:Lc], ps[:, 0:Lc], AF.Silu),
                        reads=[pk], writes=[("p2svz", mi)], name="p2ze")
                it += 1
        if STOP < 4:
            C.store(xc[0][:, 0:Lc], accT[0:128, Lc * c:Lc * (c + 1)], rkeys=[("p2xc", 0)], name="p2st")
            continue
        xck = [("p2xc", k) for k in range(NDT)]
        C.mm(psX[0:DTR, 0:Lc], [(xpd[k][:, 0:DTR], xc[k][:, 0:Lc]) for k in range(NDT)],
             rkeys=xck + xpdk, wkey="p2psX", name="p2xdmm")
        C.vstep(lambda e: e.tensor_copy(dtTbf[:, 0:Lc], psX[0:DTR, 0:Lc]),
                reads=["p2psX"], writes=["p2dtT"], name="p2xde")
        C.mm(psX[0:32, Lc:2 * Lc], [(xpbc[k][:, 0:32], xc[k][:, 0:Lc]) for k in range(NDT)],
             rkeys=xck + xpbck, wkey="p2psX", name="p2bcmm")
        C.vstep(lambda e: e.tensor_copy(bc_sb[:, 0:Lc], psX[0:32, Lc:2 * Lc]),
                reads=["p2psX"], writes=["p2bc"], name="p2bce")
        if STOP < 5:
            C.store(xc[0][:, 0:Lc], accT[0:128, Lc * c:Lc * (c + 1)], rkeys=[("p2xc", 0)], name="p2st")
            continue
        def d_sum(dm):
            par = dm % 2
            t1_, t2_ = dt1[par], dt2[par]
            C.vstep(lambda e, dm=dm, t1_=t1_, t2_=t2_: e.tensor_add(delta[dm][:, 0:Lc], t1_[:, 0:Lc], t2_[:, 0:Lc]),
                    reads=[("p2dt1", par), ("p2dt2", par)], writes=[("p2del", dm)], name="p2dsum")
        for dm in range(NDT):
            par = dm % 2
            ps_d = psums[par]
            pk_d = pskeys[par]
            rw, t1_, t2_ = raw[par], dt1[par], dt2[par]
            C.tstep(lambda e, dm=dm, ps_d=ps_d: e.matmul(ps_d[:, 0:Lc], dtp[:, 128 * dm:128 * (dm + 1)],
                                              dtTbf[:, 0:Lc], start=True, stop=True),
                    reads=["p2dtT", "p2dtp"], writes=[pk_d], name="p2dmm")
            C.vstep(lambda e, dm=dm, rw=rw, ps_d=ps_d: e.tensor_scalar_add(rw[:, 0:Lc], ps_d[:, 0:Lc], dtb[:, dm:dm + 1]),
                    reads=[pk_d, ("p2dtb", dm)], writes=[("p2raw", par)], name="p2draw")
            C.vstep(lambda e, rw=rw, t1_=t1_: e.tensor_scalar_min(t1_[:, 0:Lc], rw[:, 0:Lc], 30.0),
                    reads=[("p2raw", par)], writes=[("p2dt1", par)], name="p2dmin")
            C.vstep(lambda e, rw=rw, t2_=t2_: e.tensor_scalar(t2_[:, 0:Lc], rw[:, 0:Lc], -30.0, 0.0, ALU.add, ALU.max),
                    reads=[("p2raw", par)], writes=[("p2dt2", par)], name="p2dtail")
            C.sstep(lambda e, t1_=t1_: e.activation(t1_[:, 0:Lc], t1_[:, 0:Lc], AF.Exp),
                    reads=[("p2dt1", par)], writes=[("p2dt1", par)], name="p2dexp")
            C.sstep(lambda e, t1_=t1_: e.activation(t1_[:, 0:Lc], t1_[:, 0:Lc], AF.Ln, bias=1.0),
                    reads=[("p2dt1", par)], writes=[("p2dt1", par)], name="p2dln")
            if dm >= 1:
                d_sum(dm - 1)
        d_sum(NDT - 1)
        if STOP < 6:
            C.store(delta[0][:, 0:Lc], accT[0:128, Lc * c:Lc * (c + 1)], rkeys=[("p2del", 0)], name="p2st")
            continue
        bci = 0
        for s in range(DS):
            for which, dstb, selc in ((0, Ball, s), (1, Call, 16 + s)):
                ps_b, pk_b = psums[bci % 2], pskeys[bci % 2]
                bci += 1
                C.tstep(lambda e, selc=selc, ps_b=ps_b: e.matmul(
                            ps_b[:, 0:Lc], ssel[:, 128 * selc:128 * (selc + 1)],
                            bc_sb[0:32, 0:Lc], start=True, stop=True),
                        reads=["p2bc", "p2sel"], writes=[pk_b], name="p2Bmm")
                C.sstep(lambda e, s=s, dstb=dstb, ps_b=ps_b: e.activation(
                            dstb[:, Lc * s:Lc * (s + 1)], ps_b[:, 0:Lc], AF.Copy),
                        reads=[pk_b], writes=["p2Ball" if dstb is Ball else "p2Call"], name="p2Be")
        if STOP < 7:
            C.store(delta[0][:, 0:Lc], accT[0:128, Lc * c:Lc * (c + 1)], rkeys=[("p2del", 0)], name="p2st")
            continue
        def scan_pre(dt):
            par = dt % 2
            du_, dub_, h_all = du_t[par], dub_t[par], h_all2[par]
            C.vstep(lambda e, dt=dt, du_=du_: e.tensor_mul(du_[:, 0:Lc], delta[dt][:, 0:Lc], xc[dt][:, 0:Lc]),
                    reads=[("p2del", dt), ("p2xc", dt)], writes=[("p2du", par)], name="p2dut")
            C.vstep(lambda e, du_=du_, dub_=dub_: e.tensor_copy(dub_[:, 0:Lc], du_[:, 0:Lc]),
                    reads=[("p2du", par)], writes=[("p2dub", par)], name="p2dub")
            dub3 = dub_[:, 0:Lc].unsqueeze(1).broadcast_to([128, DS, Lc])
            hsl = h_all[:, 0:DS * LcR].rearrange("p (s u) -> p s u", u=LcR)[:, :, 1:LcR]
            C.vstep(lambda e, dub3=dub3, hsl=hsl: e.tensor_tensor(
                        hsl, dub3,
                        Ball[:, :].rearrange("p (s t) -> p s t", s=DS), ALU.mult),
                    reads=[("p2dub", par), "p2Ball"] + [("p2h", par, s) for s in range(DS)],
                    writes=[("p2h", par, s) for s in range(DS)] + [("p2hm", par)], name="p2bmul")
            C.vstep(lambda e, dt=dt, h_all=h_all: e.tensor_copy(
                        h_all[:, 0:DS * LcR:LcR], hstate[:, DS * dt:DS * (dt + 1)]),
                    reads=["p2hst"], writes=[("p2h", par, "rst")], name="p2seed")

        scan_pre(0)
        for dt in range(NDT):
            par = dt % 2
            du_, dub_, h_all = du_t[par], dub_t[par], h_all2[par]
            t1_ = ctmp1[par]
            if dt + 1 < NDT:
                scan_pre(dt + 1)
            a_all = a_all2[par]
            for s in range(DS):
                C.sstep(lambda e, dt=dt, s=s, a_all=a_all: e.activation(
                            a_all[:, LcR * s + 1:LcR * (s + 1)], delta[dt][:, 0:Lc], AF.Exp,
                            scale=Ab[:, DS * dt + s:DS * dt + s + 1]),
                        reads=[("p2del", dt), "p2Ae", ("p2arst", par)],
                        writes=[("p2a", par)], name="p2aexp")
            C.vstep(lambda e, h_all=h_all, a_all=a_all: e.tensor_tensor_scan(
                        h_all[:, 0:DS * LcR], a_all[:, 0:DS * LcR],
                        h_all[:, 0:DS * LcR], 0.0, ALU.mult, ALU.add),
                    reads=[("p2a", par), ("p2arst", par), ("p2h", par, "rst")] +
                          [("p2h", par, s) for s in range(DS)],
                    writes=[("p2h", par, s) for s in range(DS)] + [("p2hm", par)], name="p2scan")
            C.vstep(lambda e, dt=dt, h_all=h_all: e.tensor_copy(
                        hstate[:, DS * dt:DS * (dt + 1)],
                        h_all[:, LcR - 1:DS * LcR:LcR]),
                    reads=[("p2h", par, s) for s in range(DS)], writes=["p2hst"], name="p2hupd")
            hsl2 = h_all[:, 0:DS * LcR].rearrange("p (s u) -> p s u", u=LcR)[:, :, 1:LcR]
            C.vstep(lambda e, hsl2=hsl2: e.tensor_tensor(hsl2, hsl2,
                        Call[:, :].rearrange("p (s t) -> p s t", s=DS), ALU.mult),
                    reads=[("p2h", par, s) for s in range(DS)] + ["p2Call", "p2hst"],
                    writes=[("p2hm", par)] + [("p2h", par, s) for s in range(DS)], name="p2ymul")
            C.vstep(lambda e, du_=du_, h_all=h_all: e.tensor_reduce(du_[:, 0:Lc],
                        h_all[:, 0:DS * LcR].rearrange("p (s u) -> p u s", u=LcR)[:, 1:LcR, :],
                        AX.X, ALU.add),
                    reads=[("p2hm", par)], writes=[("p2du", par)], name="p2yred")
            C.vstep(lambda e, dt=dt, du_=du_, t1_=t1_: e.scalar_tensor_tensor(
                        t1_[:, 0:Lc], xc[dt][:, 0:Lc], Dpc[:, dt:dt + 1], du_[:, 0:Lc],
                        ALU.mult, ALU.add),
                    reads=[("p2xc", dt), ("p2Dp", dt), ("p2du", par)], writes=[("p2ct1", par)], name="p2yD")
            C.vstep(lambda e, dt=dt, t1_=t1_: e.tensor_mul(y3bf[dt][:, 0:Lc], t1_[:, 0:Lc], svz[dt][:, 0:Lc]),
                    reads=[("p2ct1", par), ("p2svz", dt)], writes=[("p2y3", dt)], name="p2y3")
        if STOP < 8:
            C.vstep(lambda e: e.tensor_copy(tmp1[:, 0:Lc], y3bf[0][:, 0:Lc]),
                    reads=[("p2y3", 0)], writes=["p2t1"], name="p2dbgc")
            C.store(tmp1[:, 0:Lc], accT[0:128, Lc * c:Lc * (c + 1)], rkeys=["p2t1"], name="p2st")
            continue
        y3k = [("p2y3", k) for k in range(NDT)]
        for half in range(2):
            for k4 in range(NDT // 2):
                kk = k4 % 2
                C.load(opj[kk][:, :],
                       opj_d.rearrange("(g p) n -> p g n", p=128)[:, 2 * k4:2 * (k4 + 1), :],
                       key=("p2opj", kk), queue=("Q" if k4 % 2 == 0 else "QG"), name="p2lop")
                for ks in range(2):
                    k = 2 * k4 + ks
                    for mm4 in range(4):
                        m = 4 * half + mm4
                        po = psO[mm4]
                        wk = [("p2psO", mm4)] + (["p2lnpB"] if mm4 == 3 else [])
                        C.tstep(lambda e, m=m, k=k, ks=ks, kk=kk, po=po: e.matmul(
                                    po[:, 0:Lc], opj[kk][:, D * ks + 128 * m:D * ks + 128 * (m + 1)],
                                    y3bf[k][:, 0:Lc], start=(k == 0), stop=(k == NDT - 1),
                                    skip_group_check=True),
                                reads=[("p2opj", kk), y3k[k]], writes=wk, name="p2omm")
            for mm4 in range(4):
                m = 4 * half + mm4
                C.vstep(lambda e, mm4=mm4, m=m: e.tensor_copy(comb[m][:, 0:Lc], psO[mm4][:, 0:Lc]),
                        reads=[("p2psO", mm4)], writes=[("p2mps", m)], name="p2oev")
        if STOP < 9:
            C.store(comb[0][:, 0:Lc], accT[0:128, Lc * c:Lc * (c + 1)], rkeys=[("p2mps", 0)], name="p2st")
            continue
        for m in range(8):
            par = m % 2
            t1_, t2_ = ctmp1[par], ctmp2[par]
            gm_, dm_ = gmt[par], dmt[par]
            C.load(gm_[:], gT[128 * m:128 * (m + 1), Lc * c:Lc * (c + 1)], key=("p2gm", par),
                   queue="QG", name="p2lg")
            C.load(dm_[:], dT[128 * m:128 * (m + 1), Lc * c:Lc * (c + 1)], key=("p2dm", par),
                   queue="QG", name="p2ld")
            C.vstep(lambda e, m=m, t1_=t1_: e.tensor_scalar_mul(t1_[:, 0:Lc], comb[m][:, 0:Lc],
                                                                fcol[:, 0:1]),
                    reads=[("p2mps", m), "p2f"], writes=[("p2ct1", par)], name="p2cf")
            C.vstep(lambda e, t1_=t1_, dm_=dm_: e.scalar_tensor_tensor(t1_[:, 0:Lc], dm_[:, 0:Lc], fmcol[:, 0:1],
                                                                       t1_[:, 0:Lc], ALU.mult, ALU.add),
                    reads=[("p2dm", par), "p2fm", ("p2ct1", par)], writes=[("p2ct1", par)], name="p2cd")
            C.vstep(lambda e, t2_=t2_, gm_=gm_: e.tensor_scalar(t2_[:, 0:Lc], gm_[:, 0:Lc], -1.0, 1.0, ALU.mult, ALU.add),
                    reads=[("p2gm", par)], writes=[("p2ct2", par)], name="p2cg")
            C.vstep(lambda e, t1_=t1_, t2_=t2_: e.tensor_mul(t1_[:, 0:Lc], t1_[:, 0:Lc], t2_[:, 0:Lc]),
                    reads=[("p2ct1", par), ("p2ct2", par)], writes=[("p2ct1", par)], name="p2cm")
            C.vstep(lambda e, m=m, t2_=t2_, gm_=gm_: e.tensor_mul(t2_[:, 0:Lc], xnc[m][:, 0:Lc], gm_[:, 0:Lc]),
                    reads=["p2xnB", ("p2gm", par)], writes=[("p2ct2", par)], name="p2cx")
            C.vstep(lambda e, m=m, t1_=t1_, t2_=t2_: e.tensor_add(comb[m][:, 0:Lc], t1_[:, 0:Lc], t2_[:, 0:Lc]),
                    reads=[("p2ct1", par), ("p2ct2", par)], writes=[("p2comb", m)], name="p2cc")
        if STOP < 10:
            C.store(comb[0][:, 0:Lc], accT[0:128, Lc * c:Lc * (c + 1)], rkeys=[("p2comb", 0)], name="p2st")
            continue
        meanb, rstdb = emit_lnT(C, com, [t[:, 0:Lc] for t in comb],
                                [("p2comb", m) for m in range(8)], Lc, ln)
        for m in range(8):
            par = m % 2
            t1_, t2_, xm_ = ctmp1[par], ctmp2[par], xmt[par]
            C.vstep(lambda e, m=m, t1_=t1_: e.tensor_sub(t1_[:, 0:Lc], comb[m][:, 0:Lc], meanb[:, 0:Lc]),
                    reads=[("p2comb", m), "p2lnmb"], writes=[("p2ct1", par)], name="p2nsub")
            C.vstep(lambda e, t1_=t1_: e.tensor_mul(t1_[:, 0:Lc], t1_[:, 0:Lc], rstdb[:, 0:Lc]),
                    reads=[("p2ct1", par), "p2lnrb"], writes=[("p2ct1", par)], name="p2nmul")
            C.vstep(lambda e, m=m, t1_=t1_: e.tensor_scalar(t1_[:, 0:Lc], t1_[:, 0:Lc],
                                                            ng[:, m:m + 1], nb_[:, m:m + 1], ALU.mult, ALU.add),
                    reads=[("p2ct1", par), ("p2ng", m), ("p2nb", m)], writes=[("p2ct1", par)], name="p2naff")
            C.load(xm_[:], xT[128 * m:128 * (m + 1), Lc * c:Lc * (c + 1)], key=("p2xm", par),
                   queue="QG", name="p2lx")
            C.vstep(lambda e, t1_=t1_, t2_=t2_, xm_=xm_: e.tensor_add(t2_[:, 0:Lc], t1_[:, 0:Lc], xm_[:, 0:Lc]),
                    reads=[("p2ct1", par), ("p2xm", par)], writes=[("p2ct2", par)], name="p2acc")
            C.store(t2_[:, 0:Lc], accT[128 * m:128 * (m + 1), Lc * c:Lc * (c + 1)],
                    rkeys=[("p2ct2", par)], name="p2st")



def ph_p3(C, com):
    xnbf = C.dscratch("xnT_bf", (D, T), BF16)
    accT = C.dscratch("accT", (D, T), F32)
    x1T = C.dscratch("x1T", (D, T), F32)
    cpj_d = C.dscratch("cprojT_bf", (16 * D, D), BF16)
    rm_d = C.din("rm_bd", (4, 8, 128, 128), F32)
    ru_d = C.din("ru_bd", (4, 8, 4, 128, 128), F32)

    import os
    Lc = 256
    NCH = T // Lc
    if os.environ.get('P3_NCH'):
        NCH = int(os.environ['P3_NCH'])

    xn = []
    for k in range(8):
        t = C.sb("p3xn", (128, T), BF16)
        C.load(t[:], xnbf[128 * k:128 * (k + 1), :], key=("p3xn", k), name="p3lxn")
        xn.append(t)
    rm = {}
    ru = {}
    for h in range(4):
        for g in range(8):
            rm[h, g] = C.sb("p3rm", (128, 128), BF16)
            for mt in range(4):
                ru[h, g, mt] = C.sb("p3ru", (128, 128), BF16)
    with C.sub_scope():
        stage = [C.sb("p3stg", (128, 128)) for _ in range(2)]
        si = 0
        for h in range(4):
            for g in range(8):
                st_ = stage[si % 2]
                C.load(st_[:], rm_d[h, g], key=("p3stg", si % 2), name="p3lrm")
                C.vstep(lambda e, t=rm[h, g], st_=st_: e.tensor_copy(t[:], st_[:]),
                        reads=[("p3stg", si % 2)], writes=[("p3rm", h, g)], name="p3crm")
                si += 1
                for mt in range(4):
                    st_ = stage[si % 2]
                    C.load(st_[:], ru_d[h, g, mt], key=("p3stg", si % 2), name="p3lru")
                    C.vstep(lambda e, t2=ru[h, g, mt], st_=st_: e.tensor_copy(t2[:], st_[:]),
                            reads=[("p3stg", si % 2)], writes=[("p3ru", h, g, mt)], name="p3cru")
                    si += 1
    y1h = [C.sb("p3y1", (128, Lc), BF16) for _ in range(8)]
    z1c = [C.sb("p3z1", (128, Lc), BF16) for _ in range(128)]
    wc = [C.sb("p3wc", (128, 4 * D), BF16) for _ in range(2)]
    amt = [C.sb("p3am", (128, Lc)) for _ in range(2)]
    x1t = [C.sb("p3x1", (128, Lc)) for _ in range(2)]
    psY = [C.psum("p3psY0", (128, Lc)), C.psum("p3psY1", (128, Lc))]
    psZ = [C.psum("p3psZ0", (128, Lc)), C.psum("p3psZ1", (128, Lc))]
    psC = [C.psum(f"p3psC{b}", (128, 2 * Lc)) for b in range(4)]

    for c in range(NCH):
        for h in range(4):
            for g in range(8):
                pY, pYk = psY[g % 2], ("p3psY", g % 2)
                C.tstep(lambda e, h=h, g=g, c=c, pY=pY: e.matmul(
                            pY[:, 0:Lc], rm[h, g][:], xn[g][:, Lc * c:Lc * (c + 1)],
                            start=True, stop=True),
                        reads=[("p3rm", h, g), ("p3xn", g)], writes=[pYk], name="p3y1mm")
                C.vstep(lambda e, g=g, pY=pY: e.tensor_copy(y1h[g][:, 0:Lc], pY[:, 0:Lc]),
                        reads=[pYk], writes=[("p3y1", g)], name="p3y1e")
            zi = 0
            for g in range(8):
                for mt in range(4):
                    pZ, pZk = psZ[zi % 2], ("p3psZ", zi % 2)
                    zi += 1
                    C.tstep(lambda e, h=h, g=g, mt=mt, pZ=pZ: e.matmul(
                                pZ[:, 0:Lc], ru[h, g, mt][:], y1h[g][:, 0:Lc],
                                start=True, stop=True),
                            reads=[("p3ru", h, g, mt), ("p3y1", g)], writes=[pZk], name="p3z1mm")
                    kt = h * 32 + g * 4 + mt
                    C.sstep(lambda e, kt=kt, pZ=pZ: e.activation(z1c[kt][:, 0:Lc], pZ[:, 0:Lc], AF.Gelu),
                            reads=[pZk], writes=[("p3z1", kt)], name="p3gelu")
        for k4 in range(32):
            kk = k4 % 2
            C.load(wc[kk][:, :],
                   cpj_d.rearrange("(g p) n -> p g n", p=128)[:, 4 * k4:4 * (k4 + 1), :],
                   key=("p3wc", kk), queue=("Q" if k4 % 2 == 0 else "QG"), name="p3lwc")
            for ks in range(4):
                kt = 4 * k4 + ks

                def cfn(e, kt=kt, ks=ks, kk=kk):
                    last = None
                    for m in range(8):
                        b = m // 2
                        off = (m % 2) * Lc
                        last = e.matmul(
                            psC[b][:, off:off + Lc], wc[kk][:, D * ks + 128 * m:D * ks + 128 * (m + 1)],
                            z1c[kt][:, 0:Lc],
                            start=(kt == 0 and m % 2 == 0), stop=(kt == 127 and m % 2 == 1),
                            skip_group_check=True)
                    return last
                C.tstep(cfn, reads=[("p3wc", kk), ("p3z1", kt)],
                        writes=[("p3psC", b) for b in range(4)], name="p3cmm")
        for m in range(8):
            b = m // 2
            off = (m % 2) * Lc
            par = m % 2
            C.load(amt[par][:, 0:Lc], accT[128 * m:128 * (m + 1), Lc * c:Lc * (c + 1)],
                   key=("p3am", par), queue="QG", name="p3lac")
            C.vstep(lambda e, b=b, off=off, par=par: e.tensor_add(
                        x1t[par][:, 0:Lc], psC[b][:, off:off + Lc], amt[par][:, 0:Lc]),
                    reads=[("p3psC", b), ("p3am", par)], writes=[("p3x1", par)], name="p3add")
            C.store(x1t[par][:, 0:Lc], x1T[128 * m:128 * (m + 1), Lc * c:Lc * (c + 1)],
                    rkeys=[("p3x1", par)], name="p3st")



def ph_p4(C, com):
    x1T = C.dscratch("x1T", (D, T), F32)
    xn1bf = C.dscratch("xn1T_bf", (D, T), BF16)
    xqbf = C.dscratch("xqT_bf", (D, T), BF16)
    W = 512
    NCH = T // W
    x1 = [C.sb("p4x1", (128, W)) for _ in range(8)]
    xn1 = [C.sb("p4xn", (128, W)) for _ in range(8)]
    xnb = C.sb("p4xnb", (128, W), BF16)
    ln = LnScratch(C, W, "p4ln")
    xtok = [C.sb("p4tok", (128, D)) for _ in range(4)]
    xqtok = [C.sb("p4qt", (128, D), BF16) for _ in range(4)]
    scl = C.sb("p4scl", (128, D // 16))
    inv = C.sb("p4inv", (128, D // 16))
    qr = C.sb("p4qr", (128, D))
    qa = C.sb("p4qa", (128, D))
    qL = C.sb("p4qL", (128, D))
    qsg = C.sb("p4qsg", (128, D))
    xqT = C.sb("p4xqT", (128, W), BF16)
    pstr = [C.psum("p4pstr0", (128, 128)), C.psum("p4pstr1", (128, 128))]
    pstrb = [C.psum("p4pstrb0", (128, 128), BF16), C.psum("p4pstrb1", (128, 128), BF16)]
    for c in range(NCH):
        keys = []
        for i in range(8):
            k = ("p4x1", i)
            C.load(x1[i][:], x1T[128 * i:128 * (i + 1), c * W:(c + 1) * W], key=k, name="p4lx")
            keys.append(k)
        meanb, rstdb = emit_lnT(C, com, [t[:, 0:W] for t in x1], keys, W, ln)
        for i in range(8):
            C.vstep(lambda e, i=i: e.tensor_sub(xn1[i][:, 0:W], x1[i][:, 0:W], meanb[:, 0:W]),
                    reads=[("p4x1", i), "p4lnmb"], writes=[("p4f32", i)], name="p4sub")
            C.vstep(lambda e, i=i: e.tensor_mul(xn1[i][:, 0:W], xn1[i][:, 0:W], rstdb[:, 0:W]),
                    reads=[("p4f32", i), "p4lnrb"], writes=[("p4f32", i)], name="p4mul")
            C.vstep(lambda e, i=i: e.tensor_copy(xnb[:, 0:W], xn1[i][:, 0:W]),
                    reads=[("p4f32", i)], writes=["p4xnb"], name="p4cb")
            C.store(xnb[:, 0:W], xn1bf[128 * i:128 * (i + 1), c * W:(c + 1) * W],
                    rkeys=["p4xnb"], name="p4st")
        trl = [(tt, ft) for tt in range(4) for ft in range(8)]
        for i, (tt, ft) in enumerate(trl):
            par = i % 2
            C.tstep(lambda e, tt=tt, ft=ft, par=par: e.transpose(
                        pstr[par][:, 0:128], xn1[ft][:, 128 * tt:128 * (tt + 1)], com.ident[:]),
                    reads=[("p4f32", ft), "ident"], writes=[("p4pstr", par)], name="p4tr")
            C.vstep(lambda e, tt=tt, ft=ft, par=par: e.tensor_copy(
                        xtok[tt][:, 128 * ft:128 * (ft + 1)], pstr[par][:, 0:128]),
                    reads=[("p4pstr", par)], writes=[("p4tok", tt)], name="p4tre")
        for tt in range(4):
            emit_scales(C, xtok[tt][:, 0:D], scl, inv, 128, D, "p4s",
                        extra_reads=[("p4tok", tt)], ckey="p4sC", ikey="p4sI")
            emit_qdq(C, xtok[tt][:, 0:D], xqtok[tt][:, 0:D], bview(inv, 128, D), bview(scl, 128, D),
                     qr, qa, qL, qsg, 128, D, "p4q",
                     skey=("p4tok", tt), ikey="p4sI", ckey="p4sC", tt_eng="G")
            C.P.last_write[("p4qd", tt)] = C.P.last_write["p4qD"]
        for ft in range(8):
            for tt in range(4):
                par = tt % 2
                C.tstep(lambda e, tt=tt, ft=ft, par=par: e.transpose(
                            pstrb[par][:, 0:128], xqtok[tt][:, 128 * ft:128 * (ft + 1)], com.identb[:]),
                        reads=[("p4qd", tt), "identb", "p4qD"], writes=[("p4pstrb", par)], name="p4trq")
                C.vstep(lambda e, tt=tt, ft=ft, par=par: e.tensor_copy(
                            xqT[:, 128 * tt:128 * (tt + 1)], pstrb[par][:, 0:128]),
                        reads=[("p4pstrb", par)], writes=["p4xqT"], name="p4trqe")
            C.store(xqT[:, 0:W], xqbf[128 * ft:128 * (ft + 1), c * W:(c + 1) * W],
                    rkeys=["p4xqT"], name="p4stq")



def ph_p6(C, com):
    xqbf = C.dscratch("xqT_bf", (D, T), BF16)
    x2aT = C.dscratch("x2aT", (D, T), F32)
    outT = C.dout("outT", (D, T), F32)
    wgq_d = C.dscratch("wgqT_bf", (D, 2816), BF16)
    wuq_d = C.dscratch("wuqT_bf", (D, 2816), BF16)
    wdq_d = C.dscratch("wdqT_bf", (2816, D), BF16)
    I2816 = 2816
    NKD = 22

    xq = []
    for k in range(8):
        t = C.sb("p6xq", (128, T), BF16)
        C.load(t[:], xqbf[128 * k:128 * (k + 1), :], key=("p6xq", k), name="p6lxq")
        xq.append(t)
    wg = []
    wu = []
    for k in range(8):
        t = C.sb("p6wg", (128, I2816), BF16)
        C.load(t[:], wgq_d[128 * k:128 * (k + 1), :], key=("p6wg", k), name="p6lwg")
        wg.append(t)
        t2 = C.sb("p6wu", (128, I2816), BF16)
        C.load(t2[:], wuq_d[128 * k:128 * (k + 1), :], key=("p6wu", k), name="p6lwu")
        wu.append(t2)
    wd = [C.sb("p6wd", (128, 4 * D), BF16) for _ in range(2)]

    hbuf = C.sb("p6h", (128, I2816))
    sg = C.sb("p6sg", (128, 512))
    hq = C.sb("p6hq", (128, I2816), BF16)
    hqT = [C.sb("p6hqT", (128, 512), BF16) for _ in range(NKD)]
    HF = I2816 // 2
    scl = C.sb("p6scl", (128, 88))
    inv = C.sb("p6inv", (128, 88))
    qr = C.sb("p6qr", (128, HF))
    qa = C.sb("p6qa", (128, HF))
    qL = C.sb("p6qL", (128, HF))
    qsg = C.sb("p6qsg", (128, HF))
    x2m = C.sb("p6x2", (128, 512))
    om = C.sb("p6om", (128, 512))
    x2m2 = [x2m, x2m]
    om2 = [om, om]
    psG = [C.psum("p6psG0", (128, 512)), C.psum("p6psG1", (128, 512))]
    psU = [C.psum("p6psU0", (128, 512)), C.psum("p6psU1", (128, 512))]
    psD = [C.psum(f"p6psD{i}", (128, 512)) for i in range(2)]
    psT = [C.psum("p6psT0", (128, 128), BF16), C.psum("p6psT1", (128, 128), BF16)]
    sg2 = [sg, C.sb("p6sg2", (128, 512))]

    NW = [512] * 5 + [256]
    for tg in range(4):
        for t4 in range(4):
            tt = tg * 4 + t4
            offs = []
            o = 0
            for wn in NW:
                offs.append((o, wn))
                o += wn

            def gu_part(wi):
                off, wn = offs[wi]
                par = wi % 2
                for kk in range(8):
                    C.tstep(lambda e, tt=tt, kk=kk, off=off, wn=wn, par=par: e.matmul(
                                psG[par][:, 0:wn], xq[kk][:, 128 * tt:128 * (tt + 1)],
                                wg[kk][:, off:off + wn], start=(kk == 0), stop=(kk == 7)),
                            reads=[("p6xq", kk), ("p6wg", kk)], writes=[("p6psG", par)], name="p6gmm")
                for kk in range(8):
                    C.tstep(lambda e, tt=tt, kk=kk, off=off, wn=wn, par=par: e.matmul(
                                psU[par][:, 0:wn], xq[kk][:, 128 * tt:128 * (tt + 1)],
                                wu[kk][:, off:off + wn], start=(kk == 0), stop=(kk == 7)),
                            reads=[("p6xq", kk), ("p6wu", kk)], writes=[("p6psU", par)], name="p6umm")

            def sh_part(wi):
                off, wn = offs[wi]
                par = wi % 2
                sgp = sg2[par]
                sgk = ("p6sg", par)
                C.sstep(lambda e, wn=wn, sgp=sgp, par=par: e.activation(sgp[:, 0:wn], psG[par][:, 0:wn], AF.Silu),
                        reads=[("p6psG", par)], writes=[sgk], name="p6silu")
                C.vstep(lambda e, off=off, wn=wn, sgp=sgp, par=par: e.tensor_mul(
                            hbuf[:, off:off + wn], sgp[:, 0:wn], psU[par][:, 0:wn]),
                        reads=[sgk, ("p6psU", par)], writes=["p6hb"], name="p6hmul")
            gu_part(0)
            for wi in range(1, len(NW)):
                gu_part(wi)
                sh_part(wi - 1)
            sh_part(len(NW) - 1)
            for hf in range(2):
                emit_scales(C, hbuf[:, hf * HF:(hf + 1) * HF], scl, inv, 128, HF, "p6s",
                            extra_reads=["p6hb"], ckey="p6sC", ikey="p6sI")
                emit_qdq(C, hbuf[:, hf * HF:(hf + 1) * HF], hq[:, hf * HF:(hf + 1) * HF],
                         bview(inv, 128, HF), bview(scl, 128, HF), qr, qa, qL, qsg,
                         128, HF, "p6q", skey="p6hb", ikey="p6sI", ckey="p6sC", tt_eng="G")
                C.P.last_write["p6qD2"] = C.P.last_write["p6qD"]
            for k in range(NKD):
                pT, pTk = psT[k % 2], ("p6psT", k % 2)
                C.tstep(lambda e, k=k, pT=pT: e.transpose(pT[:, 0:128], hq[:, 128 * k:128 * (k + 1)],
                                                          com.identb[:]),
                        reads=["p6qD", "p6qD2", "identb"], writes=[pTk], name="p6trh")
                C.vstep(lambda e, k=k, t4=t4, pT=pT: e.tensor_copy(hqT[k][:, 128 * t4:128 * (t4 + 1)],
                                                                   pT[:, 0:128]),
                        reads=[pTk], writes=[("p6hqT", k)], name="p6trhe")
        for half in range(4):
            for k4 in range(6):
                kk = k4 % 2
                nk = min(4, NKD - 4 * k4)
                C.load(wd[kk][:, 0:nk * D],
                       wdq_d.rearrange("(g p) n -> p g n", p=128)[:, 4 * k4:4 * k4 + nk, :],
                       key=("p6wd", kk), queue=("Q" if k4 % 2 == 0 else "QG"), name="p6lwd")
                for ks in range(nk):
                    k = 4 * k4 + ks
                    for mm2 in range(2):
                        m = 2 * half + mm2
                        C.tstep(lambda e, m=m, k=k, ks=ks, kk=kk, mm2=mm2: e.matmul(
                                    psD[mm2][:, 0:512], wd[kk][:, D * ks + 128 * m:D * ks + 128 * (m + 1)],
                                    hqT[k][:, 0:512],
                                    start=(k == 0), stop=(k == NKD - 1)),
                                reads=[("p6wd", kk)] + [("p6hqT", k)], writes=[("p6psD", mm2)], name="p6dmm")
            for mm2 in range(2):
                m = 2 * half + mm2
                C.load(x2m[:, 0:512], x2aT[128 * m:128 * (m + 1), 512 * tg:512 * (tg + 1)],
                       key="p6x2", queue="QG", name="p6lx2")
                C.vstep(lambda e, mm2=mm2: e.tensor_add(om[:, 0:512], psD[mm2][:, 0:512],
                                                        x2m[:, 0:512]),
                        reads=[("p6psD", mm2), "p6x2"], writes=["p6om"], name="p6add")
                C.store(om[:, 0:512], outT[128 * m:128 * (m + 1), 512 * tg:512 * (tg + 1)],
                        rkeys=["p6om"], name="p6st")



def ph_p5(C, com):
    xn1bf = C.dscratch("xn1T_bf", (D, T), BF16)
    x1T = C.dscratch("x1T", (D, T), F32)
    x2aT = C.dscratch("x2aT", (D, T), F32)
    qkv_d = C.din("qkvT", (D, 3 * D), F32)
    wo_d = C.din("WoT", (D, D), F32)
    bqkv_d = C.din("b_qkv", (3 * D,), F32)
    bo_d = C.din("b_o", (D,), F32)
    msk_d = C.din("mask128", (128, 128), F32)

    xn1 = []
    for k in range(8):
        t = C.sb("p5xn", (128, T), BF16)
        C.load(t[:], xn1bf[128 * k:128 * (k + 1), :], key=("p5xn", k), name="p5lxn")
        xn1.append(t)
    qkv, qkvk = load_wbf(C, "p5qkv", qkv_d, D, 3 * D)
    wo, wok = load_wbf(C, "p5wo", wo_d, D, D)
    bq = C.sb("p5bq", (64, 48))
    for j in range(48):
        C.load(bq[:, j:j + 1], bqkv_d[64 * j:64 * (j + 1)].unsqueeze(-1), key=("p5bq", j), name="p5lbq")
    bo, _ = load_bias_cols(C, "p5bo", bo_d, D)
    mskb = C.sb("p5msk", (128, 128), BF16)
    with C.sub_scope():
        mskf = C.sb("p5mskf", (128, 128))
        C.load(mskf[:], msk_d, key="p5mskf", name="p5lmsk")
        C.vstep(lambda e: e.tensor_copy(mskb[:], mskf[:]), reads=["p5mskf"], writes=["p5msk"], name="p5cmsk")

    QT = C.sb("p5QT", (64, T), BF16)
    KT = C.sb("p5KT", (64, T), BF16)
    VT = C.sb("p5VT", (64, T), BF16)
    Vtok = [C.sb("p5Vt", (128, 65), BF16) for _ in range(16)]
    PT = [C.sb("p5PT", (128, 512), BF16) for _ in range(2)]
    OT = [C.sb("p5OT", (128, T), BF16) for _ in range(8)]
    rec = C.sb("p5rec", (1, 512))
    recb = C.sb("p5recb", (64, 512))
    x1m = C.sb("p5x1m", (128, 512))
    aom = C.sb("p5aom", (128, 512))
    psQ = [C.psum("p5psQ0", (64, 512)), C.psum("p5psQ1", (64, 512))]
    psS = [C.psum("p5psS0", (128, 512)), C.psum("p5psS1", (128, 512))]
    psO = C.psum("p5psO", (65, 512))
    psR = C.psum("p5psR", (64, 512))
    psVt = C.psum("p5psVt", (128, 64), BF16)
    psW = C.psum("p5psW", (128, 512))

    for tt in range(16):
        C.vstep(lambda e, tt=tt: e.memset(Vtok[tt][:, 64:65], 1.0), writes=[("p5vone", tt)], name="p5one")

    for h in range(16):
        qkvjobs = []
        for part, dst, dkey in ((0, QT, "p5QT"), (1, KT, "p5KT"), (2, VT, "p5VT")):
            for qc in range(4):
                qkvjobs.append((part, dst, dkey, qc))

        def qkv_mm(i):
            part, dst, dkey, qc = qkvjobs[i]
            col = part * D + h * 64
            par = i % 2
            pairs = [(qkv[k][:, col:col + 64], xn1[k][:, 512 * qc:512 * (qc + 1)])
                     for k in range(8)]
            C.mm(psQ[par][0:64, 0:512], pairs, rkeys=qkvk + [("p5xn", k) for k in range(8)],
                 wkey=("p5psQ", par), name="p5qkvmm")

        def qkv_ev(i):
            part, dst, dkey, qc = qkvjobs[i]
            bidx = part * 16 + h
            par = i % 2
            pq = psQ[par]
            if part == 0:
                C.vstep(lambda e, qc=qc, bidx=bidx, pq=pq: e.tensor_scalar(
                            QT[0:64, 512 * qc:512 * (qc + 1)], pq[0:64, 0:512],
                            bq[:, bidx:bidx + 1], 0.125, ALU.add, ALU.mult),
                        reads=[("p5psQ", par), ("p5bq", bidx)], writes=[dkey], name="p5qe")
            else:
                C.vstep(lambda e, qc=qc, bidx=bidx, dst=dst, pq=pq: e.tensor_scalar_add(
                            dst[0:64, 512 * qc:512 * (qc + 1)], pq[0:64, 0:512],
                            bq[:, bidx:bidx + 1]),
                        reads=[("p5psQ", par), ("p5bq", bidx)], writes=[dkey], name="p5kve")
        qkv_mm(0)
        for i in range(1, len(qkvjobs)):
            qkv_mm(i)
            qkv_ev(i - 1)
        qkv_ev(len(qkvjobs) - 1)
        for tt in range(16):
            C.tstep(lambda e, tt=tt: e.transpose(psVt[:, 0:64], VT[0:64, 128 * tt:128 * (tt + 1)],
                                                 com.identb[0:64, 0:64]),
                    reads=["p5VT", "identb"], writes=["p5psVt"], name="p5vtr")
            C.vstep(lambda e, tt=tt: e.tensor_copy(Vtok[tt][:, 0:64], psVt[:, 0:64]),
                    reads=["p5psVt"], writes=[("p5vt", tt)], name="p5vte")
        ot = OT[h // 2]
        orow = (h % 2) * 64
        for qc in range(4):
            nkt = 4 * qc + 4

            def s_part(kt, qc=qc):
                par = kt % 2
                pS, pSk = psS[par], ("p5psS", par)
                pt_, ptk = PT[par], ("p5PT", par)
                C.tstep(lambda e, kt=kt, qc=qc, pS=pS: e.matmul(
                            pS[:, 0:512], KT[0:64, 128 * kt:128 * (kt + 1)],
                            QT[0:64, 512 * qc:512 * (qc + 1)], start=True, stop=True),
                        reads=["p5KT", "p5QT"], writes=[pSk], name="p5smm")
                scol = (kt - 4 * qc) * 128
                if kt >= 4 * qc:
                    if scol > 0:
                        C.vstep(lambda e, scol=scol, pt_=pt_: e.memset(pt_[:, 0:scol], 0.0),
                                reads=[], writes=[ptk], name="p5pz")
                    C.sstep(lambda e, scol=scol, pt_=pt_, pS=pS: e.activation(pt_[:, scol:512], pS[:, scol:512], AF.Exp),
                            reads=[pSk, ptk], writes=[ptk], name="p5exp")
                    C.vstep(lambda e, scol=scol, pt_=pt_: e.tensor_mul(pt_[:, scol:scol + 128],
                                                              pt_[:, scol:scol + 128], mskb[:]),
                            reads=[ptk, "p5msk"], writes=[ptk], name="p5msk2")
                else:
                    C.sstep(lambda e, pt_=pt_, pS=pS: e.activation(pt_[:, 0:512], pS[:, 0:512], AF.Exp),
                            reads=[pSk], writes=[ptk], name="p5exp")

            def pv_part(kt, nkt=nkt):
                par = kt % 2
                pt_, ptk = PT[par], ("p5PT", par)
                C.tstep(lambda e, kt=kt, nkt=nkt, pt_=pt_: e.matmul(
                            psO[0:65, 0:512], Vtok[kt][:, 0:65], pt_[:, 0:512],
                            start=(kt == 0), stop=(kt == nkt - 1)),
                        reads=[("p5vt", kt), ("p5vone", kt), ptk], writes=["p5psO"], name="p5pv")
            s_part(0)
            for kt in range(1, nkt):
                s_part(kt)
                pv_part(kt - 1)
            pv_part(nkt - 1)
            C.vstep(lambda e: e.reciprocal(rec[0:1, 0:512], psO[64:65, 0:512]),
                    reads=["p5psO"], writes=["p5rec"], name="p5rcp")
            C.tstep(lambda e: e.matmul(psR[0:64, 0:512], com.ones1[0:1, 0:64], rec[0:1, 0:512],
                                       start=True, stop=True),
                    reads=["p5rec", "ones1"], writes=["p5psR"], name="p5rb")
            C.vstep(lambda e: e.tensor_copy(recb[0:64, 0:512], psR[0:64, 0:512]),
                    reads=["p5psR"], writes=["p5recb"], name="p5rbe")
            C.vstep(lambda e, ot=ot, orow=orow, qc=qc: e.tensor_mul(
                        ot[orow:orow + 64, 512 * qc:512 * (qc + 1)], psO[0:64, 0:512],
                        recb[0:64, 0:512]),
                    reads=["p5psO", "p5recb"], writes=[("p5OT", h // 2)], name="p5onorm")
    for m in range(8):
        for qc in range(4):
            pairs = [(wo[k][:, 128 * m:128 * (m + 1)], OT[k][:, 512 * qc:512 * (qc + 1)])
                     for k in range(8)]
            C.mm(psW[:, 0:512], pairs, rkeys=wok + [("p5OT", k) for k in range(8)],
                 wkey="p5psW", name="p5womm")
            C.load(x1m[:, 0:512], x1T[128 * m:128 * (m + 1), 512 * qc:512 * (qc + 1)],
                   key="p5x1m", queue="QG", name="p5lx1")
            C.vstep(lambda e, m=m: e.tensor_scalar_add(aom[:, 0:512], psW[:, 0:512], bo[:, m:m + 1]),
                    reads=["p5psW", ("p5bo", m)], writes=["p5aom"], name="p5wbe")
            C.vstep(lambda e: e.tensor_add(aom[:, 0:512], aom[:, 0:512], x1m[:, 0:512]),
                    reads=["p5aom", "p5x1m"], writes=["p5aom"], name="p5res")
            C.store(aom[:, 0:512], x2aT[128 * m:128 * (m + 1), 512 * qc:512 * (qc + 1)],
                    rkeys=["p5aom"], name="p5st")



PHASE_ORDER = ["pw", "p0", "p1", "p2", "p3", "p4", "p5", "p6"]
_BUILD_CACHE = {}


def _get_program(phases, fused):
    key = (tuple(phases), fused)
    if key not in _BUILD_CACHE:
        _BUILD_CACHE[key] = build_program(phases, fused)
    return _BUILD_CACHE[key]


FUSED = True


def kernel(**inputs):
    hp = prep_common(inputs)
    hp["constv"] = np.tile(np.array(CONSTS, np.float32), (128, 1))
    x = np.asarray(inputs["L0_attn_x"], np.float32)
    envs = []
    for b in range(NB):
        env = dict(hp)
        env["xT"] = np.ascontiguousarray(x[b].T)
        envs.append(env)
    if FUSED:
        nc, C = _get_program(PHASE_ORDER, True)
        in_names = [n for n, k in C.kinds.items() if k == "ExternalInput"]
        in_maps = [{n: env[n] for n in in_names} for env in envs]
        res = run_bass_kernel_spmd(nc, in_maps, core_ids=list(range(NB)))
        out = np.stack([np.ascontiguousarray(np.asarray(r["outT"]).T) for r in res.results], axis=0)
        return out.astype(np.float32)
    for ph in PHASE_ORDER:
        nc, C = _get_program([ph], False)
        in_names = [n for n, k in C.kinds.items() if k == "ExternalInput"]
        out_names = [n for n, k in C.kinds.items() if k == "ExternalOutput"]
        in_maps = [{n: env[n] for n in in_names} for env in envs]
        res = run_bass_kernel_spmd(nc, in_maps, core_ids=list(range(NB)))
        for env, r in zip(envs, res.results):
            for n in out_names:
                env[n] = np.asarray(r[n])
    out = np.stack([np.ascontiguousarray(env["outT"].T) for env in envs], axis=0)
    return out.astype(np.float32)


# version 15
# speedup vs baseline: 1.0173x; 1.0067x over previous
import numpy as np
import ml_dtypes
import concourse.bass as bass
import concourse.mybir as mybir
from concourse.bass_utils import run_bass_kernel_spmd
from contextlib import ExitStack

F32 = mybir.dt.float32
BF16 = mybir.dt.bfloat16
AF = mybir.ActivationFunctionType
ALU = mybir.AluOpType
AX = mybir.AxisListType

T = 2048
D = 1024
DI = 2048
DS = 16
DTR = 64
NB = 8

STREAMS = ("T", "V", "S", "G", "Q")
SEM_ENGINES = ("T", "V", "S", "G", "Q", "QG", "QSY")
STREAM_OF = {"T": "T", "V": "V", "S": "S", "G": "G", "Q": "Q", "QG": "G", "QSY": "Q"}


class Step:
    __slots__ = ("engine", "stream", "fn", "deps", "inc", "marked", "val", "name")

    def __init__(self, engine, fn, deps, inc, name=""):
        self.engine = engine
        self.stream = STREAM_OF[engine]
        self.fn = fn
        self.deps = deps
        self.inc = inc
        self.marked = False
        self.val = None
        self.name = name


class Prog:

    def __init__(self):
        self.steps = []
        self.last_write = {}
        self.readers = {}

    def step(self, engine, fn, reads=(), writes=(), deps=(), name=""):
        dd = set(d for d in deps if d is not None)
        for k in reads:
            w = self.last_write.get(k)
            if w is not None:
                dd.add(w)
        for k in writes:
            w = self.last_write.get(k)
            if w is not None:
                dd.add(w)
            for r in self.readers.get(k, ()):
                dd.add(r)
        s = Step(engine, fn, dd, 16 if engine in ("Q", "QG") else 1, name)
        self.steps.append(s)
        for k in reads:
            self.readers.setdefault(k, set()).add(s)
        for k in writes:
            self.last_write[k] = s
            self.readers[k] = set()
        return s

    def dma(self, queue, fn, reads=(), writes=(), deps=(), name=""):
        assert queue in ("Q", "QG")
        return self.step(queue, fn, reads, writes, deps, name)

    def barrier(self):
        lasts = {}
        for s in self.steps:
            lasts[s.engine] = s
        prev = list(lasts.values())
        out = []
        for e in ("T", "V", "S", "G", "QSY"):
            out.append(self.step(e, None, deps=prev, name="barrier"))
        return out

    def emit(self, nc, es):
        for s in self.steps:
            for d in s.deps:
                d.marked = True
        cnt = dict.fromkeys(SEM_ENGINES, 0)
        for s in self.steps:
            if s.marked:
                cnt[s.engine] += s.inc
                s.val = cnt[s.engine]
        epoch_end = {}
        fence = {}
        cur = {"Q": [], "QG": []}
        closed = {"Q": 0, "QG": 0}

        def close_epoch(q):
            if cur[q]:
                end = cur[q][-1].val
                for m in cur[q]:
                    epoch_end[m] = end
                closed[q] = end
                cur[q].clear()
        for s in self.steps:
            if s.engine in ("Q", "QG"):
                q = s.engine
                if s.deps and cur[q]:
                    close_epoch(q)
                if not cur[q] and closed[q] > 0:
                    fence[s] = closed[q]
                cur[q].append(s)
            for d in s.deps:
                if d.engine in ("Q", "QG") and d is not s and d in cur[d.engine] \
                   and not (s.engine == d.engine):
                    close_epoch(d.engine)
        for q in ("Q", "QG"):
            if cur[q]:
                end = cur[q][-1].val
                for m in cur[q]:
                    epoch_end[m] = end
        sems = {e: es.enter_context(nc.semaphore(f"sem_{e}")) for e in SEM_ENGINES}
        per_stream = {e: [s for s in self.steps if s.stream == e] for e in STREAMS}
        watermark = {}

        def run_stream(ename, eng):
            wm = watermark.setdefault(ename, dict.fromkeys(SEM_ENGINES, 0))
            for s in per_stream[ename]:
                if s in fence:
                    v = fence[s]
                    if v > wm[s.engine]:
                        eng.wait_ge(sems[s.engine], v)
                        wm[s.engine] = v
                for d in sorted(s.deps, key=lambda d: (d.engine, d.val or 0)):
                    assert d.marked and d.val is not None, (s.name, d.name)
                    v = epoch_end[d] if d.engine in ("Q", "QG") else d.val
                    if v > wm[d.engine]:
                        eng.wait_ge(sems[d.engine], v)
                        wm[d.engine] = v
                inst = s.fn(eng) if s.fn is not None else None
                if s.marked:
                    if inst is None:
                        eng.sem_inc(sems[s.engine], s.inc)
                    else:
                        inst.then_inc(sems[s.engine], s.inc)

        return run_stream

    def build_block(self, nc, es):
        run_stream = self.emit(nc, es)
        with nc.Block() as block:
            @block.tensor
            def _(e):
                run_stream("T", e)

            @block.vector
            def _(e):
                run_stream("V", e)

            @block.scalar
            def _(e):
                run_stream("S", e)

            @block.gpsimd
            def _(e):
                run_stream("G", e)

            @block.sync
            def _(e):
                run_stream("Q", e)



def prep_common(I):
    hp = {}
    f32 = np.float32

    def T_(a):
        return np.ascontiguousarray(np.asarray(a, f32).T)

    hp["inprojT"] = T_(I["L0_attn_in_proj_w"])
    hp["gateT"] = T_(I["L0_attn_gate_w"])
    hp["fdw1T"] = T_(I["L0_attn_fd_w1"])
    hp["fdw2T"] = T_(I["L0_attn_fd_w2"])
    hp["outprojT"] = T_(I["L0_attn_out_proj_w"])
    xp = np.asarray(I["L0_attn_x_proj_w"], f32)
    hp["xprojT_dt"] = T_(xp[:DTR])
    hp["xprojT_BC"] = T_(xp[DTR:DTR + 32])
    hp["dtprojT"] = T_(I["L0_attn_dt_proj_w"])
    hp["convw"] = np.ascontiguousarray(np.asarray(I["L0_attn_conv_w"], f32)[:, 0, :])
    for k_src, k_dst in [
        ("L0_attn_conv_b", "conv_b"), ("L0_attn_dt_proj_b", "dt_b"),
        ("L0_attn_Dp", "Dp"), ("L0_attn_gate_b", "gate_b"),
        ("L0_attn_fd_b1", "fd_b1"), ("L0_attn_fd_ln_g", "fd_ln_g"),
        ("L0_attn_fd_ln_b", "fd_ln_b"), ("L0_attn_fd_b2", "fd_b2"),
        ("L0_attn_norm_g", "norm_g"), ("L0_attn_norm_b", "norm_b"),
        ("L1_attn_b_qkv", "b_qkv"), ("L1_attn_b_o", "b_o"),
    ]:
        hp[k_dst] = np.ascontiguousarray(np.asarray(I[k_src], f32))
    hp["A_log"] = np.ascontiguousarray(np.asarray(I["L0_attn_A_log"], f32))
    hp["balance"] = np.asarray(I["L0_attn_balance_factor"], f32).reshape(1)

    rm = np.asarray(I["L0_ffn_rail_maker"], f32)
    ru = np.asarray(I["L0_ffn_rail_up_proj"], f32)
    rm_bd = np.zeros((4, 8, 128, 128), f32)
    for h in range(4):
        for g in range(8):
            for p in range(8):
                r = 8 * g + p
                rm_bd[h, g, 16 * p:16 * p + 16, 16 * p:16 * p + 16] = rm[h, r]
    hp["rm_bd"] = rm_bd
    ru_bd = np.zeros((4, 8, 4, 128, 128), f32)
    for h in range(4):
        for g in range(8):
            for mt in range(4):
                for p2 in range(2):
                    r = 8 * g + 2 * mt + p2
                    blk = ru[h, r].T
                    ru_bd[h, g, mt, 16 * (2 * mt + p2):16 * (2 * mt + p2) + 16,
                          64 * p2:64 * p2 + 64] = blk
    hp["ru_bd"] = ru_bd
    hp["cprojT"] = T_(I["L0_ffn_w_cproj"])

    hp["qkvT"] = T_(I["L1_attn_W_qkv"])
    hp["WoT"] = T_(I["L1_attn_W_o"])
    hp["wg_nat"] = np.ascontiguousarray(np.asarray(I["L1_ffn_w_gate"], f32))
    hp["wu_nat"] = np.ascontiguousarray(np.asarray(I["L1_ffn_w_up"], f32))
    hp["wd_nat"] = np.ascontiguousarray(np.asarray(I["L1_ffn_w_down"], f32))
    hp["wgT"] = T_(I["L1_ffn_w_gate"])
    hp["wuT"] = T_(I["L1_ffn_w_up"])
    hp["wdT"] = T_(I["L1_ffn_w_down"])

    hp["ident"] = np.eye(128, dtype=f32)
    sel = np.zeros((32, 32 * 128), f32)
    for r in range(32):
        sel[r, r * 128:(r + 1) * 128] = 1.0
    hp["sel32"] = sel
    e8 = np.zeros((8, 128), f32)
    for p in range(128):
        e8[p // 16, p] = 1.0
    hp["E8"] = e8
    esa = np.zeros((128, 16 * 128), f32)
    for j in range(16):
        for p in range(128):
            k = 8 * j + p // 16
            if k < 128:
                esa[k, j * 128 + p] = 1.0
    hp["EselA"] = esa
    esb = np.zeros((48, 6 * 128), f32)
    for j in range(6):
        for p in range(128):
            esb[8 * j + p // 16, j * 128 + p] = 1.0
    hp["EselB"] = esb
    msk = np.tril(np.ones((128, 128), f32)).T
    hp["mask128"] = np.ascontiguousarray(msk)
    return hp



class Ctx:
    def __init__(self, nc, P, es, fused, want_in, want_out):
        self.nc = nc
        self.P = P
        self.es = es
        self.fused = fused
        self.want_in = want_in
        self.want_out = want_out
        self.dram = {}
        self.kinds = {}
        self.uniq = 0

    def _mk(self, name, shape, dt, kind):
        if name in self.dram:
            return self.dram[name]
        ap = self.nc.dram_tensor(name, list(shape), dt, kind=kind).ap()
        self.dram[name] = ap
        self.kinds[name] = kind
        return ap

    def din(self, name, shape, dt):
        return self._mk(name, shape, dt, "ExternalInput")

    def dscratch(self, name, shape, dt):
        if self.fused:
            return self._mk(name, shape, dt, "Internal")
        kind = "ExternalOutput" if name in self.want_out else "ExternalInput"
        return self._mk(name, shape, dt, kind)

    def dout(self, name, shape, dt):
        return self._mk(name, shape, dt, "ExternalOutput")

    from contextlib import contextmanager as _cm

    @_cm
    def sub_scope(self):
        old_es = self.es
        self.es = ExitStack()
        try:
            yield
        finally:
            self.P.barrier()
            self.es.close()
            self.es = old_es

    def sb(self, name, shape, dt=F32):
        self.uniq += 1
        return self.es.enter_context(self.nc.sbuf_tensor(f"{name}_{self.uniq}", list(shape), dt))

    def psum(self, name, shape, dt=F32):
        self.uniq += 1
        return self.es.enter_context(self.nc.psum_tensor(f"{name}_{self.uniq}", list(shape), dt))

    def load(self, dst, src, key, queue="Q", deps=(), name="load"):
        return self.P.dma(queue, lambda e: e.dma_start(out=dst, in_=src),
                          reads=(), writes=[key], deps=deps, name=name)

    def store(self, src, dst, rkeys, queue="Q", deps=(), name="store"):
        return self.P.dma(queue, lambda e: e.dma_start(out=dst, in_=src),
                          reads=rkeys, writes=(), deps=deps, name=name)

    def mm(self, psum_ap, pairs, rkeys, wkey, deps=(), name="mm"):
        n = len(pairs)

        def fn(eng):
            last = None
            for i, (l, r) in enumerate(pairs):
                last = eng.matmul(psum_ap, l, r, start=(i == 0), stop=(i == n - 1))
            return last
        return self.P.step("T", fn, reads=rkeys, writes=[wkey], deps=deps, name=name)

    def vstep(self, fn, reads=(), writes=(), deps=(), name="v"):
        return self.P.step("V", fn, reads=reads, writes=writes, deps=deps, name=name)

    def sstep(self, fn, reads=(), writes=(), deps=(), name="s"):
        return self.P.step("S", fn, reads=list(reads) + ["consts"], writes=writes,
                           deps=deps, name=name)

    def tstep(self, fn, reads=(), writes=(), deps=(), name="t"):
        return self.P.step("T", fn, reads=reads, writes=writes, deps=deps, name=name)

    def gstep(self, fn, reads=(), writes=(), deps=(), name="g"):
        return self.P.step("G", fn, reads=reads, writes=writes, deps=deps, name=name)


def ceil_div(a, b):
    return (a + b - 1) // b


def emit_qdq(C, src, dst, invb, sclb, r, a, L, sgn, p, F, tag, extra_reads=(),
             skey=None, ikey=None, ckey=None, eng="V", tt_eng=None):
    v = C.vstep if eng == "V" else C.gstep
    vt = C.gstep if tt_eng == "G" else v
    s = C.sstep
    nb = F // 16
    skey = skey or (tag + "S")
    ikey = ikey or (tag + "I")
    ckey = ckey or (tag + "C")
    rk = [skey] + list(extra_reads)
    src3 = src[0:p, 0:F].rearrange("q (b k) -> q b k", k=16)
    v(lambda e: e.tensor_tensor(r[0:p, 0:F].rearrange("q (b k) -> q b k", k=16), src3, invb, ALU.mult),
      reads=rk + [ikey], writes=[tag + "r"], name=tag + "r")
    s(lambda e: e.sign(sgn[0:p, 0:F], src[0:p, 0:F]), reads=rk, writes=[tag + "g"], name=tag + "g")
    s(lambda e: e.activation(a[0:p, 0:F], r[0:p, 0:F], AF.Abs), reads=[tag + "r"], writes=[tag + "a"], name=tag + "a")
    v(lambda e: e.tensor_single_scalar(L[0:p, 0:F], a[0:p, 0:F], 0.25, ALU.is_gt),
      reads=[tag + "a"], writes=[tag + "L"], name=tag + "L")
    for thr in (0.75, 1.25, 1.75):
        v(lambda e, t=thr: e.scalar_tensor_tensor(L[0:p, 0:F], a[0:p, 0:F], t, L[0:p, 0:F], ALU.is_gt, ALU.add),
          reads=[tag + "a", tag + "L"], writes=[tag + "L"], name=tag + "L")
    v(lambda e: e.tensor_scalar(r[0:p, 0:F], a[0:p, 0:F], 5.0, 2.0, ALU.is_gt, ALU.mult),
      reads=[tag + "a", tag + "r"], writes=[tag + "H"], name=tag + "H")
    for thr in (3.5, 2.5):
        v(lambda e, t=thr: e.scalar_tensor_tensor(r[0:p, 0:F], a[0:p, 0:F], t, r[0:p, 0:F], ALU.is_gt, ALU.add),
          reads=[tag + "a", tag + "H"], writes=[tag + "H"], name=tag + "H")
    v(lambda e: e.scalar_tensor_tensor(L[0:p, 0:F], L[0:p, 0:F], 0.5, r[0:p, 0:F], ALU.mult, ALU.add),
      reads=[tag + "L", tag + "H"], writes=[tag + "q"], name=tag + "q")
    v(lambda e: e.tensor_tensor(a[0:p, 0:F].rearrange("q (b k) -> q b k", k=16),
                                sgn[0:p, 0:F].rearrange("q (b k) -> q b k", k=16), sclb, ALU.mult),
      reads=[tag + "g", ckey], writes=[tag + "ss"], name=tag + "ss")
    v(lambda e: e.tensor_tensor(dst[0:p, 0:F], L[0:p, 0:F], a[0:p, 0:F], ALU.mult),
      reads=[tag + "q", tag + "ss"], writes=[tag + "D"], name=tag + "D")


def emit_scales(C, src, scl, inv, p, F, tag, extra_reads=(), ckey=None, ikey=None, eng="V"):
    v = C.vstep if eng == "V" else C.gstep
    vr = C.vstep
    nb = F // 16
    ckey = ckey or (tag + "C")
    ikey = ikey or (tag + "I")
    rk = list(extra_reads)
    v(lambda e: e.tensor_reduce(scl[0:p, 0:nb], src[0:p, 0:F].rearrange("q (b k) -> q b k", k=16),
                                AX.X, ALU.max, apply_absolute_value=True),
      reads=rk, writes=[ckey], name=tag + "amax")
    v(lambda e: e.tensor_scalar(scl[0:p, 0:nb], scl[0:p, 0:nb], 1.0 / 6.0, 1e-12, ALU.mult, ALU.max),
      reads=[ckey], writes=[ckey], name=tag + "scl")
    vr(lambda e: e.reciprocal(inv[0:p, 0:nb], scl[0:p, 0:nb]),
      reads=[ckey], writes=[ikey], name=tag + "inv")


def bview(t, p, F):
    return t[0:p, 0:F // 16].unsqueeze(-1).broadcast_to([p, F // 16, 16])


def fview(t, p, F):
    return t[0:p, 0:F].rearrange("q (b k) -> q b k", k=16)


CONSTS = [1e-5, 1.0, 0.5, 0.7071067811865476, 30.0, -30.0, 0.125, 6.0]


class Commons:
    def __init__(self, C):
        nc = C.nc
        self.ident = C.sb("ident", (128, 128), F32)
        ident_d = C.din("ident", (128, 128), F32)
        C.load(self.ident[:], ident_d, key="ident")
        self.ones128 = C.sb("ones128", (128, 1), F32)
        self.ones1 = C.sb("ones1", (1, 128), F32)
        C.vstep(lambda e: e.memset(self.ones128[:], 1.0), writes=["ones128"], name="o128")
        C.vstep(lambda e: e.memset(self.ones1[:], 1.0), writes=["ones1"], name="o1")
        self.identb = C.sb("identb", (128, 128), BF16)
        C.vstep(lambda e: e.tensor_copy(self.identb[:], self.ident[:]),
                reads=["ident"], writes=["identb"], name="identb")
        self.cst = C.sb("cst", (128, len(CONSTS)), F32)
        cst_d = C.din("constv", (128, len(CONSTS)), F32)
        C.load(self.cst[:], cst_d, key="consts")
        for i, v in enumerate(CONSTS):
            nc.const_aps.register(F32, float(v), self.cst[:, i:i + 1])


class LnScratch:
    def __init__(self, C, W, tag):
        self.tag = tag
        self.W = W
        self.sq = [C.sb(tag + "sq", (128, W)) for _ in range(2)]
        self.ps1 = C.psum(tag + "p12", (1, W))
        self.psB = C.psum(tag + "pB", (128, W))
        self.rowm = C.sb(tag + "rm", (1, W))
        self.rowr = C.sb(tag + "rr", (1, W))
        self.rowt = C.sb(tag + "rt", (1, W))
        self.meanb = C.sb(tag + "mb", (128, W))
        self.rstdb = C.sb(tag + "rb", (128, W))


def emit_lnT(C, com, xt, keys, W, ln, eps=1e-5):
    n = len(xt)
    Cdim = n * 128
    tag = ln.tag
    sq, ps1, psB = ln.sq, ln.ps1, ln.psB
    rowm, rowr, rowt, meanb, rstdb = ln.rowm, ln.rowr, ln.rowt, ln.meanb, ln.rstdb
    assert W <= ln.W
    for i in range(n):
        C.tstep(lambda e, i=i: e.matmul(ps1[0:1, 0:W], com.ones128[:, 0:1], xt[i],
                                        start=(i == 0), stop=(i == n - 1)),
                reads=[keys[i], "ones128"], writes=[tag + "p1"], name=tag + "s1")
    C.vstep(lambda e: e.tensor_scalar_mul(rowm[0:1, 0:W], ps1[0:1, 0:W], 1.0 / Cdim),
            reads=[tag + "p1"], writes=[tag + "rm"], name=tag + "mean")
    for i in range(n):
        sq_ = sq[i % 2]
        C.sstep(lambda e, i=i, sq_=sq_: e.activation(sq_[:, 0:W], xt[i], AF.Square),
                reads=[keys[i]], writes=[(tag + "sq", i % 2)], name=tag + "sq")
        C.tstep(lambda e, i=i, sq_=sq_: e.matmul(ps1[0:1, 0:W], com.ones128[:, 0:1], sq_[:, 0:W],
                                        start=(i == 0), stop=(i == n - 1)),
                reads=[(tag + "sq", i % 2), "ones128", tag + "rm"], writes=[tag + "p1"], name=tag + "s2")
    C.vstep(lambda e: e.tensor_scalar_mul(rowr[0:1, 0:W], ps1[0:1, 0:W], 1.0 / Cdim),
            reads=[tag + "p1"], writes=[tag + "rr"], name=tag + "msq")
    C.vstep(lambda e: e.tensor_mul(rowt[0:1, 0:W], rowm[0:1, 0:W], rowm[0:1, 0:W]),
            reads=[tag + "rm"], writes=[tag + "rt"], name=tag + "m2")
    C.vstep(lambda e: e.tensor_sub(rowr[0:1, 0:W], rowr[0:1, 0:W], rowt[0:1, 0:W]),
            reads=[tag + "rr", tag + "rt"], writes=[tag + "rr"], name=tag + "var")
    C.sstep(lambda e: e.activation(rowr[0:1, 0:W], rowr[0:1, 0:W], AF.Sqrt, bias=eps),
            reads=[tag + "rr"], writes=[tag + "rr"], name=tag + "sqrt")
    C.vstep(lambda e: e.reciprocal(rowr[0:1, 0:W], rowr[0:1, 0:W]),
            reads=[tag + "rr"], writes=[tag + "rr"], name=tag + "rcp")
    extra_pb = getattr(ln, "pb_alias", [])
    C.tstep(lambda e: e.matmul(psB[:, 0:W], com.ones1[0:1, :], rowm[0:1, 0:W], start=True, stop=True),
            reads=[tag + "rm", "ones1"], writes=[tag + "pB"] + extra_pb, name=tag + "bm")
    C.vstep(lambda e: e.tensor_copy(meanb[:, 0:W], psB[:, 0:W]),
            reads=[tag + "pB"], writes=[tag + "mb"], name=tag + "em")
    C.tstep(lambda e: e.matmul(psB[:, 0:W], com.ones1[0:1, :], rowr[0:1, 0:W], start=True, stop=True),
            reads=[tag + "rr", "ones1"], writes=[tag + "pB"] + extra_pb, name=tag + "br")
    C.vstep(lambda e: e.tensor_copy(rstdb[:, 0:W], psB[:, 0:W]),
            reads=[tag + "pB"], writes=[tag + "rb"], name=tag + "er")
    return meanb, rstdb



def ph_p0(C, com):
    xT = C.din("xT", (D, T), F32)
    xnbf = C.dscratch("xnT_bf", (D, T), BF16)
    W = 512
    xt = [C.sb(f"p0x{i}", (128, W)) for i in range(8)]
    tmp = C.sb("p0t", (128, W))
    xnb = C.sb("p0nb", (128, W), BF16)
    ln = LnScratch(C, W, "p0ln")
    for c in range(T // W):
        keys = []
        for i in range(8):
            k = ("p0x", i)
            C.load(xt[i][:], xT[128 * i:128 * (i + 1), c * W:(c + 1) * W], key=k, name=f"p0ld{i}")
            keys.append(k)
        meanb, rstdb = emit_lnT(C, com, [t[:, 0:W] for t in xt], keys, W, ln)
        for i in range(8):
            C.vstep(lambda e, i=i: e.tensor_sub(tmp[:, 0:W], xt[i][:, 0:W], meanb[:, 0:W]),
                    reads=[("p0x", i), "p0ln" + "mb"], writes=["p0t"], name="p0sub")
            C.vstep(lambda e: e.tensor_mul(xnb[:, 0:W], tmp[:, 0:W], rstdb[:, 0:W]),
                    reads=["p0t", "p0ln" + "rb"], writes=["p0nb"], name="p0mul")
            C.store(xnb[:, 0:W], xnbf[128 * i:128 * (i + 1), c * W:(c + 1) * W],
                    rkeys=["p0nb"], name="p0st")





PHASE_IO = {
    "pw": ([], ["inprojT_bf", "outprojT_bf", "cprojT_bf", "wgqT_bf", "wuqT_bf", "wdqT_bf"]),
    "p0": (["xT"], ["xnT_bf"]),
    "p1": (["xnT_bf"], ["gT", "dT"]),
    "p2": (["xT", "xnT_bf", "gT", "dT", "inprojT_bf", "outprojT_bf"], ["accT"]),
    "p3": (["xnT_bf", "accT", "cprojT_bf"], ["x1T"]),
    "p4": (["x1T"], ["xn1T_bf", "xqT_bf"]),
    "p5": (["xn1T_bf", "x1T"], ["x2aT"]),
    "p6": (["xqT_bf", "x2aT", "wgqT_bf", "wuqT_bf", "wdqT_bf"], ["outT"]),
}



NPDT = {"float32": np.float32, "bfloat16": ml_dtypes.bfloat16}


def build_program(phases, fused):
    nc = bass.Bass("TRN2", target_bir_lowering=False)
    es = ExitStack()
    P = Prog()
    want_in, want_out = set(), set()
    for ph in phases:
        i, o = PHASE_IO[ph]
        want_in |= set(i)
        want_out |= set(o)
    C = Ctx(nc, P, es, fused, want_in, want_out)
    groups = []
    i = 0
    phl = list(phases)
    while i < len(phl):
        if phl[i] == "pw" and i + 1 < len(phl) and phl[i + 1] == "p0":
            groups.append(("pwc",))
            groups.append(("pwq", "p0"))
            i += 2
        else:
            groups.append((phl[i],))
            i += 1
    with es:
        com = Commons(C)
        first = True
        for grp in groups:
            if not first:
                P.barrier()
            first = False
            if len(phl) > 1:
                with C.sub_scope():
                    if len(grp) == 2:
                        s0 = len(P.steps)
                        globals()["ph_" + grp[0]](C, com)
                        s1 = len(P.steps)
                        globals()["ph_" + grp[1]](C, com)
                        s2 = len(P.steps)
                        a, b = P.steps[s0:s1], P.steps[s1:s2]
                        ratio = max(1, len(a) // max(1, len(b)))
                        merged, ai, bi = [], 0, 0
                        while ai < len(a) or bi < len(b):
                            merged.extend(a[ai:ai + ratio])
                            ai += ratio
                            if bi < len(b):
                                merged.append(b[bi])
                                bi += 1
                        P.steps[s0:s2] = merged
                    else:
                        globals()["ph_" + grp[0]](C, com)
            else:
                globals()["ph_" + grp[0]](C, com)
        out_dmas = [s for s in P.steps if s.engine in ("Q", "QG")]
        P.step("QSY", None, deps=out_dmas, name="final_wait")
        P.build_block(nc, es)
    return nc, C


def run_phase_group(phases, env_per_core, fused=False):
    nc, C = build_program(phases, fused)
    in_names = [n for n, k in C.kinds.items() if k == "ExternalInput"]
    out_names = [n for n, k in C.kinds.items() if k == "ExternalOutput"]
    in_maps = [{n: env[n] for n in in_names} for env in env_per_core]
    res = run_bass_kernel_spmd(nc, in_maps, core_ids=list(range(len(env_per_core))))
    for env, r in zip(env_per_core, res.results):
        for n in out_names:
            env[n] = np.asarray(r[n])
    return env_per_core


def load_wbf(C, tag, dram, R, Ccols, queue="Q"):
    nk = R // 128
    f32buf = [C.sb(tag + "f", (128, Ccols)) for _ in range(2)]
    tiles = []
    keys = []
    for k in range(nk):
        fb = f32buf[k % 2]
        fkey = (tag, "f", k % 2)
        C.load(fb[:], dram[128 * k:128 * (k + 1), 0:Ccols], key=fkey, queue=queue, name=tag + "ld")
        bt = C.sb(tag + "b", (128, Ccols), BF16)
        C.vstep(lambda e, fb=fb, bt=bt: e.tensor_copy(bt[:], fb[:]),
                reads=[fkey], writes=[(tag, "b", k)], name=tag + "cast")
        tiles.append(bt)
        keys.append((tag, "b", k))
    return tiles, keys


def load_bias_cols(C, tag, dram, n):
    nt = n // 128
    t = C.sb(tag, (128, nt))
    for j in range(nt):
        C.load(t[:, j:j + 1], dram[128 * j:128 * (j + 1)].unsqueeze(-1), key=(tag, j), name=tag + "ld")
    return t, [(tag, j) for j in range(nt)]


def emit_proj(C, tag, lhsT, lkeys, rhs, rkeys, Cout, W, nchunks, psums, evac_fn,
              chunk_list=None, pskeys=None, rhs_local=False):
    nk = len(lhsT)
    nm = Cout // 128
    if pskeys is None:
        pskeys = [(id(p), "ps") for p in psums]
    it = 0
    for m in range(nm):
        for c in (chunk_list if chunk_list is not None else range(nchunks)):
            ps = psums[it % len(psums)]
            pkey = pskeys[it % len(psums)]
            c_r = 0 if rhs_local else c
            pairs = [(lhsT[k][:, 128 * m:128 * (m + 1)], rhs[k][:, W * c_r:W * (c_r + 1)])
                     for k in range(nk)]
            C.mm(ps[:, 0:W], pairs, rkeys=lkeys + rkeys, wkey=pkey, name=tag + "mm")
            evac_fn(m, c, ps, pkey)
            it += 1



def ph_p1(C, com):
    xnbf = C.dscratch("xnT_bf", (D, T), BF16)
    gT = C.dscratch("gT", (D, T), F32)
    dT = C.dscratch("dT", (D, T), F32)
    gate_w = C.din("gateT", (D, D), F32)
    fdw1 = C.din("fdw1T", (D, D), F32)
    fdw2 = C.din("fdw2T", (D, D), F32)
    gate_b = C.din("gate_b", (D,), F32)
    fd_b1 = C.din("fd_b1", (D,), F32)
    fd_ln_g = C.din("fd_ln_g", (D,), F32)
    fd_ln_b = C.din("fd_ln_b", (D,), F32)
    fd_b2 = C.din("fd_b2", (D,), F32)

    xn = []
    xkeys = []
    for k in range(8):
        t = C.sb("p1xn", (128, T), BF16)
        C.load(t[:], xnbf[128 * k:128 * (k + 1), :], key=("p1xn", k), name="p1xnld")
        xn.append(t)
        xkeys.append(("p1xn", k))
    wg, wgk = load_wbf(C, "p1wg", gate_w, D, D)
    w1, w1k = load_wbf(C, "p1w1", fdw1, D, D)
    w2, w2k = load_wbf(C, "p1w2", fdw2, D, D)
    gb, _ = load_bias_cols(C, "p1gb", gate_b, D)
    b1, _ = load_bias_cols(C, "p1b1", fd_b1, D)
    lng, _ = load_bias_cols(C, "p1lng", fd_ln_g, D)
    lnb, _ = load_bias_cols(C, "p1lnb", fd_ln_b, D)
    b2, _ = load_bias_cols(C, "p1b2", fd_b2, D)

    W = 512
    NCH = T // W
    psums = [C.psum("p1ps", (128, W)) for _ in range(2)]
    gtmp = C.sb("p1gt", (128, W))

    def gate_evac(m, c, ps, pkey):
        C.sstep(lambda e: e.activation(gtmp[:, 0:W], ps[:, 0:W], AF.Sigmoid, bias=gb[:, m:m + 1]),
                reads=[pkey, ("p1gb", m)], writes=["p1gt"], name="p1gs")
        C.store(gtmp[:, 0:W], gT[128 * m:128 * (m + 1), W * c:W * (c + 1)],
                rkeys=["p1gt"], name="p1gst")
    emit_proj(C, "p1g", wg, wgk, xn, xkeys, D, W, NCH, psums, gate_evac)

    d1 = [C.sb("p1d1", (128, W)) for _ in range(8)]
    dlb = [C.sb("p1dl", (128, W), BF16) for _ in range(8)]
    ln = LnScratch(C, W, "p1ln")
    dtmp = C.sb("p1dt", (128, W))
    for c in range(NCH):
        def d1_evac(m, c2, ps, pkey, c0=c):
            C.sstep(lambda e, m=m, ps=ps: e.add(d1[m][:, 0:W], ps[:, 0:W], b1[:, m:m + 1]),
                    reads=[pkey, ("p1b1", m)], writes=[("p1d1", m)], name="p1d1e")
        emit_proj(C, "p1a", w1, w1k, xn, xkeys, D, W, NCH, psums, d1_evac, chunk_list=[c])
        meanb, rstdb = emit_lnT(C, com, [t[:, 0:W] for t in d1],
                                [("p1d1", m) for m in range(8)], W, ln)
        for m in range(8):
            C.vstep(lambda e, m=m: e.tensor_sub(dtmp[:, 0:W], d1[m][:, 0:W], meanb[:, 0:W]),
                    reads=[("p1d1", m), "p1lnmb"], writes=["p1dt"], name="p1sub")
            C.vstep(lambda e: e.tensor_mul(dtmp[:, 0:W], dtmp[:, 0:W], rstdb[:, 0:W]),
                    reads=["p1dt", "p1lnrb"], writes=["p1dt"], name="p1mul")
            C.vstep(lambda e, m=m: e.tensor_scalar(dtmp[:, 0:W], dtmp[:, 0:W],
                                                   lng[:, m:m + 1], lnb[:, m:m + 1], ALU.mult, ALU.add),
                    reads=["p1dt", ("p1lng", m), ("p1lnb", m)], writes=["p1dt"], name="p1aff")
            C.vstep(lambda e, m=m: e.tensor_scalar_max(dlb[m][:, 0:W], dtmp[:, 0:W], 0.0),
                    reads=["p1dt"], writes=[("p1dl", m)], name="p1relu")
        def d2_evac(m, c2, ps, pkey, c0=c):
            C.sstep(lambda e, m=m, ps=ps: e.add(dtmp[:, 0:W], ps[:, 0:W], b2[:, m:m + 1]),
                    reads=[pkey, ("p1b2", m)], writes=["p1dt"], name="p1d2e")
            C.store(dtmp[:, 0:W], dT[128 * m:128 * (m + 1), W * c0:W * (c0 + 1)],
                    rkeys=["p1dt"], name="p1dst")
        emit_proj(C, "p1b", w2, w2k, dlb, [("p1dl", m) for m in range(8)], D, W, NCH,
                  psums, d2_evac, chunk_list=[c], rhs_local=True)



def quant_weight(C, com, wn, R, Cc, qs):
    nat = C.din(wn + "_nat", (R, Cc), F32)
    wT = C.din(wn + "T", (Cc, R), F32)
    out = C.dscratch(wn + "qT_bf", (Cc, R), BF16)
    nb = Cc // 16
    ntn = R // 128
    ntT = Cc // 128
    (natf, sclbuf, invbuf, sclTA, sclTB, invTA, invTB, scl8, inv8,
     sclrep, invrep, wtile, qt, qr, qa, qL, qsg, pstr, psE) = qs
    e8 = qs_e8(C)
    for r in range(ntn):
        C.load(natf[:, 0:Cc], nat[128 * r:128 * (r + 1), :], key=("qwSS", 0), name="qwldn")
        emit_scales(C, natf[:, 0:Cc], sclbuf[:, r * nb:(r + 1) * nb],
                    invbuf[:, r * nb:(r + 1) * nb], 128, Cc, f"qs{wn}{r}",
                    extra_reads=[("qwSS", 0)], ckey=("qw_scl", r), ikey=("qw_inv", r))
    for r in range(ntn):
        for half, (b0, bw) in enumerate([(0, min(nb, 128))] + ([(128, nb - 128)] if nb > 128 else [])):
            dstS = sclTA if half == 0 else sclTB
            dstI = invTA if half == 0 else invTB
            for buf, dst, kk in ((sclbuf, dstS, "qw_scl"), (invbuf, dstI, "qw_inv")):
                C.tstep(lambda e, r=r, b0=b0, bw=bw, buf=buf: e.transpose(
                            pstr[0:bw, 0:128], buf[:, r * nb + b0:r * nb + b0 + bw], com.ident[:]),
                        reads=[(kk, r), "ident"], writes=["qw_pstr"], name="qwtr")
                C.vstep(lambda e, r=r, bw=bw, dst=dst: e.tensor_copy(
                            dst[0:bw, 128 * r:128 * (r + 1)], pstr[0:bw, 0:128]),
                        reads=["qw_pstr"], writes=[("qw_sclT", kk, half)], name="qwtre")
    esel_a, esel_b = qs_esel(C)
    for j in range(ntT):
        jp = j % 2
        if 8 * (j + 1) <= 128:
            esl = esel_a[0:(64 if ntn == 22 else 128), 128 * j:128 * (j + 1)]
            srcS, srcI = sclTA, invTA
            rows = 64 if ntn == 22 else 128
            skey2, ikey2 = ("qw_sclT", "qw_scl", 0), ("qw_sclT", "qw_inv", 0)
        else:
            esl = esel_b[0:48, 128 * (j - 16):128 * (j - 15)]
            srcS, srcI = sclTB, invTB
            rows = 48
            skey2, ikey2 = ("qw_sclT", "qw_scl", 1), ("qw_sclT", "qw_inv", 1)
        nchunk = ceil_div(R, 512)
        psE_ = psE[jp]
        for cc in range(nchunk):
            w = min(512, R - cc * 512)
            C.tstep(lambda e, cc=cc, w=w, jp=jp, psE_=psE_, esl=esl, srcS=srcS, rows=rows: e.matmul(
                        psE_[:, 0:w], esl, srcS[0:rows, cc * 512:cc * 512 + w],
                        start=True, stop=True),
                    reads=[skey2, "Esel"], writes=[("qw_psE", jp)], name="qwrep")
            C.vstep(lambda e, cc=cc, w=w, jp=jp, psE_=psE_: e.tensor_copy(
                        sclrep[jp][:, cc * 512:cc * 512 + w], psE_[:, 0:w]),
                    reads=[("qw_psE", jp)], writes=[("qw_srep", jp)], name="qwrepe")
            C.tstep(lambda e, cc=cc, w=w, jp=jp, psE_=psE_, esl=esl, srcI=srcI, rows=rows: e.matmul(
                        psE_[:, 0:w], esl, srcI[0:rows, cc * 512:cc * 512 + w],
                        start=True, stop=True),
                    reads=[ikey2, "Esel"], writes=[("qw_psE", jp)], name="qwrep")
            C.vstep(lambda e, cc=cc, w=w, jp=jp, psE_=psE_: e.tensor_copy(
                        invrep[jp][:, cc * 512:cc * 512 + w], psE_[:, 0:w]),
                    reads=[("qw_psE", jp)], writes=[("qw_irep", jp)], name="qwrepe")
        jp = j % 2
        wt_, qt_ = wtile[jp], qt[jp]
        C.load(wt_[:, 0:R], wT[128 * j:128 * (j + 1), :], key=("qwSS", jp), name="qwldT")
        HB = R // 2
        C.P.step("V", None, reads=["qw_qtb"], name="qtgate") if False else None
        C.P.step("V", lambda e: e.engine_nop(), reads=[], writes=["qw_qtb"], name="qtgate")
        for hf in range(2):
            emit_qdq(C, wt_[:, hf * HB:hf * HB + HB], qt_[:, hf * HB:hf * HB + HB],
                     fview(invrep[jp][:, hf * HB:hf * HB + HB], 128, HB),
                     fview(sclrep[jp][:, hf * HB:hf * HB + HB], 128, HB),
                     qr[jp], qa[jp], qL[jp], qsg[jp], 128, HB, f"qw_q{jp}",
                     skey=("qwSS", jp), ikey=("qw_irep", jp), ckey=("qw_srep", jp))
        C.store(qt_[:, 0:R], out[128 * j:128 * (j + 1), :], rkeys=[f"qw_q{jp}D", "qw_qtb"], name="qwst")


def qs_esel(C):
    if not hasattr(C, "_esel"):
        ta = C.sb("EselAsb", (128, 16 * 128))
        C.load(ta[:], C.din("EselA", (128, 16 * 128), F32), key="Esel", name="eselld")
        tb = C.sb("EselBsb", (48, 6 * 128))
        C.load(tb[:], C.din("EselB", (48, 6 * 128), F32), key="Esel", name="eselld")
        C._esel = (ta, tb)
    return C._esel


def qs_e8(C):
    if not hasattr(C, "_e8"):
        t = C.sb("E8sb", (8, 128))
        C.load(t[:], C.din("E8", (8, 128), F32), key="E8", name="e8ld")
        C._e8 = t
    return C._e8


def ph_pw(C, com):
    ph_pwc(C, com)
    ph_pwq(C, com)


def ph_pwc(C, com):
    with C.sub_scope():
        castf = [C.sb("pwcf", (128, 4096)) for _ in range(4)]
        castb = [C.sb("pwcb", (128, 4096), BF16) for _ in range(4)]
        for name, R, Cc in [("inprojT", 1024, 4096), ("outprojT", 2048, 1024),
                            ("cprojT", 16384, 1024)]:
            src = C.din(name, (R, Cc), F32)
            dst = C.dscratch(name + "_bf", (R, Cc), BF16)
            for k in range(R // 128):
                i = k % 4
                C.load(castf[i][:, 0:Cc], src[128 * k:128 * (k + 1), :], key=("pwf", i), name="pwld")
                est = C.vstep if k % 2 == 0 else C.sstep
                fn = (lambda e, i=i, Cc=Cc: e.tensor_copy(castb[i][:, 0:Cc], castf[i][:, 0:Cc]))                     if k % 2 == 0 else                     (lambda e, i=i, Cc=Cc: e.activation(castb[i][:, 0:Cc], castf[i][:, 0:Cc], AF.Copy))
                est(fn, reads=[("pwf", i)], writes=[("pwb", i)], name="pwc")
                C.store(castb[i][:, 0:Cc], dst[128 * k:128 * (k + 1), :],
                        rkeys=[("pwb", i)], queue="QG", name="pwst")


def ph_pwq(C, com):
    _qwt = [C.sb("q_wt", (128, 2816)) for _ in range(2)]
    qs = (_qwt[0], C.sb("q_scl", (128, 1408)), C.sb("q_inv", (128, 1408)),
          C.sb("q_sTA", (128, 2816)), C.sb("q_sTB", (48, 1024)),
          C.sb("q_iTA", (128, 2816)), C.sb("q_iTB", (48, 1024)),
          None, None,
          [C.sb("q_srep", (128, 2816)) for _ in range(2)], [C.sb("q_irep", (128, 2816)) for _ in range(2)],
          _qwt, [C.sb("q_qt", (128, 2816), BF16)] * 2,
          [C.sb("q_r", (128, 1408)) for _ in range(2)], [C.sb("q_a", (128, 1408)) for _ in range(2)],
          [C.sb("q_L", (128, 1408)) for _ in range(2)], [C.sb("q_sg", (128, 1408)) for _ in range(2)],
          C.psum("q_pstr", (128, 128)), [C.psum("q_psE0", (128, 512)), C.psum("q_psE1", (128, 512))])
    quant_weight(C, com, "wg", 2816, 1024, qs)
    C.P.barrier()
    quant_weight(C, com, "wu", 2816, 1024, qs)
    C.P.barrier()
    quant_weight(C, com, "wd", 1024, 2816, qs)



def ph_p2(C, com):
    xT = C.din("xT", (D, T), F32)
    xnbf = C.dscratch("xnT_bf", (D, T), BF16)
    gT = C.dscratch("gT", (D, T), F32)
    dT = C.dscratch("dT", (D, T), F32)
    accT = C.dscratch("accT", (D, T), F32)
    ipj_d = C.dscratch("inprojT_bf", (D, 2 * DI), BF16)
    opj_d = C.dscratch("outprojT_bf", (DI, D), BF16)
    xpd_d = C.din("xprojT_dt", (DI, DTR), F32)
    xpbc_d = C.din("xprojT_BC", (DI, 32), F32)
    dtp_d = C.din("dtprojT", (DTR, DI), F32)
    cw_d = C.din("convw", (DI, 4), F32)
    sel_d = C.din("sel32", (32, 32 * 128), F32)
    A_d = C.din("A_log", (DI, DS), F32)
    bal_d = C.din("balance", (1,), F32)

    import os
    Lc = 256
    NCH = T // Lc
    if os.environ.get('P2_NCH'):
        NCH = int(os.environ['P2_NCH'])
    STOP = int(os.environ.get('P2_STOP', '10'))
    NDT = DI // 128

    xpd, xpdk = load_wbf(C, "p2xpd", xpd_d, DI, DTR)
    xpbc, xpbck = load_wbf(C, "p2xpbc", xpbc_d, DI, 32)
    dtp = C.sb("p2dtp", (DTR, DI), BF16)
    with C.sub_scope():
        dtpf = C.sb("p2dtpf", (DTR, DI))
        C.load(dtpf[:], dtp_d, key="p2dtpf", name="p2ldtp")
        C.vstep(lambda e: e.tensor_copy(dtp[:], dtpf[:]), reads=["p2dtpf"], writes=["p2dtp"], name="p2cdtp")
    cw = C.sb("p2cw", (128, 4 * NDT))
    for dt in range(NDT):
        C.load(cw[:, 4 * dt:4 * dt + 4], cw_d[128 * dt:128 * (dt + 1), :], key=("p2cw", dt), name="p2lcw")
    convb, _ = load_bias_cols(C, "p2cb", C.din("conv_b", (DI,), F32), DI)
    dtb, _ = load_bias_cols(C, "p2dtb", C.din("dt_b", (DI,), F32), DI)
    Dpc, _ = load_bias_cols(C, "p2Dp", C.din("Dp", (DI,), F32), DI)
    ng, _ = load_bias_cols(C, "p2ng", C.din("norm_g", (D,), F32), D)
    nb_, _ = load_bias_cols(C, "p2nb", C.din("norm_b", (D,), F32), D)
    Ab = C.sb("p2A", (128, DS * NDT))
    for dt in range(NDT):
        C.load(Ab[:, DS * dt:DS * (dt + 1)], A_d[128 * dt:128 * (dt + 1), :], key=("p2A", dt), name="p2lA")
    C.sstep(lambda e: e.activation(Ab[:, :], Ab[:, :], AF.Exp),
            reads=[("p2A", dt) for dt in range(NDT)], writes=["p2Ae"], name="p2expA")
    C.vstep(lambda e: e.tensor_scalar_mul(Ab[:, :], Ab[:, :], -1.0),
            reads=["p2Ae"], writes=["p2Ae"], name="p2negA")
    ssel = C.sb("p2sel", (32, 32 * 128), BF16)
    with C.sub_scope():
        sself = C.sb("p2self", (32, 32 * 128))
        C.load(sself[:], sel_d, key="p2self", name="p2lsel")
        C.vstep(lambda e: e.tensor_copy(ssel[:], sself[:]), reads=["p2self"], writes=["p2sel"], name="p2csel")
    balv = C.sb("p2bal", (1, 1))
    C.load(balv[:], bal_d.unsqueeze(0), key="p2bal", name="p2lbal")
    C.sstep(lambda e: e.activation(balv[:], balv[:], AF.Sigmoid), reads=["p2bal"], writes=["p2bal2"], name="p2sbal")
    fcol = C.sb("p2f", (128, 1))
    fmcol = C.sb("p2fm", (128, 1))

    ipjbig = C.sb("p2ipjB", (128, 8 * 1024), BF16)
    opj = [C.sb("p2opj", (128, 2 * D), BF16) for _ in range(2)]
    xnbig = C.sb("p2xnc", (128, 8 * Lc), BF16)
    xnc = [xnbig[:, Lc * k:Lc * (k + 1)] for k in range(8)]
    xi = [C.sb("p2xi", (128, Lc + 3)) for _ in range(NDT)]
    xc = [C.sb("p2xc", (128, Lc), BF16) for _ in range(NDT)]
    svz = [C.sb("p2svz", (128, Lc), BF16) for _ in range(NDT)]
    delta = [C.sb("p2del", (128, Lc)) for _ in range(NDT)]
    y3bf = [C.sb("p2y3", (128, Lc), BF16) for _ in range(NDT)]
    LcR = Lc + 1
    h_all2 = [C.sb("p2hall", (128, DS * LcR), BF16) for _ in range(2)]
    a_all2 = [C.sb("p2aall", (128, DS * LcR), BF16) for _ in range(2)]
    Ball = C.sb("p2Ball", (128, DS * Lc), BF16)
    Call = C.sb("p2Call", (128, DS * Lc), BF16)
    hstate = C.sb("p2hst", (128, DS * NDT))
    du_t = [C.sb("p2du", (128, Lc)) for _ in range(2)]
    dub_t = [C.sb("p2dub", (128, Lc), BF16) for _ in range(2)]

    dtTbf = C.sb("p2dtT", (DTR, Lc), BF16)
    bc_sb = C.sb("p2bc", (32, Lc), BF16)
    raw = [C.sb("p2raw", (128, Lc)) for _ in range(2)]
    dt1 = [C.sb("p2dt1", (128, Lc)) for _ in range(2)]
    dt2 = [C.sb("p2dt2", (128, Lc)) for _ in range(2)]
    tmp1 = C.sb("p2t1", (128, Lc))
    ctmp1 = [C.sb("p2ct1", (128, Lc)) for _ in range(2)]
    ctmp2 = [C.sb("p2ct2", (128, Lc)) for _ in range(2)]

    comb = [C.sb("p2comb", (128, Lc)) for _ in range(8)]
    gmt = [C.sb("p2gm", (128, Lc)) for _ in range(2)]
    dmt = [C.sb("p2dm", (128, Lc)) for _ in range(2)]
    xmt = [C.sb("p2xm", (128, Lc)) for _ in range(2)]
    ln = LnScratch(C, Lc, "p2ln")
    ln.pb_alias = [("p2psO", 3)]
    psA = C.psum("p2psA", (128, Lc))
    psA2 = C.psum("p2psA2", (128, Lc))
    psX = C.psum("p2psX", (128, 2 * Lc))
    psO = [C.psum(f"p2psO{i}", (128, Lc)) for i in range(3)]
    psO.append(ln.psB)
    psums = [psA, psA2]
    pskeys = ["p2psA", "p2psA2"]
    C.tstep(lambda e: e.matmul(psX[:, 0:1], com.ones1[0:1, :], balv[0:1, 0:1], start=True, stop=True),
            reads=["p2bal2", "ones1"], writes=["p2psX"], name="p2bf")
    C.vstep(lambda e: e.tensor_copy(fcol[:], psX[:, 0:1]), reads=["p2psX"], writes=["p2f"], name="p2ef")
    C.vstep(lambda e: e.tensor_scalar(fmcol[:], fcol[:], -1.0, 1.0, ALU.mult, ALU.add),
            reads=["p2f"], writes=["p2fm"], name="p2fmc")

    C.vstep(lambda e: e.memset(hstate[:], 0.0), writes=["p2hst"], name="p2hs0")
    for par in range(2):
        C.vstep(lambda e, par=par: e.memset(a_all2[par][:, 0:DS * LcR:LcR], 0.0),
                writes=[("p2arst", par)], name="p2arst")
    for dt in range(NDT):
        C.vstep(lambda e, dt=dt: e.memset(xi[dt][:, 0:3], 0.0), writes=[("p2xih", dt)], name="p2xih0")

    for c in range(NCH):
        C.load(xnbig[:, :], xnbf.rearrange("(k p) t -> p k t", p=128)[:, :, Lc * c:Lc * (c + 1)],
               key="p2xnB", name="p2lxn")
        it = 0
        for q in range(2):
            C.load(ipjbig[:, :],
                   ipj_d.rearrange("(k p) n -> p k n", p=128)[:, :, q * 1024:(q + 1) * 1024],
                   key="p2ipjB", queue=("Q" if q == 0 else "QG"), name="p2lip")
            for mi8 in range(8):
                mi = q * 8 + mi8
                ps, pk = psums[it % len(psums)], pskeys[it % len(psums)]
                pairs = [(ipjbig[:, 1024 * k + 128 * mi8:1024 * k + 128 * (mi8 + 1)],
                          xnc[k][:, 0:Lc]) for k in range(8)]
                C.mm(ps[:, 0:Lc], pairs, rkeys=["p2ipjB", "p2xnB"], wkey=pk, name="p2ipmm")
                C.sstep(lambda e, mi=mi, ps=ps: e.activation(xi[mi][:, 3:3 + Lc], ps[:, 0:Lc], AF.Copy),
                        reads=[pk], writes=[("p2xi", mi)], name="p2ipe")
                it += 1
        for dt in range(NDT):
            cva = dt1[dt % 2]
            C.vstep(lambda e, dt=dt, cva=cva: e.tensor_scalar_mul(cva[:, 0:Lc], xi[dt][:, 0:Lc],
                                                                  cw[:, 4 * dt:4 * dt + 1]),
                    reads=[("p2xi", dt), ("p2xih", dt), ("p2cw", dt)], writes=[("p2dt1", dt % 2)], name="p2cv0")
            for j in (1, 2, 3):
                C.vstep(lambda e, dt=dt, j=j, cva=cva: e.scalar_tensor_tensor(
                            cva[:, 0:Lc], xi[dt][:, j:j + Lc], cw[:, 4 * dt + j:4 * dt + j + 1],
                            cva[:, 0:Lc], ALU.mult, ALU.add),
                        reads=[("p2xi", dt), ("p2dt1", dt % 2), ("p2cw", dt)], writes=[("p2dt1", dt % 2)], name="p2cvj")
            C.sstep(lambda e, dt=dt, cva=cva: e.activation(xc[dt][:, 0:Lc], cva[:, 0:Lc], AF.Silu,
                                                  bias=convb[:, dt:dt + 1]),
                    reads=[("p2dt1", dt % 2), ("p2cb", dt)], writes=[("p2xc", dt)], name="p2silu")
            C.vstep(lambda e, dt=dt: e.tensor_copy(xi[dt][:, 0:3], xi[dt][:, Lc:Lc + 3]),
                    reads=[("p2xi", dt)], writes=[("p2xih", dt)], name="p2hist")
        if STOP < 3:
            C.store(xc[0][:, 0:Lc], accT[0:128, Lc * c:Lc * (c + 1)], rkeys=[("p2xc", 0)], name="p2st")
            continue
        for q in range(2):
            C.load(ipjbig[:, :],
                   ipj_d.rearrange("(k p) n -> p k n", p=128)[:, :, DI + q * 1024:DI + (q + 1) * 1024],
                   key="p2ipjB", queue=("Q" if q == 0 else "QG"), name="p2lipz")
            for mi8 in range(8):
                mi = q * 8 + mi8
                ps, pk = psums[it % len(psums)], pskeys[it % len(psums)]
                pairs = [(ipjbig[:, 1024 * k + 128 * mi8:1024 * k + 128 * (mi8 + 1)],
                          xnc[k][:, 0:Lc]) for k in range(8)]
                C.mm(ps[:, 0:Lc], pairs, rkeys=["p2ipjB", "p2xnB"], wkey=pk, name="p2zmm")
                C.sstep(lambda e, mi=mi, ps=ps: e.activation(svz[mi][:, 0:Lc], ps[:, 0:Lc], AF.Silu),
                        reads=[pk], writes=[("p2svz", mi)], name="p2ze")
                it += 1
        if STOP < 4:
            C.store(xc[0][:, 0:Lc], accT[0:128, Lc * c:Lc * (c + 1)], rkeys=[("p2xc", 0)], name="p2st")
            continue
        xck = [("p2xc", k) for k in range(NDT)]
        C.mm(psX[0:DTR, 0:Lc], [(xpd[k][:, 0:DTR], xc[k][:, 0:Lc]) for k in range(NDT)],
             rkeys=xck + xpdk, wkey="p2psX", name="p2xdmm")
        C.vstep(lambda e: e.tensor_copy(dtTbf[:, 0:Lc], psX[0:DTR, 0:Lc]),
                reads=["p2psX"], writes=["p2dtT"], name="p2xde")
        C.mm(psX[0:32, Lc:2 * Lc], [(xpbc[k][:, 0:32], xc[k][:, 0:Lc]) for k in range(NDT)],
             rkeys=xck + xpbck, wkey="p2psX", name="p2bcmm")
        C.vstep(lambda e: e.tensor_copy(bc_sb[:, 0:Lc], psX[0:32, Lc:2 * Lc]),
                reads=["p2psX"], writes=["p2bc"], name="p2bce")
        if STOP < 5:
            C.store(xc[0][:, 0:Lc], accT[0:128, Lc * c:Lc * (c + 1)], rkeys=[("p2xc", 0)], name="p2st")
            continue
        def d_sum(dm):
            par = dm % 2
            t1_, t2_ = dt1[par], dt2[par]
            C.vstep(lambda e, dm=dm, t1_=t1_, t2_=t2_: e.tensor_add(delta[dm][:, 0:Lc], t1_[:, 0:Lc], t2_[:, 0:Lc]),
                    reads=[("p2dt1", par), ("p2dt2", par)], writes=[("p2del", dm)], name="p2dsum")
        for dm in range(NDT):
            par = dm % 2
            ps_d = psums[par]
            pk_d = pskeys[par]
            rw, t1_, t2_ = raw[par], dt1[par], dt2[par]
            C.tstep(lambda e, dm=dm, ps_d=ps_d: e.matmul(ps_d[:, 0:Lc], dtp[:, 128 * dm:128 * (dm + 1)],
                                              dtTbf[:, 0:Lc], start=True, stop=True),
                    reads=["p2dtT", "p2dtp"], writes=[pk_d], name="p2dmm")
            C.sstep(lambda e, dm=dm, rw=rw, ps_d=ps_d: e.add(rw[:, 0:Lc], ps_d[:, 0:Lc], dtb[:, dm:dm + 1]),
                    reads=[pk_d, ("p2dtb", dm)], writes=[("p2raw", par)], name="p2draw")
            C.vstep(lambda e, rw=rw, t1_=t1_: e.tensor_scalar_min(t1_[:, 0:Lc], rw[:, 0:Lc], 30.0),
                    reads=[("p2raw", par)], writes=[("p2dt1", par)], name="p2dmin")
            C.vstep(lambda e, rw=rw, t2_=t2_: e.tensor_scalar(t2_[:, 0:Lc], rw[:, 0:Lc], -30.0, 0.0, ALU.add, ALU.max),
                    reads=[("p2raw", par)], writes=[("p2dt2", par)], name="p2dtail")
            C.sstep(lambda e, t1_=t1_: e.activation(t1_[:, 0:Lc], t1_[:, 0:Lc], AF.Exp),
                    reads=[("p2dt1", par)], writes=[("p2dt1", par)], name="p2dexp")
            C.sstep(lambda e, t1_=t1_: e.activation(t1_[:, 0:Lc], t1_[:, 0:Lc], AF.Ln, bias=1.0),
                    reads=[("p2dt1", par)], writes=[("p2dt1", par)], name="p2dln")
            if dm >= 1:
                d_sum(dm - 1)
        d_sum(NDT - 1)
        if STOP < 6:
            C.store(delta[0][:, 0:Lc], accT[0:128, Lc * c:Lc * (c + 1)], rkeys=[("p2del", 0)], name="p2st")
            continue
        bci = 0
        for s in range(DS):
            for which, dstb, selc in ((0, Ball, s), (1, Call, 16 + s)):
                ps_b, pk_b = psums[bci % 2], pskeys[bci % 2]
                bci += 1
                C.tstep(lambda e, selc=selc, ps_b=ps_b: e.matmul(
                            ps_b[:, 0:Lc], ssel[:, 128 * selc:128 * (selc + 1)],
                            bc_sb[0:32, 0:Lc], start=True, stop=True),
                        reads=["p2bc", "p2sel"], writes=[pk_b], name="p2Bmm")
                C.sstep(lambda e, s=s, dstb=dstb, ps_b=ps_b: e.activation(
                            dstb[:, Lc * s:Lc * (s + 1)], ps_b[:, 0:Lc], AF.Copy),
                        reads=[pk_b], writes=["p2Ball" if dstb is Ball else "p2Call"], name="p2Be")
        if STOP < 7:
            C.store(delta[0][:, 0:Lc], accT[0:128, Lc * c:Lc * (c + 1)], rkeys=[("p2del", 0)], name="p2st")
            continue
        def scan_pre(dt):
            par = dt % 2
            du_, dub_, h_all = du_t[par], dub_t[par], h_all2[par]
            C.vstep(lambda e, dt=dt, du_=du_: e.tensor_mul(du_[:, 0:Lc], delta[dt][:, 0:Lc], xc[dt][:, 0:Lc]),
                    reads=[("p2del", dt), ("p2xc", dt)], writes=[("p2du", par)], name="p2dut")
            C.vstep(lambda e, du_=du_, dub_=dub_: e.tensor_copy(dub_[:, 0:Lc], du_[:, 0:Lc]),
                    reads=[("p2du", par)], writes=[("p2dub", par)], name="p2dub")
            dub3 = dub_[:, 0:Lc].unsqueeze(1).broadcast_to([128, DS, Lc])
            hsl = h_all[:, 0:DS * LcR].rearrange("p (s u) -> p s u", u=LcR)[:, :, 1:LcR]
            C.vstep(lambda e, dub3=dub3, hsl=hsl: e.tensor_tensor(
                        hsl, dub3,
                        Ball[:, :].rearrange("p (s t) -> p s t", s=DS), ALU.mult),
                    reads=[("p2dub", par), "p2Ball"] + [("p2h", par, s) for s in range(DS)],
                    writes=[("p2h", par, s) for s in range(DS)] + [("p2hm", par)], name="p2bmul")
            C.vstep(lambda e, dt=dt, h_all=h_all: e.tensor_copy(
                        h_all[:, 0:DS * LcR:LcR], hstate[:, DS * dt:DS * (dt + 1)]),
                    reads=["p2hst"], writes=[("p2h", par, "rst")], name="p2seed")

        scan_pre(0)
        for dt in range(NDT):
            par = dt % 2
            du_, dub_, h_all = du_t[par], dub_t[par], h_all2[par]
            t1_ = ctmp1[par]
            if dt + 1 < NDT:
                scan_pre(dt + 1)
            a_all = a_all2[par]
            for s in range(DS):
                C.sstep(lambda e, dt=dt, s=s, a_all=a_all: e.activation(
                            a_all[:, LcR * s + 1:LcR * (s + 1)], delta[dt][:, 0:Lc], AF.Exp,
                            scale=Ab[:, DS * dt + s:DS * dt + s + 1]),
                        reads=[("p2del", dt), "p2Ae", ("p2arst", par)],
                        writes=[("p2a", par)], name="p2aexp")
            C.vstep(lambda e, h_all=h_all, a_all=a_all: e.tensor_tensor_scan(
                        h_all[:, 0:DS * LcR], a_all[:, 0:DS * LcR],
                        h_all[:, 0:DS * LcR], 0.0, ALU.mult, ALU.add),
                    reads=[("p2a", par), ("p2arst", par), ("p2h", par, "rst")] +
                          [("p2h", par, s) for s in range(DS)],
                    writes=[("p2h", par, s) for s in range(DS)] + [("p2hm", par)], name="p2scan")
            C.vstep(lambda e, dt=dt, h_all=h_all: e.tensor_copy(
                        hstate[:, DS * dt:DS * (dt + 1)],
                        h_all[:, LcR - 1:DS * LcR:LcR]),
                    reads=[("p2h", par, s) for s in range(DS)], writes=["p2hst"], name="p2hupd")
            hsl2 = h_all[:, 0:DS * LcR].rearrange("p (s u) -> p s u", u=LcR)[:, :, 1:LcR]
            C.vstep(lambda e, hsl2=hsl2: e.tensor_tensor(hsl2, hsl2,
                        Call[:, :].rearrange("p (s t) -> p s t", s=DS), ALU.mult),
                    reads=[("p2h", par, s) for s in range(DS)] + ["p2Call", "p2hst"],
                    writes=[("p2hm", par)] + [("p2h", par, s) for s in range(DS)], name="p2ymul")
            C.vstep(lambda e, du_=du_, h_all=h_all: e.tensor_reduce(du_[:, 0:Lc],
                        h_all[:, 0:DS * LcR].rearrange("p (s u) -> p u s", u=LcR)[:, 1:LcR, :],
                        AX.X, ALU.add),
                    reads=[("p2hm", par)], writes=[("p2du", par)], name="p2yred")
            C.vstep(lambda e, dt=dt, du_=du_, t1_=t1_: e.scalar_tensor_tensor(
                        t1_[:, 0:Lc], xc[dt][:, 0:Lc], Dpc[:, dt:dt + 1], du_[:, 0:Lc],
                        ALU.mult, ALU.add),
                    reads=[("p2xc", dt), ("p2Dp", dt), ("p2du", par)], writes=[("p2ct1", par)], name="p2yD")
            C.vstep(lambda e, dt=dt, t1_=t1_: e.tensor_mul(y3bf[dt][:, 0:Lc], t1_[:, 0:Lc], svz[dt][:, 0:Lc]),
                    reads=[("p2ct1", par), ("p2svz", dt)], writes=[("p2y3", dt)], name="p2y3")
        if STOP < 8:
            C.vstep(lambda e: e.tensor_copy(tmp1[:, 0:Lc], y3bf[0][:, 0:Lc]),
                    reads=[("p2y3", 0)], writes=["p2t1"], name="p2dbgc")
            C.store(tmp1[:, 0:Lc], accT[0:128, Lc * c:Lc * (c + 1)], rkeys=["p2t1"], name="p2st")
            continue
        y3k = [("p2y3", k) for k in range(NDT)]
        for half in range(2):
            for k4 in range(NDT // 2):
                kk = k4 % 2
                C.load(opj[kk][:, :],
                       opj_d.rearrange("(g p) n -> p g n", p=128)[:, 2 * k4:2 * (k4 + 1), :],
                       key=("p2opj", kk), queue=("Q" if k4 % 2 == 0 else "QG"), name="p2lop")
                for ks in range(2):
                    k = 2 * k4 + ks
                    for mm4 in range(4):
                        m = 4 * half + mm4
                        po = psO[mm4]
                        wk = [("p2psO", mm4)] + (["p2lnpB"] if mm4 == 3 else [])
                        C.tstep(lambda e, m=m, k=k, ks=ks, kk=kk, po=po: e.matmul(
                                    po[:, 0:Lc], opj[kk][:, D * ks + 128 * m:D * ks + 128 * (m + 1)],
                                    y3bf[k][:, 0:Lc], start=(k == 0), stop=(k == NDT - 1),
                                    skip_group_check=True),
                                reads=[("p2opj", kk), y3k[k]], writes=wk, name="p2omm")
            for mm4 in range(4):
                m = 4 * half + mm4
                C.vstep(lambda e, mm4=mm4, m=m: e.tensor_copy(comb[m][:, 0:Lc], psO[mm4][:, 0:Lc]),
                        reads=[("p2psO", mm4)], writes=[("p2mps", m)], name="p2oev")
        if STOP < 9:
            C.store(comb[0][:, 0:Lc], accT[0:128, Lc * c:Lc * (c + 1)], rkeys=[("p2mps", 0)], name="p2st")
            continue
        for m in range(8):
            par = m % 2
            t1_, t2_ = ctmp1[par], ctmp2[par]
            gm_, dm_ = gmt[par], dmt[par]
            C.load(gm_[:], gT[128 * m:128 * (m + 1), Lc * c:Lc * (c + 1)], key=("p2gm", par),
                   queue="QG", name="p2lg")
            C.load(dm_[:], dT[128 * m:128 * (m + 1), Lc * c:Lc * (c + 1)], key=("p2dm", par),
                   queue="QG", name="p2ld")
            C.vstep(lambda e, m=m, t1_=t1_: e.tensor_scalar_mul(t1_[:, 0:Lc], comb[m][:, 0:Lc],
                                                                fcol[:, 0:1]),
                    reads=[("p2mps", m), "p2f"], writes=[("p2ct1", par)], name="p2cf")
            C.vstep(lambda e, t1_=t1_, dm_=dm_: e.scalar_tensor_tensor(t1_[:, 0:Lc], dm_[:, 0:Lc], fmcol[:, 0:1],
                                                                       t1_[:, 0:Lc], ALU.mult, ALU.add),
                    reads=[("p2dm", par), "p2fm", ("p2ct1", par)], writes=[("p2ct1", par)], name="p2cd")
            C.vstep(lambda e, t2_=t2_, gm_=gm_: e.tensor_scalar(t2_[:, 0:Lc], gm_[:, 0:Lc], -1.0, 1.0, ALU.mult, ALU.add),
                    reads=[("p2gm", par)], writes=[("p2ct2", par)], name="p2cg")
            C.vstep(lambda e, t1_=t1_, t2_=t2_: e.tensor_mul(t1_[:, 0:Lc], t1_[:, 0:Lc], t2_[:, 0:Lc]),
                    reads=[("p2ct1", par), ("p2ct2", par)], writes=[("p2ct1", par)], name="p2cm")
            C.vstep(lambda e, m=m, t2_=t2_, gm_=gm_: e.tensor_mul(t2_[:, 0:Lc], xnc[m][:, 0:Lc], gm_[:, 0:Lc]),
                    reads=["p2xnB", ("p2gm", par)], writes=[("p2ct2", par)], name="p2cx")
            C.vstep(lambda e, m=m, t1_=t1_, t2_=t2_: e.tensor_add(comb[m][:, 0:Lc], t1_[:, 0:Lc], t2_[:, 0:Lc]),
                    reads=[("p2ct1", par), ("p2ct2", par)], writes=[("p2comb", m)], name="p2cc")
        if STOP < 10:
            C.store(comb[0][:, 0:Lc], accT[0:128, Lc * c:Lc * (c + 1)], rkeys=[("p2comb", 0)], name="p2st")
            continue
        meanb, rstdb = emit_lnT(C, com, [t[:, 0:Lc] for t in comb],
                                [("p2comb", m) for m in range(8)], Lc, ln)
        for m in range(8):
            par = m % 2
            t1_, t2_, xm_ = ctmp1[par], ctmp2[par], xmt[par]
            C.vstep(lambda e, m=m, t1_=t1_: e.tensor_sub(t1_[:, 0:Lc], comb[m][:, 0:Lc], meanb[:, 0:Lc]),
                    reads=[("p2comb", m), "p2lnmb"], writes=[("p2ct1", par)], name="p2nsub")
            C.vstep(lambda e, t1_=t1_: e.tensor_mul(t1_[:, 0:Lc], t1_[:, 0:Lc], rstdb[:, 0:Lc]),
                    reads=[("p2ct1", par), "p2lnrb"], writes=[("p2ct1", par)], name="p2nmul")
            C.vstep(lambda e, m=m, t1_=t1_: e.tensor_scalar(t1_[:, 0:Lc], t1_[:, 0:Lc],
                                                            ng[:, m:m + 1], nb_[:, m:m + 1], ALU.mult, ALU.add),
                    reads=[("p2ct1", par), ("p2ng", m), ("p2nb", m)], writes=[("p2ct1", par)], name="p2naff")
            C.load(xm_[:], xT[128 * m:128 * (m + 1), Lc * c:Lc * (c + 1)], key=("p2xm", par),
                   queue="QG", name="p2lx")
            C.vstep(lambda e, t1_=t1_, t2_=t2_, xm_=xm_: e.tensor_add(t2_[:, 0:Lc], t1_[:, 0:Lc], xm_[:, 0:Lc]),
                    reads=[("p2ct1", par), ("p2xm", par)], writes=[("p2ct2", par)], name="p2acc")
            C.store(t2_[:, 0:Lc], accT[128 * m:128 * (m + 1), Lc * c:Lc * (c + 1)],
                    rkeys=[("p2ct2", par)], name="p2st")



def ph_p3(C, com):
    xnbf = C.dscratch("xnT_bf", (D, T), BF16)
    accT = C.dscratch("accT", (D, T), F32)
    x1T = C.dscratch("x1T", (D, T), F32)
    cpj_d = C.dscratch("cprojT_bf", (16 * D, D), BF16)
    rm_d = C.din("rm_bd", (4, 8, 128, 128), F32)
    ru_d = C.din("ru_bd", (4, 8, 4, 128, 128), F32)

    import os
    Lc = 256
    NCH = T // Lc
    if os.environ.get('P3_NCH'):
        NCH = int(os.environ['P3_NCH'])

    xn = []
    for k in range(8):
        t = C.sb("p3xn", (128, T), BF16)
        C.load(t[:], xnbf[128 * k:128 * (k + 1), :], key=("p3xn", k), name="p3lxn")
        xn.append(t)
    rm = {}
    ru = {}
    for h in range(4):
        for g in range(8):
            rm[h, g] = C.sb("p3rm", (128, 128), BF16)
            for mt in range(4):
                ru[h, g, mt] = C.sb("p3ru", (128, 128), BF16)
    with C.sub_scope():
        stage = [C.sb("p3stg", (128, 128)) for _ in range(2)]
        si = 0
        for h in range(4):
            for g in range(8):
                st_ = stage[si % 2]
                C.load(st_[:], rm_d[h, g], key=("p3stg", si % 2), name="p3lrm")
                C.vstep(lambda e, t=rm[h, g], st_=st_: e.tensor_copy(t[:], st_[:]),
                        reads=[("p3stg", si % 2)], writes=[("p3rm", h, g)], name="p3crm")
                si += 1
                for mt in range(4):
                    st_ = stage[si % 2]
                    C.load(st_[:], ru_d[h, g, mt], key=("p3stg", si % 2), name="p3lru")
                    C.vstep(lambda e, t2=ru[h, g, mt], st_=st_: e.tensor_copy(t2[:], st_[:]),
                            reads=[("p3stg", si % 2)], writes=[("p3ru", h, g, mt)], name="p3cru")
                    si += 1
    y1h = [C.sb("p3y1", (128, Lc), BF16) for _ in range(8)]
    z1c = [C.sb("p3z1", (128, Lc), BF16) for _ in range(128)]
    wc = [C.sb("p3wc", (128, 4 * D), BF16) for _ in range(2)]
    amt = [C.sb("p3am", (128, Lc)) for _ in range(2)]
    x1t = [C.sb("p3x1", (128, Lc)) for _ in range(2)]
    psY = [C.psum("p3psY0", (128, Lc)), C.psum("p3psY1", (128, Lc))]
    psZ = [C.psum("p3psZ0", (128, Lc)), C.psum("p3psZ1", (128, Lc))]
    psC = [C.psum(f"p3psC{b}", (128, 2 * Lc)) for b in range(4)]

    for c in range(NCH):
        for h in range(4):
            for g in range(8):
                pY, pYk = psY[g % 2], ("p3psY", g % 2)
                C.tstep(lambda e, h=h, g=g, c=c, pY=pY: e.matmul(
                            pY[:, 0:Lc], rm[h, g][:], xn[g][:, Lc * c:Lc * (c + 1)],
                            start=True, stop=True),
                        reads=[("p3rm", h, g), ("p3xn", g)], writes=[pYk], name="p3y1mm")
                C.sstep(lambda e, g=g, pY=pY: e.activation(y1h[g][:, 0:Lc], pY[:, 0:Lc], AF.Copy),
                        reads=[pYk], writes=[("p3y1", g)], name="p3y1e")
            zi = 0
            for g in range(8):
                for mt in range(4):
                    pZ, pZk = psZ[zi % 2], ("p3psZ", zi % 2)
                    zi += 1
                    C.tstep(lambda e, h=h, g=g, mt=mt, pZ=pZ: e.matmul(
                                pZ[:, 0:Lc], ru[h, g, mt][:], y1h[g][:, 0:Lc],
                                start=True, stop=True),
                            reads=[("p3ru", h, g, mt), ("p3y1", g)], writes=[pZk], name="p3z1mm")
                    kt = h * 32 + g * 4 + mt
                    C.sstep(lambda e, kt=kt, pZ=pZ: e.activation(z1c[kt][:, 0:Lc], pZ[:, 0:Lc], AF.Gelu),
                            reads=[pZk], writes=[("p3z1", kt)], name="p3gelu")
        for k4 in range(32):
            kk = k4 % 2
            C.load(wc[kk][:, :],
                   cpj_d.rearrange("(g p) n -> p g n", p=128)[:, 4 * k4:4 * (k4 + 1), :],
                   key=("p3wc", kk), queue=("Q" if k4 % 2 == 0 else "QG"), name="p3lwc")
            for ks in range(4):
                kt = 4 * k4 + ks

                def cfn(e, kt=kt, ks=ks, kk=kk):
                    last = None
                    for m in range(8):
                        b = m // 2
                        off = (m % 2) * Lc
                        last = e.matmul(
                            psC[b][:, off:off + Lc], wc[kk][:, D * ks + 128 * m:D * ks + 128 * (m + 1)],
                            z1c[kt][:, 0:Lc],
                            start=(kt == 0 and m % 2 == 0), stop=(kt == 127 and m % 2 == 1),
                            skip_group_check=True)
                    return last
                C.tstep(cfn, reads=[("p3wc", kk), ("p3z1", kt)],
                        writes=[("p3psC", b) for b in range(4)], name="p3cmm")
        for m in range(8):
            b = m // 2
            off = (m % 2) * Lc
            par = m % 2
            C.load(amt[par][:, 0:Lc], accT[128 * m:128 * (m + 1), Lc * c:Lc * (c + 1)],
                   key=("p3am", par), queue="QG", name="p3lac")
            C.vstep(lambda e, b=b, off=off, par=par: e.tensor_add(
                        x1t[par][:, 0:Lc], psC[b][:, off:off + Lc], amt[par][:, 0:Lc]),
                    reads=[("p3psC", b), ("p3am", par)], writes=[("p3x1", par)], name="p3add")
            C.store(x1t[par][:, 0:Lc], x1T[128 * m:128 * (m + 1), Lc * c:Lc * (c + 1)],
                    rkeys=[("p3x1", par)], name="p3st")



def ph_p4(C, com):
    x1T = C.dscratch("x1T", (D, T), F32)
    xn1bf = C.dscratch("xn1T_bf", (D, T), BF16)
    xqbf = C.dscratch("xqT_bf", (D, T), BF16)
    W = 512
    NCH = T // W
    x1 = [C.sb("p4x1", (128, W)) for _ in range(8)]
    xn1 = [C.sb("p4xn", (128, W)) for _ in range(8)]
    xnb = C.sb("p4xnb", (128, W), BF16)
    ln = LnScratch(C, W, "p4ln")
    xtok = [C.sb("p4tok", (128, D)) for _ in range(4)]
    xqtok = [C.sb("p4qt", (128, D), BF16) for _ in range(4)]
    scl = C.sb("p4scl", (128, D // 16))
    inv = C.sb("p4inv", (128, D // 16))
    qr = C.sb("p4qr", (128, D))
    qa = C.sb("p4qa", (128, D))
    qL = C.sb("p4qL", (128, D))
    qsg = C.sb("p4qsg", (128, D))
    xqT = C.sb("p4xqT", (128, W), BF16)
    pstr = [C.psum("p4pstr0", (128, 128)), C.psum("p4pstr1", (128, 128))]
    pstrb = [C.psum("p4pstrb0", (128, 128), BF16), C.psum("p4pstrb1", (128, 128), BF16)]
    for c in range(NCH):
        keys = []
        for i in range(8):
            k = ("p4x1", i)
            C.load(x1[i][:], x1T[128 * i:128 * (i + 1), c * W:(c + 1) * W], key=k, name="p4lx")
            keys.append(k)
        meanb, rstdb = emit_lnT(C, com, [t[:, 0:W] for t in x1], keys, W, ln)
        for i in range(8):
            C.vstep(lambda e, i=i: e.tensor_sub(xn1[i][:, 0:W], x1[i][:, 0:W], meanb[:, 0:W]),
                    reads=[("p4x1", i), "p4lnmb"], writes=[("p4f32", i)], name="p4sub")
            C.vstep(lambda e, i=i: e.tensor_mul(xn1[i][:, 0:W], xn1[i][:, 0:W], rstdb[:, 0:W]),
                    reads=[("p4f32", i), "p4lnrb"], writes=[("p4f32", i)], name="p4mul")
            C.vstep(lambda e, i=i: e.tensor_copy(xnb[:, 0:W], xn1[i][:, 0:W]),
                    reads=[("p4f32", i)], writes=["p4xnb"], name="p4cb")
            C.store(xnb[:, 0:W], xn1bf[128 * i:128 * (i + 1), c * W:(c + 1) * W],
                    rkeys=["p4xnb"], name="p4st")
        trl = [(tt, ft) for tt in range(4) for ft in range(8)]
        for i, (tt, ft) in enumerate(trl):
            par = i % 2
            C.tstep(lambda e, tt=tt, ft=ft, par=par: e.transpose(
                        pstr[par][:, 0:128], xn1[ft][:, 128 * tt:128 * (tt + 1)], com.ident[:]),
                    reads=[("p4f32", ft), "ident"], writes=[("p4pstr", par)], name="p4tr")
            C.vstep(lambda e, tt=tt, ft=ft, par=par: e.tensor_copy(
                        xtok[tt][:, 128 * ft:128 * (ft + 1)], pstr[par][:, 0:128]),
                    reads=[("p4pstr", par)], writes=[("p4tok", tt)], name="p4tre")
        for tt in range(4):
            emit_scales(C, xtok[tt][:, 0:D], scl, inv, 128, D, "p4s",
                        extra_reads=[("p4tok", tt)], ckey="p4sC", ikey="p4sI")
            emit_qdq(C, xtok[tt][:, 0:D], xqtok[tt][:, 0:D], bview(inv, 128, D), bview(scl, 128, D),
                     qr, qa, qL, qsg, 128, D, "p4q",
                     skey=("p4tok", tt), ikey="p4sI", ckey="p4sC", tt_eng="G")
            C.P.last_write[("p4qd", tt)] = C.P.last_write["p4qD"]
        for ft in range(8):
            for tt in range(4):
                par = tt % 2
                C.tstep(lambda e, tt=tt, ft=ft, par=par: e.transpose(
                            pstrb[par][:, 0:128], xqtok[tt][:, 128 * ft:128 * (ft + 1)], com.identb[:]),
                        reads=[("p4qd", tt), "identb", "p4qD"], writes=[("p4pstrb", par)], name="p4trq")
                C.vstep(lambda e, tt=tt, ft=ft, par=par: e.tensor_copy(
                            xqT[:, 128 * tt:128 * (tt + 1)], pstrb[par][:, 0:128]),
                        reads=[("p4pstrb", par)], writes=["p4xqT"], name="p4trqe")
            C.store(xqT[:, 0:W], xqbf[128 * ft:128 * (ft + 1), c * W:(c + 1) * W],
                    rkeys=["p4xqT"], name="p4stq")



def ph_p6(C, com):
    xqbf = C.dscratch("xqT_bf", (D, T), BF16)
    x2aT = C.dscratch("x2aT", (D, T), F32)
    outT = C.dout("outT", (D, T), F32)
    wgq_d = C.dscratch("wgqT_bf", (D, 2816), BF16)
    wuq_d = C.dscratch("wuqT_bf", (D, 2816), BF16)
    wdq_d = C.dscratch("wdqT_bf", (2816, D), BF16)
    I2816 = 2816
    NKD = 22

    xq = []
    for k in range(8):
        t = C.sb("p6xq", (128, T), BF16)
        C.load(t[:], xqbf[128 * k:128 * (k + 1), :], key=("p6xq", k), name="p6lxq")
        xq.append(t)
    wg = []
    wu = []
    for k in range(8):
        t = C.sb("p6wg", (128, I2816), BF16)
        C.load(t[:], wgq_d[128 * k:128 * (k + 1), :], key=("p6wg", k), name="p6lwg")
        wg.append(t)
        t2 = C.sb("p6wu", (128, I2816), BF16)
        C.load(t2[:], wuq_d[128 * k:128 * (k + 1), :], key=("p6wu", k), name="p6lwu")
        wu.append(t2)
    wd = [C.sb("p6wd", (128, 4 * D), BF16) for _ in range(2)]

    hbuf = C.sb("p6h", (128, I2816))
    sg = C.sb("p6sg", (128, 512))
    hq = C.sb("p6hq", (128, I2816), BF16)
    hqT = [C.sb("p6hqT", (128, 512), BF16) for _ in range(NKD)]
    HF = I2816 // 2
    scl = C.sb("p6scl", (128, 88))
    inv = C.sb("p6inv", (128, 88))
    qr = C.sb("p6qr", (128, HF))
    qa = C.sb("p6qa", (128, HF))
    qL = C.sb("p6qL", (128, HF))
    qsg = C.sb("p6qsg", (128, HF))
    x2m = C.sb("p6x2", (128, 512))
    om = C.sb("p6om", (128, 512))
    x2m2 = [x2m, x2m]
    om2 = [om, om]
    psG = [C.psum("p6psG0", (128, 512)), C.psum("p6psG1", (128, 512))]
    psU = [C.psum("p6psU0", (128, 512)), C.psum("p6psU1", (128, 512))]
    psD = [C.psum(f"p6psD{i}", (128, 512)) for i in range(2)]
    psT = [C.psum("p6psT0", (128, 128), BF16), C.psum("p6psT1", (128, 128), BF16)]
    sg2 = [sg, C.sb("p6sg2", (128, 512))]

    NW = [512] * 5 + [256]
    for tg in range(4):
        for t4 in range(4):
            tt = tg * 4 + t4
            offs = []
            o = 0
            for wn in NW:
                offs.append((o, wn))
                o += wn

            def gu_part(wi):
                off, wn = offs[wi]
                par = wi % 2
                for kk in range(8):
                    C.tstep(lambda e, tt=tt, kk=kk, off=off, wn=wn, par=par: e.matmul(
                                psG[par][:, 0:wn], xq[kk][:, 128 * tt:128 * (tt + 1)],
                                wg[kk][:, off:off + wn], start=(kk == 0), stop=(kk == 7)),
                            reads=[("p6xq", kk), ("p6wg", kk)], writes=[("p6psG", par)], name="p6gmm")
                for kk in range(8):
                    C.tstep(lambda e, tt=tt, kk=kk, off=off, wn=wn, par=par: e.matmul(
                                psU[par][:, 0:wn], xq[kk][:, 128 * tt:128 * (tt + 1)],
                                wu[kk][:, off:off + wn], start=(kk == 0), stop=(kk == 7)),
                            reads=[("p6xq", kk), ("p6wu", kk)], writes=[("p6psU", par)], name="p6umm")

            def sh_part(wi):
                off, wn = offs[wi]
                par = wi % 2
                sgp = sg2[par]
                sgk = ("p6sg", par)
                C.sstep(lambda e, wn=wn, sgp=sgp, par=par: e.activation(sgp[:, 0:wn], psG[par][:, 0:wn], AF.Silu),
                        reads=[("p6psG", par)], writes=[sgk], name="p6silu")
                C.vstep(lambda e, off=off, wn=wn, sgp=sgp, par=par: e.tensor_mul(
                            hbuf[:, off:off + wn], sgp[:, 0:wn], psU[par][:, 0:wn]),
                        reads=[sgk, ("p6psU", par)], writes=["p6hb"], name="p6hmul")
            gu_part(0)
            for wi in range(1, len(NW)):
                gu_part(wi)
                sh_part(wi - 1)
            sh_part(len(NW) - 1)
            for hf in range(2):
                emit_scales(C, hbuf[:, hf * HF:(hf + 1) * HF], scl, inv, 128, HF, "p6s",
                            extra_reads=["p6hb"], ckey="p6sC", ikey="p6sI")
                emit_qdq(C, hbuf[:, hf * HF:(hf + 1) * HF], hq[:, hf * HF:(hf + 1) * HF],
                         bview(inv, 128, HF), bview(scl, 128, HF), qr, qa, qL, qsg,
                         128, HF, "p6q", skey="p6hb", ikey="p6sI", ckey="p6sC", tt_eng="G")
                C.P.last_write["p6qD2"] = C.P.last_write["p6qD"]
            for k in range(NKD):
                pT, pTk = psT[k % 2], ("p6psT", k % 2)
                C.tstep(lambda e, k=k, pT=pT: e.transpose(pT[:, 0:128], hq[:, 128 * k:128 * (k + 1)],
                                                          com.identb[:]),
                        reads=["p6qD", "p6qD2", "identb"], writes=[pTk], name="p6trh")
                C.vstep(lambda e, k=k, t4=t4, pT=pT: e.tensor_copy(hqT[k][:, 128 * t4:128 * (t4 + 1)],
                                                                   pT[:, 0:128]),
                        reads=[pTk], writes=[("p6hqT", k)], name="p6trhe")
        for half in range(4):
            for k4 in range(6):
                kk = k4 % 2
                nk = min(4, NKD - 4 * k4)
                C.load(wd[kk][:, 0:nk * D],
                       wdq_d.rearrange("(g p) n -> p g n", p=128)[:, 4 * k4:4 * k4 + nk, :],
                       key=("p6wd", kk), queue=("Q" if k4 % 2 == 0 else "QG"), name="p6lwd")
                for ks in range(nk):
                    k = 4 * k4 + ks
                    for mm2 in range(2):
                        m = 2 * half + mm2
                        C.tstep(lambda e, m=m, k=k, ks=ks, kk=kk, mm2=mm2: e.matmul(
                                    psD[mm2][:, 0:512], wd[kk][:, D * ks + 128 * m:D * ks + 128 * (m + 1)],
                                    hqT[k][:, 0:512],
                                    start=(k == 0), stop=(k == NKD - 1)),
                                reads=[("p6wd", kk)] + [("p6hqT", k)], writes=[("p6psD", mm2)], name="p6dmm")
            for mm2 in range(2):
                m = 2 * half + mm2
                C.load(x2m[:, 0:512], x2aT[128 * m:128 * (m + 1), 512 * tg:512 * (tg + 1)],
                       key="p6x2", queue="QG", name="p6lx2")
                C.vstep(lambda e, mm2=mm2: e.tensor_add(om[:, 0:512], psD[mm2][:, 0:512],
                                                        x2m[:, 0:512]),
                        reads=[("p6psD", mm2), "p6x2"], writes=["p6om"], name="p6add")
                C.store(om[:, 0:512], outT[128 * m:128 * (m + 1), 512 * tg:512 * (tg + 1)],
                        rkeys=["p6om"], name="p6st")



def ph_p5(C, com):
    xn1bf = C.dscratch("xn1T_bf", (D, T), BF16)
    x1T = C.dscratch("x1T", (D, T), F32)
    x2aT = C.dscratch("x2aT", (D, T), F32)
    qkv_d = C.din("qkvT", (D, 3 * D), F32)
    wo_d = C.din("WoT", (D, D), F32)
    bqkv_d = C.din("b_qkv", (3 * D,), F32)
    bo_d = C.din("b_o", (D,), F32)
    msk_d = C.din("mask128", (128, 128), F32)

    xn1 = []
    for k in range(8):
        t = C.sb("p5xn", (128, T), BF16)
        C.load(t[:], xn1bf[128 * k:128 * (k + 1), :], key=("p5xn", k), name="p5lxn")
        xn1.append(t)
    qkv, qkvk = load_wbf(C, "p5qkv", qkv_d, D, 3 * D)
    wo, wok = load_wbf(C, "p5wo", wo_d, D, D)
    bq = C.sb("p5bq", (64, 48))
    for j in range(48):
        C.load(bq[:, j:j + 1], bqkv_d[64 * j:64 * (j + 1)].unsqueeze(-1), key=("p5bq", j), name="p5lbq")
    bo, _ = load_bias_cols(C, "p5bo", bo_d, D)
    mskb = C.sb("p5msk", (128, 128), BF16)
    with C.sub_scope():
        mskf = C.sb("p5mskf", (128, 128))
        C.load(mskf[:], msk_d, key="p5mskf", name="p5lmsk")
        C.vstep(lambda e: e.tensor_copy(mskb[:], mskf[:]), reads=["p5mskf"], writes=["p5msk"], name="p5cmsk")

    QT = C.sb("p5QT", (64, T), BF16)
    KT = C.sb("p5KT", (64, T), BF16)
    VT = C.sb("p5VT", (64, T), BF16)
    Vtok = [C.sb("p5Vt", (128, 65), BF16) for _ in range(16)]
    PT = [C.sb("p5PT", (128, 512), BF16) for _ in range(2)]
    OT = [C.sb("p5OT", (128, T), BF16) for _ in range(8)]
    rec = C.sb("p5rec", (1, 512))
    recb = C.sb("p5recb", (64, 512))
    x1m = C.sb("p5x1m", (128, 512))
    aom = C.sb("p5aom", (128, 512))
    psQ = [C.psum("p5psQ0", (64, 512)), C.psum("p5psQ1", (64, 512))]
    psS = [C.psum("p5psS0", (128, 512)), C.psum("p5psS1", (128, 512))]
    psO = C.psum("p5psO", (65, 512))
    psR = C.psum("p5psR", (64, 512))
    psVt = C.psum("p5psVt", (128, 64), BF16)
    psW = C.psum("p5psW", (128, 512))

    for tt in range(16):
        C.vstep(lambda e, tt=tt: e.memset(Vtok[tt][:, 64:65], 1.0), writes=[("p5vone", tt)], name="p5one")

    for h in range(16):
        qkvjobs = []
        for part, dst, dkey in ((0, QT, "p5QT"), (1, KT, "p5KT"), (2, VT, "p5VT")):
            for qc in range(4):
                qkvjobs.append((part, dst, dkey, qc))

        def qkv_mm(i):
            part, dst, dkey, qc = qkvjobs[i]
            col = part * D + h * 64
            par = i % 2
            pairs = [(qkv[k][:, col:col + 64], xn1[k][:, 512 * qc:512 * (qc + 1)])
                     for k in range(8)]
            C.mm(psQ[par][0:64, 0:512], pairs, rkeys=qkvk + [("p5xn", k) for k in range(8)],
                 wkey=("p5psQ", par), name="p5qkvmm")

        def qkv_ev(i):
            part, dst, dkey, qc = qkvjobs[i]
            bidx = part * 16 + h
            par = i % 2
            pq = psQ[par]
            if part == 0:
                C.vstep(lambda e, qc=qc, bidx=bidx, pq=pq: e.tensor_scalar(
                            QT[0:64, 512 * qc:512 * (qc + 1)], pq[0:64, 0:512],
                            bq[:, bidx:bidx + 1], 0.125, ALU.add, ALU.mult),
                        reads=[("p5psQ", par), ("p5bq", bidx)], writes=[dkey], name="p5qe")
            else:
                C.vstep(lambda e, qc=qc, bidx=bidx, dst=dst, pq=pq: e.tensor_scalar_add(
                            dst[0:64, 512 * qc:512 * (qc + 1)], pq[0:64, 0:512],
                            bq[:, bidx:bidx + 1]),
                        reads=[("p5psQ", par), ("p5bq", bidx)], writes=[dkey], name="p5kve")
        qkv_mm(0)
        for i in range(1, len(qkvjobs)):
            qkv_mm(i)
            qkv_ev(i - 1)
        qkv_ev(len(qkvjobs) - 1)
        for tt in range(16):
            C.tstep(lambda e, tt=tt: e.transpose(psVt[:, 0:64], VT[0:64, 128 * tt:128 * (tt + 1)],
                                                 com.identb[0:64, 0:64]),
                    reads=["p5VT", "identb"], writes=["p5psVt"], name="p5vtr")
            C.vstep(lambda e, tt=tt: e.tensor_copy(Vtok[tt][:, 0:64], psVt[:, 0:64]),
                    reads=["p5psVt"], writes=[("p5vt", tt)], name="p5vte")
        ot = OT[h // 2]
        orow = (h % 2) * 64
        for qc in range(4):
            nkt = 4 * qc + 4

            def s_part(kt, qc=qc):
                par = kt % 2
                pS, pSk = psS[par], ("p5psS", par)
                pt_, ptk = PT[par], ("p5PT", par)
                C.tstep(lambda e, kt=kt, qc=qc, pS=pS: e.matmul(
                            pS[:, 0:512], KT[0:64, 128 * kt:128 * (kt + 1)],
                            QT[0:64, 512 * qc:512 * (qc + 1)], start=True, stop=True),
                        reads=["p5KT", "p5QT"], writes=[pSk], name="p5smm")
                scol = (kt - 4 * qc) * 128
                if kt >= 4 * qc:
                    if scol > 0:
                        C.vstep(lambda e, scol=scol, pt_=pt_: e.memset(pt_[:, 0:scol], 0.0),
                                reads=[], writes=[ptk], name="p5pz")
                    C.sstep(lambda e, scol=scol, pt_=pt_, pS=pS: e.activation(pt_[:, scol:512], pS[:, scol:512], AF.Exp),
                            reads=[pSk, ptk], writes=[ptk], name="p5exp")
                    C.vstep(lambda e, scol=scol, pt_=pt_: e.tensor_mul(pt_[:, scol:scol + 128],
                                                              pt_[:, scol:scol + 128], mskb[:]),
                            reads=[ptk, "p5msk"], writes=[ptk], name="p5msk2")
                else:
                    C.sstep(lambda e, pt_=pt_, pS=pS: e.activation(pt_[:, 0:512], pS[:, 0:512], AF.Exp),
                            reads=[pSk], writes=[ptk], name="p5exp")

            def pv_part(kt, nkt=nkt):
                par = kt % 2
                pt_, ptk = PT[par], ("p5PT", par)
                C.tstep(lambda e, kt=kt, nkt=nkt, pt_=pt_: e.matmul(
                            psO[0:65, 0:512], Vtok[kt][:, 0:65], pt_[:, 0:512],
                            start=(kt == 0), stop=(kt == nkt - 1)),
                        reads=[("p5vt", kt), ("p5vone", kt), ptk], writes=["p5psO"], name="p5pv")
            s_part(0)
            for kt in range(1, nkt):
                s_part(kt)
                pv_part(kt - 1)
            pv_part(nkt - 1)
            C.vstep(lambda e: e.reciprocal(rec[0:1, 0:512], psO[64:65, 0:512]),
                    reads=["p5psO"], writes=["p5rec"], name="p5rcp")
            C.tstep(lambda e: e.matmul(psR[0:64, 0:512], com.ones1[0:1, 0:64], rec[0:1, 0:512],
                                       start=True, stop=True),
                    reads=["p5rec", "ones1"], writes=["p5psR"], name="p5rb")
            C.vstep(lambda e: e.tensor_copy(recb[0:64, 0:512], psR[0:64, 0:512]),
                    reads=["p5psR"], writes=["p5recb"], name="p5rbe")
            C.vstep(lambda e, ot=ot, orow=orow, qc=qc: e.tensor_mul(
                        ot[orow:orow + 64, 512 * qc:512 * (qc + 1)], psO[0:64, 0:512],
                        recb[0:64, 0:512]),
                    reads=["p5psO", "p5recb"], writes=[("p5OT", h // 2)], name="p5onorm")
    for m in range(8):
        for qc in range(4):
            pairs = [(wo[k][:, 128 * m:128 * (m + 1)], OT[k][:, 512 * qc:512 * (qc + 1)])
                     for k in range(8)]
            C.mm(psW[:, 0:512], pairs, rkeys=wok + [("p5OT", k) for k in range(8)],
                 wkey="p5psW", name="p5womm")
            C.load(x1m[:, 0:512], x1T[128 * m:128 * (m + 1), 512 * qc:512 * (qc + 1)],
                   key="p5x1m", queue="QG", name="p5lx1")
            C.vstep(lambda e, m=m: e.tensor_scalar_add(aom[:, 0:512], psW[:, 0:512], bo[:, m:m + 1]),
                    reads=["p5psW", ("p5bo", m)], writes=["p5aom"], name="p5wbe")
            C.vstep(lambda e: e.tensor_add(aom[:, 0:512], aom[:, 0:512], x1m[:, 0:512]),
                    reads=["p5aom", "p5x1m"], writes=["p5aom"], name="p5res")
            C.store(aom[:, 0:512], x2aT[128 * m:128 * (m + 1), 512 * qc:512 * (qc + 1)],
                    rkeys=["p5aom"], name="p5st")



PHASE_ORDER = ["pw", "p0", "p1", "p2", "p3", "p4", "p5", "p6"]
_BUILD_CACHE = {}


def _get_program(phases, fused):
    key = (tuple(phases), fused)
    if key not in _BUILD_CACHE:
        _BUILD_CACHE[key] = build_program(phases, fused)
    return _BUILD_CACHE[key]


FUSED = True


def kernel(**inputs):
    hp = prep_common(inputs)
    hp["constv"] = np.tile(np.array(CONSTS, np.float32), (128, 1))
    x = np.asarray(inputs["L0_attn_x"], np.float32)
    envs = []
    for b in range(NB):
        env = dict(hp)
        env["xT"] = np.ascontiguousarray(x[b].T)
        envs.append(env)
    if FUSED:
        nc, C = _get_program(PHASE_ORDER, True)
        in_names = [n for n, k in C.kinds.items() if k == "ExternalInput"]
        in_maps = [{n: env[n] for n in in_names} for env in envs]
        res = run_bass_kernel_spmd(nc, in_maps, core_ids=list(range(NB)))
        out = np.stack([np.ascontiguousarray(np.asarray(r["outT"]).T) for r in res.results], axis=0)
        return out.astype(np.float32)
    for ph in PHASE_ORDER:
        nc, C = _get_program([ph], False)
        in_names = [n for n, k in C.kinds.items() if k == "ExternalInput"]
        out_names = [n for n, k in C.kinds.items() if k == "ExternalOutput"]
        in_maps = [{n: env[n] for n in in_names} for env in envs]
        res = run_bass_kernel_spmd(nc, in_maps, core_ids=list(range(NB)))
        for env, r in zip(envs, res.results):
            for n in out_names:
                env[n] = np.asarray(r[n])
    out = np.stack([np.ascontiguousarray(env["outT"].T) for env in envs], axis=0)
    return out.astype(np.float32)
